# version 01
# baseline (speedup 1.0000x reference)
import numpy as np
from contextlib import ExitStack

import concourse.bass as bass
import concourse.mybir as mybir
from concourse.bass_utils import run_bass_kernel_spmd

F32 = mybir.dt.float32
AF = mybir.ActivationFunctionType
ALU = mybir.AluOpType

NCORE = 8
D = 4096
DV = 1024
TC = 2048
QT = 512
NQ = TC // QT
EPS = 1e-5
ISQRT_D = 1.0 / 32.0


class Sem:
    __slots__ = ("h", "n")

    def __init__(self, h):
        self.h = h
        self.n = 0

    def bump(self, k):
        self.n += k
        return self.n


class Prog:
    def __init__(self):
        self.ops = []

    def emit(self, eng, fn, waits=(), incs=()):
        rec_incs = []
        for s, k in incs:
            s.bump(k)
            rec_incs.append((s, k))
        self.ops.append((eng, fn, tuple(waits), tuple(rec_incs)))

    def replay(self, name, engine):
        for eng, fn, waits, incs in self.ops:
            if eng != name:
                continue
            for s, v in waits:
                if v > 0:
                    engine.wait_ge(s.h, v)
            ins = fn(engine)
            for s, k in incs:
                ins.then_inc(s.h, k)


def build_program(dbg_phases=("qkv", "attn", "ffn"), dbg_outs=False):
    nc = bass.Bass("TRN2", target_bir_lowering=False)

    x_img = nc.dram_tensor("x_img", [128, 32 * TC], F32, kind="ExternalInput").ap()
    wq_img = nc.dram_tensor("wq_img", [8, 128, 4096], F32, kind="ExternalInput").ap()
    wk_img = nc.dram_tensor("wk_img", [8, 128, 4096], F32, kind="ExternalInput").ap()
    wv_img = nc.dram_tensor("wv_img", [32, 128, 1024], F32, kind="ExternalInput").ap()
    wo_img = nc.dram_tensor("wo_img", [32, 128, 1024], F32, kind="ExternalInput").ap()
    wbb_img = nc.dram_tensor("wbb_img", [32, 128, 4224], F32, kind="ExternalInput").ap()
    a_img = nc.dram_tensor("a_img", [128, 4096], F32, kind="ExternalInput").ap()
    wr_img = nc.dram_tensor("wr_img", [128, 256], F32, kind="ExternalInput").ap()
    bq_img = nc.dram_tensor("bq_img", [128, 8], F32, kind="ExternalInput").ap()
    bk_img = nc.dram_tensor("bk_img", [128, 8], F32, kind="ExternalInput").ap()
    bv_row = nc.dram_tensor("bv_row", [1, 1024], F32, kind="ExternalInput").ap()
    bo_img = nc.dram_tensor("bo_img", [128, 32], F32, kind="ExternalInput").ap()
    bb_img = nc.dram_tensor("bb_img", [128, 32], F32, kind="ExternalInput").ap()
    ones_in = nc.dram_tensor("ones_in", [128, 128], F32, kind="ExternalInput").ap()
    ident_in = nc.dram_tensor("ident_in", [128, 128], F32, kind="ExternalInput").ap()
    rt_in = nc.dram_tensor("rt_in", [8, 128], F32, kind="ExternalInput").ap()

    y_img = nc.dram_tensor("y_img", [128, 32 * TC], F32, kind="ExternalOutput").ap()

    k_gin = [nc.dram_tensor(f"k_gin{q}", [128, 4096], F32).ap() for q in range(NQ)]
    v_gin = [nc.dram_tensor(f"v_gin{q}", [128, 4096], F32).ap() for q in range(NQ)]
    k_gout = [nc.dram_tensor(f"k_gout{q}", [8 * 128, 4096], F32).ap() for q in range(NQ)]
    v_gout = [nc.dram_tensor(f"v_gout{q}", [8 * 128, 4096], F32).ap() for q in range(NQ)]
    q_scr = nc.dram_tensor("q_scr", [128, NQ * 4096], F32).ap()
    o_scr = nc.dram_tensor("o_scr", [128, NQ * 4096], F32).ap()

    dbg = {}
    if dbg_outs:
        dbg["d_q"] = nc.dram_tensor("d_q", [128, NQ * 4096], F32, kind="ExternalOutput").ap()
        dbg["d_q2"] = nc.dram_tensor("d_q2", [128, NQ * 4096], F32, kind="ExternalOutput").ap()
        dbg["d_k"] = nc.dram_tensor("d_k", [NQ, 128, 4096], F32, kind="ExternalOutput").ap()
        dbg["d_v"] = nc.dram_tensor("d_v", [NQ, 128, 4096], F32, kind="ExternalOutput").ap()
        dbg["d_o"] = nc.dram_tensor("d_o", [128, NQ * 4096], F32, kind="ExternalOutput").ap()
        dbg["d_x1"] = nc.dram_tensor("d_x1", [NQ, 128, 16384], F32, kind="ExternalOutput").ap()
        dbg["d_r2"] = nc.dram_tensor("d_r2", [NQ, 128, 16384], F32, kind="ExternalOutput").ap()
        dbg["d_sfm"] = nc.dram_tensor("d_sfm", [NQ, 128, 512], F32, kind="ExternalOutput").ap()
        dbg["d_g"] = nc.dram_tensor("d_g", [NQ, 128, 512], F32, kind="ExternalOutput").ap()
        dbg["d_st"] = nc.dram_tensor("d_st", [NQ, 8, 512], F32, kind="ExternalOutput").ap()
        dbg["d_lg"] = nc.dram_tensor("d_lg", [NQ, 128, 32], F32, kind="ExternalOutput").ap()
        dbg["d_srt"] = nc.dram_tensor("d_srt", [NQ, 128, 32], F32, kind="ExternalOutput").ap()
        dbg["d_w4"] = nc.dram_tensor("d_w4", [NQ, 128, 32], F32, kind="ExternalOutput").ap()
        dbg["d_wem"] = nc.dram_tensor("d_wem", [NQ, 8, 512], F32, kind="ExternalOutput").ap()
        dbg["d_e2"] = nc.dram_tensor("d_e2", [NQ, 128, 16], F32, kind="ExternalOutput").ap()

    es = ExitStack()
    with es:
        big1 = es.enter_context(nc.sbuf_tensor("big1", [128, 16384], F32))
        big2 = es.enter_context(nc.sbuf_tensor("big2", [128, 16384], F32))
        wbuf = es.enter_context(nc.sbuf_tensor("wbuf", [128, 8448], F32))
        wr_s = es.enter_context(nc.sbuf_tensor("wr_s", [128, 256], F32))
        misc = es.enter_context(nc.sbuf_tensor("misc", [128, 4096], F32))
        gq = es.enter_context(nc.sbuf_tensor("gq", [128, 512], F32))
        consts = es.enter_context(nc.sbuf_tensor("consts", [128, 256], F32))
        rt_s = es.enter_context(nc.sbuf_tensor("rt_s", [8, 128], F32))
        bias_s = es.enter_context(nc.sbuf_tensor("bias_s", [128, 80], F32))
        bv_s = es.enter_context(nc.sbuf_tensor("bv_s", [1, 1024], F32))
        wem = es.enter_context(nc.sbuf_tensor("wem", [8, 512], F32))
        lg = es.enter_context(nc.sbuf_tensor("lg", [128, 8], F32))
        srt = es.enter_context(nc.sbuf_tensor("srt", [128, 8], F32))
        ee = es.enter_context(nc.sbuf_tensor("ee", [128, 8], F32))
        msk = es.enter_context(nc.sbuf_tensor("msk", [128, 8], F32))
        negm1 = es.enter_context(nc.sbuf_tensor("negm1", [128, 1], F32))
        l2t = es.enter_context(nc.sbuf_tensor("l2t", [128, 1], F32))
        e2t = es.enter_context(nc.sbuf_tensor("e2t", [128, 1], F32))
        r2t = es.enter_context(nc.sbuf_tensor("r2t", [128, 1], F32))
        w4 = es.enter_context(nc.sbuf_tensor("w4", [128, 32], F32))
        st1 = es.enter_context(nc.sbuf_tensor("st1", [1, 512], F32))
        st2 = es.enter_context(nc.sbuf_tensor("st2", [1, 512], F32))
        st3 = es.enter_context(nc.sbuf_tensor("st3", [1, 512], F32))
        st4 = es.enter_context(nc.sbuf_tensor("st4", [1, 512], F32))
        st5 = es.enter_context(nc.sbuf_tensor("st5", [1, 512], F32))
        st6 = es.enter_context(nc.sbuf_tensor("st6", [1, 512], F32))

        pb = [es.enter_context(nc.psum_tensor(f"pb{i}", [128, 512], F32)) for i in range(8)]

        sems = {}
        for name in ("s_c", "s_x", "s_xp0", "s_xp1", "s_w0", "s_w1",
                     "s_wv0", "s_wv1", "s_wv2", "s_wv3",
                     "s_q", "s_kk0", "s_kk1", "s_kva0", "s_kva1",
                     "s_st0", "s_st1", "s_y",
                     "s_mm", "s_evs", "s_evv", "s_cc"):
            sems[name] = Sem(es.enter_context(nc.semaphore(name)))
        s_c, s_x, s_y = sems["s_c"], sems["s_x"], sems["s_y"]
        s_xp = [sems["s_xp0"], sems["s_xp1"]]
        s_w2 = [sems["s_w0"], sems["s_w1"]]
        s_wv4 = [sems["s_wv0"], sems["s_wv1"], sems["s_wv2"], sems["s_wv3"]]
        s_q = sems["s_q"]
        s_kk = [sems["s_kk0"], sems["s_kk1"]]
        s_kva = [sems["s_kva0"], sems["s_kva1"]]
        s_st2 = [sems["s_st0"], sems["s_st1"]]
        s_mm, s_evs, s_evv = sems["s_mm"], sems["s_evs"], sems["s_evv"]
        s_cc = sems["s_cc"]

        P = Prog()

        def dma(q, out_ap, in_ap, sem, waits=(), slow=False):
            if slow:
                P.emit(q, (lambda o, i: (lambda e: e.dma_start(
                    out=o, in_=i, allow_slow_non_contiguous=True)))(out_ap, in_ap),
                       waits=waits, incs=[(sem, 16)])
            else:
                P.emit(q, (lambda o, i: (lambda e: e.dma_start(out=o, in_=i)))(out_ap, in_ap),
                       waits=waits, incs=[(sem, 16)])
            return sem.n

        def mm(out_ap, lhsT, rhs, start, stop, waits=(), inc=None):
            P.emit("tensor",
                   (lambda o, l, r, st, sp: (lambda e: e.matmul(o, l, r, start=st, stop=sp)))(
                       out_ap, lhsT, rhs, start, stop),
                   waits=waits, incs=([(inc, 1)] if inc is not None else []))
            return inc.n if inc is not None else None

        EVA = misc[:, 0:512]
        EVB = misc[:, 512:1024]
        XSA = misc[:, 1024:1536]
        XSB = misc[:, 1536:2048]
        R128 = misc[:, 2048:2560]
        M128 = misc[:, 2560:3072]
        SQA = misc[:, 3072:3584]
        SQB = misc[:, 3584:4096]
        HT = misc[:, 2048:2560]
        SFM = misc[:, 2560:3072]
        OUTT = misc[:, 3072:3584]
        OUT2 = misc[:, 3584:4096]

        ones_s = consts[:, 0:128]
        ident_s = consts[:, 128:256]
        BQ = bias_s[:, 0:8]
        BK = bias_s[:, 8:16]
        BO = bias_s[:, 16:48]
        BB = bias_s[:, 48:80]

        dma("sync", consts[:, 0:128], ones_in, s_c)
        dma("sync", consts[:, 128:256], ident_in, s_c)
        dma("sync", rt_s[:], rt_in, s_c)
        dma("sync", bias_s[:, 0:8], bq_img, s_c)
        dma("sync", bias_s[:, 8:16], bk_img, s_c)
        dma("sync", bias_s[:, 16:48], bo_img, s_c)
        dma("sync", bias_s[:, 48:80], bb_img, s_c)
        dma("sync", bv_s[:], bv_row, s_c)
        CONST_W = s_c.n
        for engn in ("tensor", "vector", "scalar", "gpsimd"):
            P.emit(engn, (lambda sh, vv: (lambda e: e.wait_ge(sh, vv)))(s_c.h, CONST_W))

        def barrier():
            vals = [(s, s.n) for s in sems.values() if s.n > 0]
            for engn in ("sync", "gpsimd", "tensor", "vector", "scalar"):
                for s, v in vals:
                    P.emit(engn, (lambda sh, vv: (lambda e: e.wait_ge(sh, vv)))(s.h, v))

        if "qkv" in dbg_phases:
            ev_hist_qk = {}
            last_v_evacs = 0
            mm_hist = {}
            st_hist = {}
            for qq in range(NQ):
                xw = [(s_mm, mm_hist.get(("vlast", qq - 1), 0))]
                for kc in range(32):
                    dma("gpsimd", big1[:, kc * 512:(kc + 1) * 512],
                        x_img[:, kc * TC + qq * QT: kc * TC + qq * QT + 512],
                        s_x, waits=(xw if kc == 0 else ()))
                XRDY = s_x.n

                for j in range(16):
                    fc = j % 8
                    isq = j < 8
                    img = wq_img if isq else wk_img
                    slot = wbuf[:, (j % 2) * 4096:(j % 2) * 4096 + 4096]
                    wv_wait = mm_hist.get((qq, "qk", j - 2), 0)
                    WL = dma("sync", slot, img[fc], s_w2[j % 2],
                             waits=[(s_mm, wv_wait)] if j >= 2 else
                                   ([(s_mm, mm_hist.get(("vlast", qq - 1), 0))] if qq > 0 else []))
                    bank = pb[j % 2]
                    mmw = [(s_w2[j % 2], WL), (s_x, XRDY),
                           (s_evs, ev_hist_qk.get((qq, j - 2), last_v_evacs))]
                    for kc in range(32):
                        mm(bank[:], slot[:, kc * 128:(kc + 1) * 128],
                           big1[:, kc * 512:(kc + 1) * 512],
                           start=(kc == 0), stop=(kc == 31),
                           waits=(mmw if kc == 0 else ()),
                           inc=(s_mm if kc == 31 else None))
                    mm_hist[(qq, "qk", j)] = s_mm.n
                    ev = EVA if (j % 2 == 0) else EVB
                    bias_ap = (BQ if isq else BK)[:, fc:fc + 1]
                    ev_reuse = st_hist.get((qq, j - 2),
                                           st_hist.get((qq - 1, "v", 6 + (j % 2)), 0))
                    P.emit("scalar",
                           (lambda o, i, b: (lambda e: e.activation(
                               out=o, in_=i, func=AF.Identity, bias=b, scale=1.0)))(
                               ev[:], bank[:], bias_ap),
                           waits=[(s_mm, mm_hist[(qq, "qk", j)]),
                                  (s_st2[j % 2], ev_reuse)],
                           incs=[(s_evs, 1)])
                    ev_hist_qk[(qq, j)] = s_evs.n
                    if isq:
                        dst = q_scr[:, qq * 4096 + fc * 512: qq * 4096 + fc * 512 + 512]
                    else:
                        dst = k_gin[qq][:, fc * 512: fc * 512 + 512]
                    st_hist[(qq, j)] = dma("gpsimd", dst, ev[:], s_st2[j % 2],
                                           waits=[(s_evs, ev_hist_qk[(qq, j)])])

                wv_loads = {}
                for kc in range(32):
                    slot = wbuf[:, (kc % 4) * 1024:(kc % 4) * 1024 + 1024]
                    w_waits = []
                    if kc >= 4:
                        w_waits.append((s_mm, mm_hist[(qq, "v", kc - 4)]))
                    else:
                        w_waits.append((s_mm, mm_hist[(qq, "qk", 15)]))
                    wv_loads[kc] = dma("sync", slot, wv_img[kc], s_wv4[kc % 4], waits=w_waits)
                    mmw = [(s_wv4[kc % 4], wv_loads[kc])]
                    if kc == 0:
                        mmw.append((s_evs, ev_hist_qk[(qq, 15)]))
                    for t in range(8):
                        tkc, dvh = t // 2, t % 2
                        mm(pb[t][:],
                           big1[:, kc * 512 + tkc * 128: kc * 512 + tkc * 128 + 128],
                           slot[:, dvh * 512: dvh * 512 + 512],
                           start=(kc == 0), stop=False,
                           waits=(mmw if t == 0 else ()),
                           inc=(s_mm if t == 7 else None))
                    mm_hist[(qq, "v", kc)] = s_mm.n
                for t in range(8):
                    tkc, dvh = t // 2, t % 2
                    mm(pb[t][:], ones_s[0:1, :], bv_s[0:1, dvh * 512: dvh * 512 + 512],
                       start=False, stop=True, inc=s_mm)
                    mm_hist[(qq, "vb", t)] = s_mm.n
                    ev = EVA if (t % 2 == 0) else EVB
                    P.emit("scalar",
                           (lambda o, i: (lambda e: e.copy(out=o, in_=i)))(ev[:], pb[t][:]),
                           waits=[(s_mm, mm_hist[(qq, "vb", t)]),
                                  (s_st2[t % 2],
                                   st_hist.get((qq, "v", t - 2),
                                               st_hist[(qq, 14 + (t % 2))]))],
                           incs=[(s_evs, 1)])
                    evn = s_evs.n
                    dst = v_gin[qq][:, tkc * 1024 + dvh * 512: tkc * 1024 + dvh * 512 + 512]
                    st_hist[(qq, "v", t)] = dma("gpsimd", dst, ev[:], s_st2[t % 2],
                                                waits=[(s_evs, evn)])
                last_v_evacs = s_evs.n
                mm_hist[("vlast", qq)] = mm_hist[(qq, "vb", 7)]

                ccw = [(s_st2[0], st_hist[(qq, "v", 6)]),
                       (s_st2[1], st_hist[(qq, "v", 7)])]
                P.emit("gpsimd",
                       (lambda i, o: (lambda e: e.collective_compute(
                           "AllGather", ALU.bypass,
                           replica_groups=[list(range(NCORE))],
                           ins=[i], outs=[o])))(k_gin[qq], k_gout[qq]),
                       waits=ccw, incs=[(s_cc, 1)])
                P.emit("gpsimd",
                       (lambda i, o: (lambda e: e.collective_compute(
                           "AllGather", ALU.bypass,
                           replica_groups=[list(range(NCORE))],
                           ins=[i], outs=[o])))(v_gin[qq], v_gout[qq]),
                       incs=[(s_cc, 1)])

            barrier()

        if "attn" in dbg_phases:
            KSLOT = [big1[:, 0:4096], big1[:, 4096:8192]]
            VSLOT = [big1[:, 8192:12288], big1[:, 12288:16384]]
            QT_S = big2[:, 0:4096]
            OSB = big2[:, 4096:8192]
            PSL = [[big2[:, 8192 + (s * 4 + t) * 512: 8192 + (s * 4 + t) * 512 + 512]
                    for t in range(4)] for s in range(2)]
            RREC = big2[:, 12288:12800]

            kv_hist = {}
            smm = {}
            omm = {}
            expc = {}
            flsh = {}
            sums_fin = {}
            attn_st = {}
            for g in range(NQ):
                qw = [(s_mm, omm.get((g - 1, 31, 1), 0))]
                QL = dma("sync", QT_S, q_scr[:, g * 4096: g * 4096 + 4096], s_q, waits=qw)
                NB = 32
                for b in range(NB):
                    qq, r = b // 8, b % 8
                    sl = b % 2
                    kvw = [(s_mm, smm.get((g, b - 2, 3), 0))]
                    kv_hist[(g, b, "k")] = dma(
                        "sync", KSLOT[sl],
                        k_gout[qq][r * 128:(r + 1) * 128, :], s_kk[sl],
                        waits=kvw + [(s_cc, 2 * (qq + 1))])
                    vw = [(s_mm, omm.get((g, b - 2, 1),
                                         omm.get((g - 1, 30 + (b % 2), 1), 0)))]
                    kv_hist[(g, b, "v")] = dma(
                        "sync", VSLOT[sl],
                        v_gout[qq][r * 128:(r + 1) * 128, :], s_kva[sl], waits=vw)
                    for tkc in range(4):
                        bank = pb[tkc % 2]
                        w = [(s_evs, expc.get((g, b, tkc - 2),
                                              expc.get((g, b - 1, tkc + 2), 0)))]
                        if tkc == 0:
                            w.append((s_kk[sl], kv_hist[(g, b, "k")]))
                            w.append((s_q, QL))
                        for dc in range(8):
                            mm(bank[:],
                               KSLOT[sl][:, dc * 512 + tkc * 128: dc * 512 + tkc * 128 + 128],
                               QT_S[:, dc * 512: dc * 512 + 512],
                               start=(dc == 0), stop=(dc == 7),
                               waits=(w if dc == 0 else ()),
                               inc=(s_mm if dc == 7 else None))
                        smm[(g, b, tkc)] = s_mm.n
                    if b > 0:
                        bp = b - 1
                        slp = bp % 2
                        w = [(s_evv, flsh[(g, bp, 0)])]
                        for i in range(4):
                            dvc = 4 + i
                            for tkc in range(4):
                                mm(pb[2 + i][:],
                                   VSLOT[slp][:, tkc * 1024 + dvc * 128: tkc * 1024 + dvc * 128 + 128],
                                   PSL[bp % 2][tkc][:],
                                   start=(tkc == 0), stop=(tkc == 3),
                                   waits=(w if (i == 0 and tkc == 0) else ()),
                                   inc=(s_mm if (i == 3 and tkc == 3) else None))
                        omm[(g, bp, 1)] = s_mm.n
                    for tkc in range(4):
                        w = [(s_mm, smm[(g, b, tkc)])]
                        if b >= 2:
                            w.append((s_mm, omm[(g, b - 2, 1)]))
                        P.emit("scalar",
                               (lambda o, i: (lambda e: e.activation(
                                   out=o, in_=i, func=AF.Exp, scale=ISQRT_D)))(
                                   PSL[b % 2][tkc][:], pb[tkc % 2][:]),
                               waits=w, incs=[(s_evs, 1)])
                        expc[(g, b, tkc)] = s_evs.n
                    for tkc in range(4):
                        mm(pb[7][0:1, :], ones_s[:, 0:1], PSL[b % 2][tkc][:],
                           start=(b == 0 and tkc == 0), stop=(b == NB - 1 and tkc == 3),
                           waits=[(s_evs, expc[(g, b, tkc)])] +
                                 ([(s_evv, sums_fin.get(g - 1, 0))]
                                  if (b == 0 and tkc == 0) else []),
                           inc=(s_mm if (b == NB - 1 and tkc == 3) else None))
                    if b == NB - 1:
                        smm[(g, "sums")] = s_mm.n
                    w = [(s_evv, flsh.get((g, b - 1, 1), flsh.get((g - 1, 31, 1), 0)))]
                    for i in range(4):
                        dvc = i
                        for tkc in range(4):
                            mm(pb[2 + i][:],
                               VSLOT[sl][:, tkc * 1024 + dvc * 128: tkc * 1024 + dvc * 128 + 128],
                               PSL[b % 2][tkc][:],
                               start=(tkc == 0), stop=(tkc == 3),
                               waits=(w + [(s_kva[sl], kv_hist[(g, b, "v")])]
                                      if (i == 0 and tkc == 0) else ()),
                               inc=(s_mm if (i == 3 and tkc == 3) else None))
                    omm[(g, b, 0)] = s_mm.n
                    for bb_, hh in ((b - 1, 1), (b, 0)):
                        if bb_ < 0:
                            continue
                        for i in range(4):
                            dvc = hh * 4 + i
                            dst = OSB[:, dvc * 512: dvc * 512 + 512]
                            if bb_ == 0:
                                fn = (lambda o, i_: (lambda e: e.tensor_copy(out=o, in_=i_)))(
                                    dst, pb[2 + i][:])
                            else:
                                fn = (lambda o, i_: (lambda e: e.tensor_tensor(
                                    out=o, in0=i_, in1=o, op=ALU.add)))(dst, pb[2 + i][:])
                            P.emit("vector", fn,
                                   waits=([(s_mm, omm[(g, bb_, hh)])] if i == 0 else []),
                                   incs=([(s_evv, 1)] if i == 3 else []))
                        flsh[(g, bb_, hh)] = s_evv.n
                bp = NB - 1
                slp = bp % 2
                w = [(s_evv, flsh[(g, bp, 0)])]
                for i in range(4):
                    dvc = 4 + i
                    for tkc in range(4):
                        mm(pb[2 + i][:],
                           VSLOT[slp][:, tkc * 1024 + dvc * 128: tkc * 1024 + dvc * 128 + 128],
                           PSL[bp % 2][tkc][:],
                           start=(tkc == 0), stop=(tkc == 3),
                           waits=(w if (i == 0 and tkc == 0) else ()),
                           inc=(s_mm if (i == 3 and tkc == 3) else None))
                omm[(g, bp, 1)] = s_mm.n
                for i in range(4):
                    dvc = 4 + i
                    dst = OSB[:, dvc * 512: dvc * 512 + 512]
                    P.emit("vector",
                           (lambda o, i_: (lambda e: e.tensor_tensor(
                               out=o, in0=i_, in1=o, op=ALU.add)))(dst, pb[2 + i][:]),
                           waits=([(s_mm, omm[(g, bp, 1)])] if i == 0 else []),
                           incs=([(s_evv, 1)] if i == 3 else []))
                flsh[(g, bp, 1)] = s_evv.n

                P.emit("vector",
                       (lambda o, i: (lambda e: e.tensor_copy(out=o, in_=i)))(
                           st1[:], pb[7][0:1, :]),
                       waits=[(s_mm, smm[(g, "sums")])])
                P.emit("vector",
                       (lambda o, i: (lambda e: e.reciprocal(o, i)))(st2[:], st1[:]),
                       incs=[(s_evv, 1)])
                sums_fin[g] = s_evv.n
                RECIP = s_evv.n
                mm(pb[6][:], ones_s[0:1, :], st2[0:1, :], start=True, stop=True,
                   waits=[(s_evv, RECIP)], inc=s_mm)
                RMM = s_mm.n
                P.emit("vector",
                       (lambda o, i: (lambda e: e.tensor_copy(out=o, in_=i)))(
                           RREC[:], pb[6][:]),
                       waits=[(s_mm, RMM)], incs=[(s_evv, 1)])
                RRDY = s_evv.n
                for dvc in range(8):
                    ev = EVA if dvc % 2 == 0 else EVB
                    P.emit("vector",
                           (lambda o, a, r_: (lambda e: e.tensor_tensor(
                               out=o, in0=a, in1=r_, op=ALU.mult)))(
                               ev[:], OSB[:, dvc * 512: dvc * 512 + 512], RREC[:]),
                           waits=[(s_evv, RRDY),
                                  (s_st2[dvc % 2],
                                   attn_st.get((g, dvc - 2),
                                               attn_st.get((g - 1, 6 + (dvc % 2)), 0)))],
                           incs=[(s_evv, 1)])
                    evn = s_evv.n
                    attn_st[(g, dvc)] = dma(
                        "gpsimd",
                        o_scr[:, g * 4096 + dvc * 512: g * 4096 + dvc * 512 + 512],
                        ev[:], s_st2[dvc % 2], waits=[(s_evv, evn)])
            barrier()

        if "ffn" in dbg_phases:
            ffn_mm = {}
            ffn_ev = {}
            y_hist = {}
            statclr = {"v": 0}
            ln_out_last = {"v": 0}
            x1_read_last = {"v": 0}
            for qq in range(NQ):
                OAT = big2[:, 0:4096]
                OL = dma("sync", OAT, o_scr[:, qq * 4096: qq * 4096 + 4096], s_q,
                         waits=[(s_evv, ln_out_last["v"])])
                if qq == 0:
                    dma("sync", wr_s[:], wr_img, s_c)
                WR_RDY = s_c.n
                for oc in range(32):
                    slot = wbuf[:, (oc % 2) * 1024 + 2048:(oc % 2) * 1024 + 3072]
                    WL = dma("sync", slot, wo_img[oc], s_w2[oc % 2],
                             waits=[(s_mm, ffn_mm.get((qq, "proj", oc - 2),
                                                      ffn_mm.get((qq - 1, "base", 31), 0)))])
                    xs = XSA if oc % 2 == 0 else XSB
                    XL = dma("gpsimd", xs[:],
                             x_img[:, oc * TC + qq * QT: oc * TC + qq * QT + 512],
                             s_xp[oc % 2], waits=[(s_evv, ffn_ev.get((qq, "r1", oc - 2), 0))])
                    bank = pb[oc % 2]
                    w = [(s_w2[oc % 2], WL), (s_q, OL),
                         (s_evv, ffn_ev.get((qq, "r1", oc - 2), 0))]
                    for dc in range(8):
                        mm(bank[:], slot[:, dc * 128:(dc + 1) * 128],
                           OAT[:, dc * 512: dc * 512 + 512],
                           start=(dc == 0), stop=(dc == 7),
                           waits=(w if dc == 0 else ()),
                           inc=(s_mm if dc == 7 else None))
                    ffn_mm[(qq, "proj", oc)] = s_mm.n
                    r1w = [(s_mm, ffn_mm[(qq, "proj", oc)]), (s_xp[oc % 2], XL)]
                    if oc == 0:
                        r1w.append((s_mm, x1_read_last["v"]))
                    P.emit("vector",
                           (lambda o, p_, b_, x_: (lambda e: e.scalar_tensor_tensor(
                               out=o, in0=p_, scalar=b_, in1=x_,
                               op0=ALU.add, op1=ALU.add)))(
                               big1[:, oc * 512: oc * 512 + 512], bank[:],
                               BO[:, oc:oc + 1], xs[:]),
                           waits=r1w,
                           incs=[(s_evv, 1)])
                    ffn_ev[(qq, "r1", oc)] = s_evv.n
                    sq = SQA if oc % 2 == 0 else SQB
                    P.emit("scalar",
                           (lambda o, i: (lambda e: e.activation(
                               out=o, in_=i, func=AF.Square)))(
                               sq[:], big1[:, oc * 512: oc * 512 + 512]),
                           waits=[(s_evv, ffn_ev[(qq, "r1", oc)])] +
                                 ([(s_mm, ffn_mm.get((qq, "sqm", oc - 2), 0))]
                                  if oc >= 2 else []),
                           incs=[(s_evs, 1)])
                    ffn_ev[(qq, "sq", oc)] = s_evs.n
                    mm(pb[6][0:1, :], ones_s[:, 0:1],
                       big1[:, oc * 512: oc * 512 + 512],
                       start=(oc == 0), stop=(oc == 31),
                       waits=[(s_evv, ffn_ev[(qq, "r1", oc)])] +
                             ([(s_evv, statclr["v"])] if oc == 0 else []),
                       inc=(s_mm if oc == 31 else None))
                    mm(pb[7][0:1, :], ones_s[:, 0:1], sq[:],
                       start=(oc == 0), stop=(oc == 31),
                       waits=[(s_evs, ffn_ev[(qq, "sq", oc)])],
                       inc=(s_mm if oc == 31 else None))
                    if oc == 31:
                        ffn_mm[(qq, "stats")] = s_mm.n
                    ffn_mm[(qq, "sqm", oc)] = s_mm.n

                def ln_apply(tag, src_big, stats_mm_val, out_to_y):
                    P.emit("vector",
                           (lambda: (lambda e: e.tensor_copy(out=st1[:], in_=pb[6][0:1, :])))(),
                           waits=[(s_mm, stats_mm_val)])
                    P.emit("vector",
                           (lambda: (lambda e: e.tensor_copy(out=st2[:], in_=pb[7][0:1, :])))(),
                           incs=[(s_evv, 1)])
                    statclr["v"] = s_evv.n
                    P.emit("vector",
                           (lambda: (lambda e: e.tensor_scalar_mul(st3[:], st1[:], 1.0 / D)))())
                    P.emit("vector",
                           (lambda: (lambda e: e.tensor_scalar_mul(st6[:], st2[:], 1.0 / D)))())
                    P.emit("vector",
                           (lambda: (lambda e: e.tensor_tensor(
                               out=st4[:], in0=st3[:], in1=st3[:], op=ALU.mult)))())
                    P.emit("vector",
                           (lambda: (lambda e: e.tensor_tensor(
                               out=st2[:], in0=st6[:], in1=st4[:], op=ALU.subtract)))())
                    P.emit("vector",
                           (lambda: (lambda e: e.tensor_scalar_add(st6[:], st2[:], EPS)))(),
                           incs=[(s_evv, 1)])
                    VEV = s_evv.n
                    P.emit("scalar",
                           (lambda: (lambda e: e.activation(out=st1[:], in_=st6[:], func=AF.Ln)))(),
                           waits=[(s_evv, VEV)])
                    P.emit("scalar",
                           (lambda: (lambda e: e.activation(
                               out=st4[:], in_=st1[:], func=AF.Exp, scale=-0.5)))(),
                           incs=[(s_evs, 1)])
                    SEV = s_evs.n
                    P.emit("vector",
                           (lambda: (lambda e: e.tensor_tensor(
                               out=st5[:], in0=st3[:], in1=st4[:], op=ALU.mult)))(),
                           waits=[(s_evs, SEV)])
                    P.emit("vector",
                           (lambda: (lambda e: e.tensor_scalar_mul(st3[:], st5[:], -1.0)))(),
                           incs=[(s_evv, 1)])
                    NEV = s_evv.n
                    mm(pb[2][:], ones_s[0:1, :], st4[0:1, :], start=True, stop=True,
                       waits=[(s_evv, NEV)], inc=s_mm)
                    mm(pb[3][:], ones_s[0:1, :], st3[0:1, :], start=True, stop=True,
                       inc=s_mm)
                    REPL = s_mm.n
                    P.emit("vector",
                           (lambda: (lambda e: e.tensor_copy(out=R128[:], in_=pb[2][:])))(),
                           waits=[(s_mm, REPL)])
                    P.emit("vector",
                           (lambda: (lambda e: e.tensor_copy(out=M128[:], in_=pb[3][:])))(),
                           incs=[(s_evv, 1)])
                    RRDY2 = s_evv.n
                    for dc in range(32):
                        P.emit("vector",
                               (lambda o, i: (lambda e: e.tensor_tensor(
                                   out=o, in0=i, in1=R128[:], op=ALU.mult)))(
                                   OUTT[:], src_big[:, dc * 512: dc * 512 + 512]),
                               waits=([(s_evv, RRDY2)] if dc == 0 else []) +
                                     ([(s_y, y_hist.get((qq, dc - 1), 0))] if out_to_y else []))
                        P.emit("vector",
                               (lambda o: (lambda e: e.tensor_tensor(
                                   out=o, in0=OUTT[:], in1=M128[:], op=ALU.add)))(
                                   src_big[:, dc * 512: dc * 512 + 512] if not out_to_y
                                   else OUT2[:]),
                               incs=[(s_evv, 1)])
                        ffn_ev[(qq, tag, dc)] = s_evv.n
                        if out_to_y:
                            y_hist[(qq, dc)] = dma(
                                "gpsimd",
                                y_img[:, dc * TC + qq * QT: dc * TC + qq * QT + 512],
                                OUT2[:], s_y, waits=[(s_evv, ffn_ev[(qq, tag, dc)])])
                    if out_to_y:
                        ln_out_last["v"] = s_evv.n

                ln_apply("x1", big1, ffn_mm[(qq, "stats")], out_to_y=False)
                X1_LAST = s_evv.n
                if dbg_outs:
                    dma("gpsimd", dbg["d_x1"][qq], big1[:], s_y,
                        waits=[(s_evv, X1_LAST)])
                    dma("gpsimd", dbg["d_st"][qq][0:1, :], st1[:], s_y)
                    dma("gpsimd", dbg["d_st"][qq][1:2, :], st2[:], s_y)
                    dma("gpsimd", dbg["d_st"][qq][2:3, :], st3[:], s_y)
                    dma("gpsimd", dbg["d_st"][qq][3:4, :], st4[:], s_y)

                for tc in range(4):
                    for kc in range(32):
                        mm(pb[4][:, 0:8],
                           big1[:, kc * 512 + tc * 128: kc * 512 + tc * 128 + 128],
                           wr_s[:, kc * 8: kc * 8 + 8],
                           start=(kc == 0), stop=(kc == 31),
                           waits=([(s_evv, X1_LAST), (s_c, WR_RDY)] if kc == 0 else ()) +
                                 ([(s_evv, ffn_ev.get((qq, "wtr", tc - 1), 0))]
                                  if kc == 0 else ()),
                           inc=(s_mm if kc == 31 else None))
                    LMM = s_mm.n
                    P.emit("scalar",
                           (lambda: (lambda e: e.copy(out=lg[:], in_=pb[4][:, 0:8])))(),
                           waits=[(s_mm, LMM)], incs=[(s_evs, 1)])
                    LEV = s_evs.n
                    def vop(fn, waits=(), incs=()):
                        P.emit("vector", fn, waits=waits, incs=incs)
                        P.emit("vector", (lambda: (lambda e: e.engine_nop()))())
                        P.emit("vector", (lambda: (lambda e: e.engine_nop()))())
                    vop((lambda: (lambda e: e.max(out=srt[:], in_=lg[:])))(),
                        waits=[(s_evs, LEV)])
                    vop((lambda: (lambda e: e.tensor_tensor(
                        out=negm1[:], in0=srt[:, 0:1], in1=srt[:, 1:2],
                        op=ALU.subtract)))())
                    P.emit("vector", (lambda: (lambda e: e.engine_nop()))(),
                           incs=[(s_evv, 1)])
                    TEV = s_evv.n
                    P.emit("scalar",
                           (lambda: (lambda e: e.activation(
                               out=l2t[:], in_=negm1[:], func=AF.Sigmoid)))(),
                           waits=[(s_evv, TEV)], incs=[(s_evs, 1)])
                    AEV = s_evs.n
                    vop((lambda: (lambda e: e.tensor_scalar(
                        e2t[:], l2t[:], -2.0, 2.0, op0=ALU.mult, op1=ALU.add)))(),
                        waits=[(s_evs, AEV)])
                    vop((lambda: (lambda e: e.tensor_scalar(
                        r2t[:], l2t[:], 4.0, -2.0, op0=ALU.mult, op1=ALU.add)))())
                    vop((lambda: (lambda e: e.tensor_scalar(
                        msk[:], lg[:], srt[:, 0:1], None, op0=ALU.is_equal)))())
                    vop((lambda: (lambda e: e.tensor_scalar(
                        ee[:], lg[:], srt[:, 1:2], None, op0=ALU.is_ge)))())
                    vop((lambda: (lambda e: e.tensor_scalar(
                        lg[:], ee[:], e2t[:], None, op0=ALU.mult)))())
                    vop((lambda tc_: (lambda e: e.scalar_tensor_tensor(
                        out=w4[:, tc_ * 8: tc_ * 8 + 8], in0=msk[:], scalar=r2t[:],
                        in1=lg[:], op0=ALU.mult, op1=ALU.add)))(tc),
                        incs=[(s_evv, 1)])
                    WEV = s_evv.n
                    P.emit("tensor",
                           (lambda tc_: (lambda e: e.transpose(
                               pb[5][0:8, tc_ * 128: tc_ * 128 + 128],
                               w4[:, tc_ * 8: tc_ * 8 + 8], ident_s[:])))(tc),
                           waits=[(s_evv, WEV)], incs=[(s_mm, 1)])
                    ffn_mm[(qq, "wt", tc)] = s_mm.n
                    ffn_ev[(qq, "wtr", tc)] = WEV
                    if dbg_outs:
                        dma("gpsimd", dbg["d_lg"][qq][:, tc * 8:(tc + 1) * 8], lg[:], s_y,
                            waits=[(s_evv, WEV)])
                        dma("gpsimd", dbg["d_srt"][qq][:, tc * 8:(tc + 1) * 8], srt[:], s_y)
                        dma("gpsimd", dbg["d_e2"][qq][:, tc * 4 + 0: tc * 4 + 1], e2t[:], s_y, slow=True)
                        dma("gpsimd", dbg["d_e2"][qq][:, tc * 4 + 1: tc * 4 + 2], r2t[:], s_y, slow=True)
                        dma("gpsimd", dbg["d_e2"][qq][:, tc * 4 + 2: tc * 4 + 3], negm1[:], s_y, slow=True)
                        dma("gpsimd", dbg["d_e2"][qq][:, tc * 4 + 3: tc * 4 + 4], l2t[:], s_y, slow=True)
                P.emit("vector",
                       (lambda: (lambda e: e.tensor_copy(out=wem[0:8, 0:512], in_=pb[5][0:8, :])))(),
                       waits=[(s_mm, ffn_mm[(qq, "wt", 3)])], incs=[(s_evv, 1)])
                WEMEV = s_evv.n
                if dbg_outs:
                    dma("gpsimd", dbg["d_w4"][qq], w4[:], s_y, waits=[(s_evv, WEMEV)])
                    dma("gpsimd", dbg["d_wem"][qq], wem[:], s_y)
                mm(pb[2][:], rt_s[0:8, :], wem[0:8, 0:512], start=True, stop=True,
                   waits=[(s_evv, WEMEV)], inc=s_mm)
                SFMM = s_mm.n
                P.emit("vector",
                       (lambda: (lambda e: e.tensor_copy(out=SFM[:], in_=pb[2][:])))(),
                       waits=[(s_mm, SFMM)], incs=[(s_evv, 1)])
                SFEV = s_evv.n
                AL = dma("sync", wbuf[:, 0:4096], a_img, s_w2[0],
                         waits=[(s_mm, ffn_mm[(qq, "proj", 31)])])
                for kc in range(32):
                    mm(pb[3][:], wbuf[:, kc * 128: kc * 128 + 128],
                       big1[:, kc * 512: kc * 512 + 512],
                       start=(kc == 0), stop=(kc == 31),
                       waits=([(s_w2[0], AL), (s_evv, X1_LAST)] if kc == 0 else ()),
                       inc=(s_mm if kc == 31 else None))
                ffn_mm[(qq, "h")] = s_mm.n
                P.emit("scalar",
                       (lambda: (lambda e: e.copy(out=HT[:], in_=pb[3][:])))(),
                       waits=[(s_mm, ffn_mm[(qq, "h")])], incs=[(s_evs, 1)])
                HEV = s_evs.n
                P.emit("vector",
                       (lambda: (lambda e: e.tensor_tensor(
                           out=gq[:, 0:512], in0=HT[:], in1=SFM[:], op=ALU.mult)))(),
                       waits=[(s_evs, HEV), (s_evv, SFEV),
                              (s_mm, ffn_mm.get((qq - 1, "base", 31), 0))],
                       incs=[(s_evv, 1)])
                GEV = s_evv.n
                if dbg_outs:
                    dma("gpsimd", dbg["d_sfm"][qq], SFM, s_y, waits=[(s_evv, GEV)])
                    dma("gpsimd", dbg["d_g"][qq], gq[:, 0:512], s_y)
                for oc in range(32):
                    slot = wbuf[:, (oc % 2) * 4224:(oc % 2) * 4224 + 4224]
                    WL = dma("sync", slot, wbb_img[oc], s_w2[oc % 2],
                             waits=[(s_mm, ffn_mm.get((qq, "base", oc - 2),
                                                      ffn_mm[(qq, "h")]))])
                    bank = pb[oc % 2]
                    w = [(s_w2[oc % 2], WL), (s_evv, GEV),
                         (s_evv, ffn_ev.get((qq, "r2", oc - 2), 0))]
                    for dc in range(32):
                        mm(bank[:], slot[:, dc * 128:(dc + 1) * 128],
                           big1[:, dc * 512: dc * 512 + 512],
                           start=(dc == 0), stop=False,
                           waits=(w if dc == 0 else ()))
                    mm(bank[:], slot[:, 4096:4224], gq[:, 0:512],
                       start=False, stop=True, inc=s_mm)
                    ffn_mm[(qq, "base", oc)] = s_mm.n
                    P.emit("vector",
                           (lambda o, p_, b_, x_: (lambda e: e.scalar_tensor_tensor(
                               out=o, in0=p_, scalar=b_, in1=x_,
                               op0=ALU.add, op1=ALU.add)))(
                               big2[:, oc * 512: oc * 512 + 512], bank[:],
                               BB[:, oc:oc + 1],
                               big1[:, oc * 512: oc * 512 + 512]),
                           waits=[(s_mm, ffn_mm[(qq, "base", oc)])],
                           incs=[(s_evv, 1)])
                    ffn_ev[(qq, "r2", oc)] = s_evv.n
                    sq = SQA if oc % 2 == 0 else SQB
                    P.emit("scalar",
                           (lambda o, i: (lambda e: e.activation(
                               out=o, in_=i, func=AF.Square)))(
                               sq[:], big2[:, oc * 512: oc * 512 + 512]),
                           waits=[(s_evv, ffn_ev[(qq, "r2", oc)])] +
                                 ([(s_mm, ffn_mm.get((qq, "sqm2", oc - 2), 0))]
                                  if oc >= 2 else []),
                           incs=[(s_evs, 1)])
                    SQEV = s_evs.n
                    mm(pb[6][0:1, :], ones_s[:, 0:1],
                       big2[:, oc * 512: oc * 512 + 512],
                       start=(oc == 0), stop=(oc == 31),
                       waits=[(s_evv, ffn_ev[(qq, "r2", oc)])] +
                             ([(s_evv, statclr["v"])] if oc == 0 else []),
                       inc=(s_mm if oc == 31 else None))
                    mm(pb[7][0:1, :], ones_s[:, 0:1], sq[:],
                       start=(oc == 0), stop=(oc == 31),
                       waits=[(s_evs, SQEV)],
                       inc=(s_mm if oc == 31 else None))
                    if oc == 31:
                        ffn_mm[(qq, "stats2")] = s_mm.n
                    ffn_mm[(qq, "sqm2", oc)] = s_mm.n

                x1_read_last["v"] = ffn_mm[(qq, "base", 31)]
                if dbg_outs:
                    dma("gpsimd", dbg["d_r2"][qq], big2[:], s_y,
                        waits=[(s_mm, ffn_mm[(qq, "stats2")])])
                ln_apply("out", big2, ffn_mm[(qq, "stats2")], out_to_y=True)
            barrier()

        if dbg_outs:
            for q in range(NQ):
                dma("gpsimd", dbg["d_k"][q], k_gin[q], s_y)
                dma("gpsimd", dbg["d_v"][q], v_gin[q], s_y)
            dma("gpsimd", dbg["d_q"], q_scr, s_y)
            dma("gpsimd", dbg["d_q2"], q_scr, s_y)
            dma("gpsimd", dbg["d_o"], o_scr, s_y)

        for s in (s_st2[0], s_st2[1], s_y):
            if s.n > 0:
                P.emit("gpsimd", (lambda sh, vv: (lambda e: e.wait_ge(sh, vv)))(s.h, s.n))

        with nc.Block() as block:
            @block.sync
            def _(eng):
                P.replay("sync", eng)

            @block.tensor
            def _(eng):
                P.replay("tensor", eng)

            @block.vector
            def _(eng):
                P.replay("vector", eng)

            @block.scalar
            def _(eng):
                P.replay("scalar", eng)

            @block.gpsimd
            def _(eng):
                P.replay("gpsimd", eng)

    return nc


def stage_inputs(inputs):
    f32 = np.float32
    x = np.asarray(inputs["L0_attn_x"], dtype=f32)[0]
    Wqkv = np.asarray(inputs["L0_attn_Wqkv"], dtype=f32)
    bqkv = np.asarray(inputs["L0_attn_bqkv"], dtype=f32)
    Wo = np.asarray(inputs["L0_attn_Wo"], dtype=f32)
    bo = np.asarray(inputs["L0_attn_bo"], dtype=f32)
    Wb = np.asarray(inputs["L0_ffn_W_base"], dtype=f32)
    bb = np.asarray(inputs["L0_ffn_b_base"], dtype=f32)
    Wr = np.asarray(inputs["L0_ffn_W_router"], dtype=f32)
    lA = np.asarray(inputs["L0_ffn_lora_A"], dtype=f32)
    lB = np.asarray(inputs["L0_ffn_lora_B"], dtype=f32)

    shared = {}
    def qk_img(W):
        return np.ascontiguousarray(
            W.reshape(32, 128, 8, 128).transpose(2, 1, 0, 3).reshape(8, 128, 4096))
    shared["wq_img"] = qk_img(Wqkv[:, 0:1024])
    shared["wk_img"] = qk_img(Wqkv[:, 1024:2048])
    shared["wv_img"] = np.ascontiguousarray(Wqkv[:, 2048:3072].reshape(32, 128, 1024))
    shared["wo_img"] = np.ascontiguousarray(
        Wo.reshape(8, 128, 32, 128).transpose(2, 1, 0, 3).reshape(32, 128, 1024))
    wb_img = Wb.reshape(32, 128, 32, 128).transpose(0, 3, 2, 1).reshape(32, 128, 4096)
    b_img = lB.reshape(8, 32, 128, 16).transpose(0, 3, 1, 2).reshape(128, 4096)
    b_chunks = b_img.reshape(128, 32, 128).transpose(1, 0, 2)
    shared["wbb_img"] = np.ascontiguousarray(
        np.concatenate([wb_img, b_chunks], axis=2))
    shared["a_img"] = np.ascontiguousarray(
        lA.reshape(8, 16, 32, 128).transpose(3, 2, 0, 1).reshape(128, 4096))
    shared["wr_img"] = np.ascontiguousarray(
        Wr.T.reshape(32, 128, 8).transpose(1, 0, 2).reshape(128, 256))
    shared["bq_img"] = np.ascontiguousarray(bqkv[0:1024].reshape(8, 128).T)
    shared["bk_img"] = np.ascontiguousarray(bqkv[1024:2048].reshape(8, 128).T)
    shared["bv_row"] = np.ascontiguousarray(bqkv[2048:3072].reshape(1, 1024))
    shared["bo_img"] = np.ascontiguousarray(bo.reshape(32, 128).T)
    shared["bb_img"] = np.ascontiguousarray(bb.reshape(32, 128).T)
    shared["ones_in"] = np.ones((128, 128), dtype=f32)
    shared["ident_in"] = np.eye(128, dtype=f32)
    rt = np.zeros((8, 128), dtype=f32)
    for e in range(8):
        rt[e, e * 16:(e + 1) * 16] = 1.0
    shared["rt_in"] = rt

    in_maps = []
    for c in range(NCORE):
        xs = x[c * TC:(c + 1) * TC]
        xi = np.ascontiguousarray(
            xs.reshape(TC, 32, 128).transpose(2, 1, 0).reshape(128, 32 * TC))
        m = dict(shared)
        m["x_img"] = xi
        in_maps.append(m)
    return in_maps


def assemble_output(results):
    out = np.empty((1, NCORE * TC, D), dtype=np.float32)
    for c in range(NCORE):
        yi = results[c]["y_img"].reshape(128, 32, TC)
        out[0, c * TC:(c + 1) * TC, :] = yi.transpose(2, 1, 0).reshape(TC, D)
    return out


def kernel(**inputs):
    in_maps = stage_inputs(inputs)
    nc = build_program()
    res = run_bass_kernel_spmd(nc, in_maps, core_ids=list(range(NCORE)))
    return assemble_output(res.results)


# version 02
# speedup vs baseline: 1.1571x; 1.1571x over previous
import numpy as np
from contextlib import ExitStack

import concourse.bass as bass
import concourse.mybir as mybir
from concourse.bass_utils import run_bass_kernel_spmd

F32 = mybir.dt.float32
BF16 = mybir.dt.bfloat16
AF = mybir.ActivationFunctionType
ALU = mybir.AluOpType

NCORE = 8
D = 4096
DV = 1024
TC = 2048
QT = 512
NQ = TC // QT
EPS = 1e-5
ISQRT_D = 1.0 / 32.0


class Sem:
    __slots__ = ("h", "n")

    def __init__(self, h):
        self.h = h
        self.n = 0

    def bump(self, k):
        self.n += k
        return self.n


class Prog:
    def __init__(self):
        self.ops = []

    def emit(self, eng, fn, waits=(), incs=()):
        rec_incs = []
        for s, k in incs:
            s.bump(k)
            rec_incs.append((s, k))
        self.ops.append((eng, fn, tuple(waits), tuple(rec_incs)))

    def replay(self, name, engine):
        for eng, fn, waits, incs in self.ops:
            if eng != name:
                continue
            for s, v in waits:
                if v > 0:
                    engine.wait_ge(s.h, v)
            ins = fn(engine)
            for s, k in incs:
                ins.then_inc(s.h, k)


def build_program(dbg_phases=("qkv", "attn", "ffn"), dbg_outs=False):
    nc = bass.Bass("TRN2", target_bir_lowering=False)

    x_img = nc.dram_tensor("x_img", [128, 32 * TC], F32, kind="ExternalInput").ap()
    wq_img = nc.dram_tensor("wq_img", [8, 128, 4096], F32, kind="ExternalInput").ap()
    wk_img = nc.dram_tensor("wk_img", [8, 128, 4096], F32, kind="ExternalInput").ap()
    wv_img = nc.dram_tensor("wv_img", [32, 128, 1024], F32, kind="ExternalInput").ap()
    wo_img = nc.dram_tensor("wo_img", [32, 128, 1024], F32, kind="ExternalInput").ap()
    wbb_img = nc.dram_tensor("wbb_img", [32, 128, 4224], F32, kind="ExternalInput").ap()
    a_img = nc.dram_tensor("a_img", [128, 4096], F32, kind="ExternalInput").ap()
    wr_img = nc.dram_tensor("wr_img", [128, 256], F32, kind="ExternalInput").ap()
    bq_img = nc.dram_tensor("bq_img", [128, 8], F32, kind="ExternalInput").ap()
    bk_img = nc.dram_tensor("bk_img", [128, 8], F32, kind="ExternalInput").ap()
    bv_row = nc.dram_tensor("bv_row", [1, 1024], F32, kind="ExternalInput").ap()
    bo_img = nc.dram_tensor("bo_img", [128, 32], F32, kind="ExternalInput").ap()
    bb_img = nc.dram_tensor("bb_img", [128, 32], F32, kind="ExternalInput").ap()
    ones_in = nc.dram_tensor("ones_in", [128, 128], F32, kind="ExternalInput").ap()
    onesb_in = nc.dram_tensor("onesb_in", [128, 128], BF16, kind="ExternalInput").ap()
    ident_in = nc.dram_tensor("ident_in", [128, 128], F32, kind="ExternalInput").ap()
    rt_in = nc.dram_tensor("rt_in", [8, 128], F32, kind="ExternalInput").ap()

    y_img = nc.dram_tensor("y_img", [128, 32 * TC], F32, kind="ExternalOutput").ap()

    wq_pr = nc.dram_tensor("wq_pr", [8, 128, 8192], BF16).ap()
    wk_pr = nc.dram_tensor("wk_pr", [8, 128, 8192], BF16).ap()
    wv_pr = nc.dram_tensor("wv_pr", [32, 128, 2048], BF16).ap()
    wo_pr = nc.dram_tensor("wo_pr", [32, 128, 2048], BF16).ap()
    wbb_pr = nc.dram_tensor("wbb_pr", [32, 128, 8448], BF16).ap()
    a_pr = nc.dram_tensor("a_pr", [128, 8192], BF16).ap()
    wr_pr = nc.dram_tensor("wr_pr", [128, 512], BF16).ap()
    k_gin = [nc.dram_tensor(f"k_gin{q}", [128, 8192], BF16).ap() for q in range(NQ)]
    v_gin = [nc.dram_tensor(f"v_gin{q}", [128, 8192], BF16).ap() for q in range(NQ)]
    k_gout = [nc.dram_tensor(f"k_gout{q}", [8 * 128, 8192], BF16).ap() for q in range(NQ)]
    v_gout = [nc.dram_tensor(f"v_gout{q}", [8 * 128, 8192], BF16).ap() for q in range(NQ)]
    q_scr = nc.dram_tensor("q_scr", [128, NQ * 8192], BF16).ap()
    o_scr = nc.dram_tensor("o_scr", [128, NQ * 8192], BF16).ap()
    r1_scr = nc.dram_tensor("r1_scr", [128, 16384], F32).ap()
    r2_scr = nc.dram_tensor("r2_scr", [128, 16384], F32).ap()

    dbg = {}
    if dbg_outs:
        dbg["d_q"] = nc.dram_tensor("d_q", [128, NQ * 8192], BF16, kind="ExternalOutput").ap()
        dbg["d_k"] = nc.dram_tensor("d_k", [NQ, 128, 8192], BF16, kind="ExternalOutput").ap()
        dbg["d_v"] = nc.dram_tensor("d_v", [NQ, 128, 8192], BF16, kind="ExternalOutput").ap()
        dbg["d_o"] = nc.dram_tensor("d_o", [128, NQ * 8192], BF16, kind="ExternalOutput").ap()
        dbg["d_x1"] = nc.dram_tensor("d_x1", [NQ, 128, 32768], BF16, kind="ExternalOutput").ap()
        dbg["d_r2"] = nc.dram_tensor("d_r2", [NQ, 128, 16384], F32, kind="ExternalOutput").ap()
        dbg["d_w4"] = nc.dram_tensor("d_w4", [NQ, 128, 32], F32, kind="ExternalOutput").ap()

    es = ExitStack()
    with es:
        abf = es.enter_context(nc.sbuf_tensor("abf", [128, 32768], BF16))
        big2 = es.enter_context(nc.sbuf_tensor("big2", [128, 8448], F32))
        cbf = es.enter_context(nc.sbuf_tensor("cbf", [128, 16384], BF16))
        wbf = es.enter_context(nc.sbuf_tensor("wbf", [128, 16896], BF16))
        misc = es.enter_context(nc.sbuf_tensor("misc", [128, 4096], F32))
        wr_s = es.enter_context(nc.sbuf_tensor("wr_s", [128, 512], BF16))
        gq = es.enter_context(nc.sbuf_tensor("gq", [128, 1024], BF16))
        consts = es.enter_context(nc.sbuf_tensor("consts", [128, 256], F32))
        onesb = es.enter_context(nc.sbuf_tensor("onesb", [128, 128], BF16))
        rt_s = es.enter_context(nc.sbuf_tensor("rt_s", [8, 128], F32))
        bias_s = es.enter_context(nc.sbuf_tensor("bias_s", [128, 80], F32))
        bv_s = es.enter_context(nc.sbuf_tensor("bv_s", [1, 1024], F32))
        wem = es.enter_context(nc.sbuf_tensor("wem", [8, 512], F32))
        lg = es.enter_context(nc.sbuf_tensor("lg", [128, 8], F32))
        srt = es.enter_context(nc.sbuf_tensor("srt", [128, 8], F32))
        ee = es.enter_context(nc.sbuf_tensor("ee", [128, 8], F32))
        msk = es.enter_context(nc.sbuf_tensor("msk", [128, 8], F32))
        negm1 = es.enter_context(nc.sbuf_tensor("negm1", [128, 1], F32))
        l2t = es.enter_context(nc.sbuf_tensor("l2t", [128, 1], F32))
        e2t = es.enter_context(nc.sbuf_tensor("e2t", [128, 1], F32))
        r2t = es.enter_context(nc.sbuf_tensor("r2t", [128, 1], F32))
        w4 = es.enter_context(nc.sbuf_tensor("w4", [128, 32], F32))
        st1 = es.enter_context(nc.sbuf_tensor("st1", [1, 512], F32))
        st2 = es.enter_context(nc.sbuf_tensor("st2", [1, 512], F32))
        st3 = es.enter_context(nc.sbuf_tensor("st3", [1, 512], F32))
        st4 = es.enter_context(nc.sbuf_tensor("st4", [1, 512], F32))
        st6 = es.enter_context(nc.sbuf_tensor("st6", [1, 512], F32))

        pb = [es.enter_context(nc.psum_tensor(f"pb{i}", [128, 512], F32)) for i in range(8)]

        sems = {}
        for name in ("s_c", "s_xp0", "s_xp1", "s_w0", "s_w1",
                     "s_wv0", "s_wv1", "s_wv2", "s_wv3",
                     "s_pw0", "s_pw1", "s_pst0", "s_pst1",
                     "s_q", "s_kk0", "s_kk1", "s_kva0", "s_kva1",
                     "s_r20", "s_r21",
                     "s_st0", "s_st1", "s_y",
                     "s_mm", "s_evs", "s_evv", "s_cc"):
            sems[name] = Sem(es.enter_context(nc.semaphore(name)))
        s_c, s_y = sems["s_c"], sems["s_y"]
        s_xp = [sems["s_xp0"], sems["s_xp1"]]
        s_w2 = [sems["s_w0"], sems["s_w1"]]
        s_wv4 = [sems["s_wv0"], sems["s_wv1"], sems["s_wv2"], sems["s_wv3"]]
        s_pw = [sems["s_pw0"], sems["s_pw1"]]
        s_pst = [sems["s_pst0"], sems["s_pst1"]]
        s_q = sems["s_q"]
        s_kk = [sems["s_kk0"], sems["s_kk1"]]
        s_kva = [sems["s_kva0"], sems["s_kva1"]]
        s_r2l = [sems["s_r20"], sems["s_r21"]]
        s_st2 = [sems["s_st0"], sems["s_st1"]]
        s_mm, s_evs, s_evv = sems["s_mm"], sems["s_evs"], sems["s_evv"]
        s_cc = sems["s_cc"]

        P = Prog()

        def dma(q, out_ap, in_ap, sem, waits=()):
            P.emit(q, (lambda o, i: (lambda e: e.dma_start(out=o, in_=i)))(out_ap, in_ap),
                   waits=waits, incs=[(sem, 16)])
            return sem.n

        def mm(out_ap, lhsT, rhs, start, stop, waits=(), inc=None):
            P.emit("tensor",
                   (lambda o, l, r, st, sp: (lambda e: e.matmul(o, l, r, start=st, stop=sp)))(
                       out_ap, lhsT, rhs, start, stop),
                   waits=waits, incs=([(inc, 1)] if inc is not None else []))
            return inc.n if inc is not None else None

        def mm3(out_ap, lh, ll, rh, rl, start, stop, waits=(), inc=None):
            mm(out_ap, lh, rh, start=start, stop=False, waits=waits)
            mm(out_ap, lh, rl, start=False, stop=False)
            return mm(out_ap, ll, rh, start=False, stop=stop, inc=(inc if stop else None))

        EVA = misc[:, 0:512]
        EVB = misc[:, 512:1024]
        XSA = misc[:, 1024:1536]
        XSB = misc[:, 1536:2048]
        R128 = misc[:, 2048:2560]
        M128 = misc[:, 2560:3072]
        SQA = misc[:, 3072:3584]
        SQB = misc[:, 3584:4096]
        HT = misc[:, 2048:2560]
        SFM = misc[:, 2560:3072]
        OUTT = misc[:, 3072:3584]
        OUT2 = misc[:, 3584:4096]
        R2LA = misc[:, 1024:1536]
        R2LB = misc[:, 1536:2048]
        PFA = misc[:, 1024:1536]
        PFB = misc[:, 1536:2048]

        ones_s = consts[:, 0:128]
        ident_s = consts[:, 128:256]
        BQ = bias_s[:, 0:8]
        BK = bias_s[:, 8:16]
        BO = bias_s[:, 16:48]
        BB = bias_s[:, 48:80]

        dma("sync", consts[:, 0:128], ones_in, s_c)
        dma("sync", consts[:, 128:256], ident_in, s_c)
        dma("sync", onesb[:], onesb_in, s_c)
        dma("sync", rt_s[:], rt_in, s_c)
        dma("sync", bias_s[:, 0:8], bq_img, s_c)
        dma("sync", bias_s[:, 8:16], bk_img, s_c)
        dma("sync", bias_s[:, 16:48], bo_img, s_c)
        dma("sync", bias_s[:, 48:80], bb_img, s_c)
        dma("sync", bv_s[:], bv_row, s_c)
        CONST_W = s_c.n
        for engn in ("tensor", "vector", "scalar", "gpsimd"):
            P.emit(engn, (lambda sh, vv: (lambda e: e.wait_ge(sh, vv)))(s_c.h, CONST_W))

        def barrier():
            vals = [(s, s.n) for s in sems.values() if s.n > 0]
            for engn in ("sync", "gpsimd", "tensor", "vector", "scalar"):
                for s, v in vals:
                    P.emit(engn, (lambda sh, vv: (lambda e: e.wait_ge(sh, vv)))(s.h, v))

        prep_items = []
        for fc in range(8):
            prep_items.append((wq_img[fc], wq_pr[fc], 4096))
            prep_items.append((wk_img[fc], wk_pr[fc], 4096))
        for kc in range(32):
            prep_items.append((wv_img[kc], wv_pr[kc], 1024))
        for oc in range(32):
            prep_items.append((wo_img[oc], wo_pr[oc], 1024))
        for oc in range(32):
            prep_items.append((wbb_img[oc], wbb_pr[oc], 4224))
        prep_items.append((a_img, a_pr, 4096))
        prep_items.append((wr_img, wr_pr, 256))

        prep_ev = {}
        prep_st = {}
        for i, (src, dst, ncols) in enumerate(prep_items):
            sl = i % 2
            stg = big2[:, sl * 4224: sl * 4224 + ncols]
            po_hi = abf[:, sl * 8448: sl * 8448 + ncols]
            po_lo = abf[:, sl * 8448 + ncols: sl * 8448 + 2 * ncols]
            LW = dma("sync", stg, src, s_pw[sl],
                     waits=[(s_evv, prep_ev.get(i - 2, 0))])
            P.emit("scalar",
                   (lambda o, i_: (lambda e: e.copy(out=o, in_=i_)))(po_hi, stg),
                   waits=[(s_pw[sl], LW), (s_pst[sl], prep_st.get(i - 2, 0))],
                   incs=[(s_evs, 1)])
            HEVp = s_evs.n
            P.emit("vector",
                   (lambda o, a, b: (lambda e: e.tensor_tensor(
                       out=o, in0=a, in1=b, op=ALU.subtract)))(po_lo, stg, po_hi),
                   waits=[(s_evs, HEVp)], incs=[(s_evv, 1)])
            prep_ev[i] = s_evv.n
            prep_st[i] = dma("gpsimd", dst,
                             abf[:, sl * 8448: sl * 8448 + 2 * ncols], s_pst[sl],
                             waits=[(s_evv, prep_ev[i])])
        barrier()

        if "qkv" in dbg_phases:
            ev_hist_qk = {}
            last_v_evacs = 0
            mm_hist = {}
            st_hist = {}
            xsplit = {}
            for qq in range(NQ):
                for kc in range(32):
                    sl = kc % 2
                    xs = XSA if sl == 0 else XSB
                    xw = [(s_evs, xsplit.get((qq, kc - 2, "hi"), 0)),
                          (s_evv, xsplit.get((qq, kc - 2, "lo"), 0))]
                    if kc < 2:
                        xw.append((s_mm, mm_hist.get(("vlast", qq - 1), 0)))
                    XL = dma("gpsimd", xs[:],
                             x_img[:, kc * TC + qq * QT: kc * TC + qq * QT + 512],
                             s_xp[sl], waits=xw)
                    xh = abf[:, kc * 512: kc * 512 + 512]
                    xl = abf[:, 16384 + kc * 512: 16384 + kc * 512 + 512]
                    hw = [(s_xp[sl], XL)]
                    if kc < 2:
                        hw.append((s_mm, mm_hist.get(("vlast", qq - 1), 0)))
                    P.emit("scalar",
                           (lambda o, i_: (lambda e: e.copy(out=o, in_=i_)))(xh, xs[:]),
                           waits=hw, incs=[(s_evs, 1)])
                    xsplit[(qq, kc, "hi")] = s_evs.n
                    P.emit("vector",
                           (lambda o, a, b: (lambda e: e.tensor_tensor(
                               out=o, in0=a, in1=b, op=ALU.subtract)))(xl, xs[:], xh),
                           waits=[(s_evs, xsplit[(qq, kc, "hi")])], incs=[(s_evv, 1)])
                    xsplit[(qq, kc, "lo")] = s_evv.n
                XRDY_S = xsplit[(qq, 31, "hi")]
                XRDY_V = xsplit[(qq, 31, "lo")]

                for j in range(16):
                    fc = j % 8
                    isq = j < 8
                    img = wq_pr if isq else wk_pr
                    slot = wbf[:, (j % 2) * 8448:(j % 2) * 8448 + 8192]
                    wv_wait = mm_hist.get((qq, "qk", j - 2), 0)
                    WL = dma("sync", slot, img[fc], s_w2[j % 2],
                             waits=[(s_mm, wv_wait)] if j >= 2 else
                                   ([(s_mm, mm_hist.get(("vlast", qq - 1), 0))] if qq > 0 else []))
                    bank = pb[j % 2]
                    mmw = [(s_w2[j % 2], WL), (s_evs, XRDY_S), (s_evv, XRDY_V),
                           (s_evv, ev_hist_qk.get((qq, j - 2), last_v_evacs))]
                    for kc in range(32):
                        mm3(bank[:],
                            slot[:, kc * 128: kc * 128 + 128],
                            slot[:, 4096 + kc * 128: 4096 + kc * 128 + 128],
                            abf[:, kc * 512: kc * 512 + 512],
                            abf[:, 16384 + kc * 512: 16384 + kc * 512 + 512],
                            start=(kc == 0), stop=(kc == 31),
                            waits=(mmw if kc == 0 else ()),
                            inc=(s_mm if kc == 31 else None))
                    mm_hist[(qq, "qk", j)] = s_mm.n
                    evh = cbf[:, (j % 2) * 1024: (j % 2) * 1024 + 512]
                    evl = cbf[:, (j % 2) * 1024 + 512: (j % 2) * 1024 + 1024]
                    bias_ap = (BQ if isq else BK)[:, fc:fc + 1]
                    ev_reuse = st_hist.get((qq, j - 2),
                                           st_hist.get((qq - 1, "v", 6 + (j % 2)), 0))
                    P.emit("scalar",
                           (lambda o, i, b: (lambda e: e.activation(
                               out=o, in_=i, func=AF.Identity, bias=b, scale=1.0)))(
                               evh, bank[:], bias_ap),
                           waits=[(s_mm, mm_hist[(qq, "qk", j)]),
                                  (s_st2[j % 2], ev_reuse)],
                           incs=[(s_evs, 1)])
                    EVH = s_evs.n
                    P.emit("vector",
                           (lambda o, p_, b_, h_: (lambda e: e.scalar_tensor_tensor(
                               out=o, in0=p_, scalar=b_, in1=h_,
                               op0=ALU.add, op1=ALU.subtract)))(
                               evl, bank[:], bias_ap, evh),
                           waits=[(s_evs, EVH)], incs=[(s_evv, 1)])
                    ev_hist_qk[(qq, j)] = s_evv.n
                    if isq:
                        dst = q_scr[:, qq * 8192 + fc * 1024: qq * 8192 + fc * 1024 + 1024]
                    else:
                        dst = k_gin[qq][:, fc * 1024: fc * 1024 + 1024]
                    st_hist[(qq, j)] = dma(
                        "gpsimd", dst, cbf[:, (j % 2) * 1024:(j % 2) * 1024 + 1024],
                        s_st2[j % 2], waits=[(s_evv, ev_hist_qk[(qq, j)])])

                wv_loads = {}
                for kc in range(32):
                    slot = wbf[:, 8448 + (kc % 4) * 2048: 8448 + (kc % 4) * 2048 + 2048]
                    w_waits = []
                    if kc >= 4:
                        w_waits.append((s_mm, mm_hist[(qq, "v", kc - 4)]))
                    else:
                        w_waits.append((s_mm, mm_hist[(qq, "qk", 15)]))
                    wv_loads[kc] = dma("sync", slot, wv_pr[kc], s_wv4[kc % 4], waits=w_waits)
                    mmw = [(s_wv4[kc % 4], wv_loads[kc])]
                    if kc == 0:
                        mmw.append((s_evv, ev_hist_qk[(qq, 15)]))
                    for t in range(8):
                        tkc, dvh = t // 2, t % 2
                        xh = abf[:, kc * 512 + tkc * 128: kc * 512 + tkc * 128 + 128]
                        xl = abf[:, 16384 + kc * 512 + tkc * 128: 16384 + kc * 512 + tkc * 128 + 128]
                        rh = slot[:, dvh * 512: dvh * 512 + 512]
                        rl = slot[:, 1024 + dvh * 512: 1024 + dvh * 512 + 512]
                        mm(pb[t][:], xh, rh, start=(kc == 0), stop=False,
                           waits=(mmw if t == 0 else ()))
                        mm(pb[t][:], xh, rl, start=False, stop=False)
                        mm(pb[t][:], xl, rh, start=False, stop=False,
                           inc=(s_mm if t == 7 else None))
                    mm_hist[(qq, "v", kc)] = s_mm.n
                for t in range(8):
                    tkc, dvh = t // 2, t % 2
                    mm(pb[t][:], ones_s[0:1, :], bv_s[0:1, dvh * 512: dvh * 512 + 512],
                       start=False, stop=True, inc=s_mm)
                    mm_hist[(qq, "vb", t)] = s_mm.n
                    evh = cbf[:, (t % 2) * 1024: (t % 2) * 1024 + 512]
                    evl = cbf[:, (t % 2) * 1024 + 512: (t % 2) * 1024 + 1024]
                    P.emit("scalar",
                           (lambda o, i: (lambda e: e.copy(out=o, in_=i)))(evh, pb[t][:]),
                           waits=[(s_mm, mm_hist[(qq, "vb", t)]),
                                  (s_st2[t % 2],
                                   st_hist.get((qq, "v", t - 2),
                                               st_hist[(qq, 14 + (t % 2))]))],
                           incs=[(s_evs, 1)])
                    EVH = s_evs.n
                    P.emit("vector",
                           (lambda o, p_, h_: (lambda e: e.tensor_tensor(
                               out=o, in0=p_, in1=h_, op=ALU.subtract)))(
                               evl, pb[t][:], evh),
                           waits=[(s_evs, EVH)], incs=[(s_evv, 1)])
                    evn = s_evv.n
                    dst = v_gin[qq][:, tkc * 2048 + dvh * 512: tkc * 2048 + dvh * 512 + 512]
                    dst2 = v_gin[qq][:, tkc * 2048 + 1024 + dvh * 512:
                                    tkc * 2048 + 1024 + dvh * 512 + 512]
                    dma("gpsimd", dst, evh, s_st2[t % 2], waits=[(s_evv, evn)])
                    st_hist[(qq, "v", t)] = dma("gpsimd", dst2, evl, s_st2[t % 2])
                last_v_evacs = s_evv.n
                mm_hist[("vlast", qq)] = mm_hist[(qq, "vb", 7)]

                ccw = [(s_st2[0], st_hist[(qq, "v", 6)]),
                       (s_st2[1], st_hist[(qq, "v", 7)])]
                P.emit("gpsimd",
                       (lambda i, o: (lambda e: e.collective_compute(
                           "AllGather", ALU.bypass,
                           replica_groups=[list(range(NCORE))],
                           ins=[i], outs=[o])))(k_gin[qq], k_gout[qq]),
                       waits=ccw, incs=[(s_cc, 1)])
                P.emit("gpsimd",
                       (lambda i, o: (lambda e: e.collective_compute(
                           "AllGather", ALU.bypass,
                           replica_groups=[list(range(NCORE))],
                           ins=[i], outs=[o])))(v_gin[qq], v_gout[qq]),
                       incs=[(s_cc, 1)])
            barrier()

        if "attn" in dbg_phases:
            KSLOT = [abf[:, 0:8192], abf[:, 8192:16384]]
            VSLOT = [abf[:, 16384:24576], abf[:, 24576:32768]]
            QTS = cbf[:, 0:8192]
            OSB = big2[:, 0:4096]
            RREC = big2[:, 4096:4608]

            def psl(s, tkc, lo):
                off = 8192 + s * 4096 + tkc * 1024 + (512 if lo else 0)
                return cbf[:, off: off + 512]

            kv_hist = {}
            smm = {}
            omm = {}
            expc = {}
            flsh = {}
            sums_fin = {}
            attn_st = {}
            for g in range(NQ):
                qw = [(s_mm, omm.get((g - 1, 31, 1), 0))]
                QL = dma("sync", QTS, q_scr[:, g * 8192: g * 8192 + 8192], s_q, waits=qw)
                NB = 32
                for b in range(NB):
                    qq, r = b // 8, b % 8
                    sl = b % 2
                    kvw = [(s_mm, smm.get((g, b - 2, 3),
                                          smm.get((g - 1, 30 + (b % 2), 3), 0)))]
                    kv_hist[(g, b, "k")] = dma(
                        "sync", KSLOT[sl],
                        k_gout[qq][r * 128:(r + 1) * 128, :], s_kk[sl],
                        waits=kvw + [(s_cc, 2 * (qq + 1))])
                    vw = [(s_mm, omm.get((g, b - 2, 1),
                                         omm.get((g - 1, 30 + (b % 2), 1), 0)))]
                    kv_hist[(g, b, "v")] = dma(
                        "sync", VSLOT[sl],
                        v_gout[qq][r * 128:(r + 1) * 128, :], s_kva[sl], waits=vw)
                    for tkc in range(4):
                        bank = pb[tkc % 2]
                        w = [(s_evv, expc.get((g, b, tkc - 2),
                                              expc.get((g, b - 1, tkc + 2),
                                                       expc.get((g - 1, NB - 1, tkc + 2), 0))))]
                        if tkc == 0:
                            w.append((s_kk[sl], kv_hist[(g, b, "k")]))
                            w.append((s_q, QL))
                        for dc in range(8):
                            kh = KSLOT[sl][:, dc * 1024 + tkc * 128: dc * 1024 + tkc * 128 + 128]
                            kl = KSLOT[sl][:, dc * 1024 + 512 + tkc * 128: dc * 1024 + 512 + tkc * 128 + 128]
                            qh = QTS[:, dc * 1024: dc * 1024 + 512]
                            ql = QTS[:, dc * 1024 + 512: dc * 1024 + 1024]
                            mm3(bank[:], kh, kl, qh, ql,
                                start=(dc == 0), stop=(dc == 7),
                                waits=(w if dc == 0 else ()),
                                inc=(s_mm if dc == 7 else None))
                        smm[(g, b, tkc)] = s_mm.n
                        we = [(s_mm, smm[(g, b, tkc)])]
                        if b >= 2:
                            we.append((s_mm, omm[(g, b - 2, 1)]))
                        else:
                            we.append((s_mm, omm.get((g - 1, 30 + b, 1), 0)))
                        pf = PFA if tkc % 2 == 0 else PFB
                        P.emit("scalar",
                               (lambda o, i: (lambda e: e.activation(
                                   out=o, in_=i, func=AF.Exp, scale=ISQRT_D)))(
                                   pf[:], pb[tkc % 2][:]),
                               waits=we + [(s_evv, expc.get((g, b, tkc - 2),
                                                            expc.get((g, b - 1, tkc + 2),
                                                                     expc.get((g - 1, NB - 1, tkc + 2), 0))))])
                        P.emit("scalar",
                               (lambda o, i: (lambda e: e.activation(
                                   out=o, in_=i, func=AF.Exp, scale=ISQRT_D)))(
                                   psl(b % 2, tkc, False), pb[tkc % 2][:]),
                               incs=[(s_evs, 1)])
                        PHEV = s_evs.n
                        P.emit("vector",
                               (lambda o, a, h_: (lambda e: e.tensor_tensor(
                                   out=o, in0=a, in1=h_, op=ALU.subtract)))(
                                   psl(b % 2, tkc, True), pf[:], psl(b % 2, tkc, False)),
                               waits=[(s_evs, PHEV)], incs=[(s_evv, 1)])
                        expc[(g, b, tkc)] = s_evv.n
                    if b > 0:
                        bp = b - 1
                        slp = bp % 2
                        w = [(s_evv, flsh[(g, bp, 0)])]
                        for i in range(4):
                            dvc = 4 + i
                            for tkc in range(4):
                                vh = VSLOT[slp][:, tkc * 2048 + dvc * 128: tkc * 2048 + dvc * 128 + 128]
                                vl = VSLOT[slp][:, tkc * 2048 + 1024 + dvc * 128: tkc * 2048 + 1024 + dvc * 128 + 128]
                                mm3(pb[2 + i][:], vh, vl,
                                    psl(bp % 2, tkc, False), psl(bp % 2, tkc, True),
                                    start=(tkc == 0), stop=(tkc == 3),
                                    waits=(w if (i == 0 and tkc == 0) else ()),
                                    inc=(s_mm if (i == 3 and tkc == 3) else None))
                        omm[(g, bp, 1)] = s_mm.n
                    for tkc in range(4):
                        mm(pb[7][0:1, :], onesb[:, 0:1], psl(b % 2, tkc, False),
                           start=(b == 0 and tkc == 0), stop=False,
                           waits=[(s_evv, expc[(g, b, tkc)])] +
                                 ([(s_evv, sums_fin.get(g - 1, 0))]
                                  if (b == 0 and tkc == 0) else []))
                        mm(pb[7][0:1, :], onesb[:, 0:1], psl(b % 2, tkc, True),
                           start=False, stop=(b == NB - 1 and tkc == 3),
                           inc=(s_mm if (b == NB - 1 and tkc == 3) else None))
                    if b == NB - 1:
                        smm[(g, "sums")] = s_mm.n
                    w = [(s_evv, flsh.get((g, b - 1, 1), flsh.get((g - 1, 31, 1), 0)))]
                    for i in range(4):
                        dvc = i
                        for tkc in range(4):
                            vh = VSLOT[sl][:, tkc * 2048 + dvc * 128: tkc * 2048 + dvc * 128 + 128]
                            vl = VSLOT[sl][:, tkc * 2048 + 1024 + dvc * 128: tkc * 2048 + 1024 + dvc * 128 + 128]
                            mm3(pb[2 + i][:], vh, vl,
                                psl(b % 2, tkc, False), psl(b % 2, tkc, True),
                                start=(tkc == 0), stop=(tkc == 3),
                                waits=((w + [(s_kva[sl], kv_hist[(g, b, "v")])])
                                       if (i == 0 and tkc == 0) else ()),
                                inc=(s_mm if (i == 3 and tkc == 3) else None))
                    omm[(g, b, 0)] = s_mm.n
                    for bb_, hh in ((b - 1, 1), (b, 0)):
                        if bb_ < 0:
                            continue
                        for i in range(4):
                            dvc = hh * 4 + i
                            dst = OSB[:, dvc * 512: dvc * 512 + 512]
                            if bb_ == 0:
                                fn = (lambda o, i_: (lambda e: e.tensor_copy(out=o, in_=i_)))(
                                    dst, pb[2 + i][:])
                            else:
                                fn = (lambda o, i_: (lambda e: e.tensor_tensor(
                                    out=o, in0=i_, in1=o, op=ALU.add)))(dst, pb[2 + i][:])
                            P.emit("vector", fn,
                                   waits=([(s_mm, omm[(g, bb_, hh)])] if i == 0 else []),
                                   incs=([(s_evv, 1)] if i == 3 else []))
                        flsh[(g, bb_, hh)] = s_evv.n
                bp = NB - 1
                slp = bp % 2
                w = [(s_evv, flsh[(g, bp, 0)])]
                for i in range(4):
                    dvc = 4 + i
                    for tkc in range(4):
                        vh = VSLOT[slp][:, tkc * 2048 + dvc * 128: tkc * 2048 + dvc * 128 + 128]
                        vl = VSLOT[slp][:, tkc * 2048 + 1024 + dvc * 128: tkc * 2048 + 1024 + dvc * 128 + 128]
                        mm3(pb[2 + i][:], vh, vl,
                            psl(bp % 2, tkc, False), psl(bp % 2, tkc, True),
                            start=(tkc == 0), stop=(tkc == 3),
                            waits=(w if (i == 0 and tkc == 0) else ()),
                            inc=(s_mm if (i == 3 and tkc == 3) else None))
                omm[(g, bp, 1)] = s_mm.n
                for i in range(4):
                    dvc = 4 + i
                    dst = OSB[:, dvc * 512: dvc * 512 + 512]
                    P.emit("vector",
                           (lambda o, i_: (lambda e: e.tensor_tensor(
                               out=o, in0=i_, in1=o, op=ALU.add)))(dst, pb[2 + i][:]),
                           waits=([(s_mm, omm[(g, bp, 1)])] if i == 0 else []),
                           incs=([(s_evv, 1)] if i == 3 else []))
                flsh[(g, bp, 1)] = s_evv.n

                P.emit("vector",
                       (lambda: (lambda e: e.tensor_copy(out=st1[:], in_=pb[7][0:1, :])))(),
                       waits=[(s_mm, smm[(g, "sums")])])
                P.emit("vector",
                       (lambda: (lambda e: e.reciprocal(st2[:], st1[:])))(),
                       incs=[(s_evv, 1)])
                sums_fin[g] = s_evv.n
                RECIP = s_evv.n
                mm(pb[6][:], ones_s[0:1, :], st2[0:1, :], start=True, stop=True,
                   waits=[(s_evv, RECIP)], inc=s_mm)
                RMM = s_mm.n
                P.emit("vector",
                       (lambda: (lambda e: e.tensor_copy(out=RREC[:], in_=pb[6][:])))(),
                       waits=[(s_mm, RMM)], incs=[(s_evv, 1)])
                RRDY = s_evv.n
                for dvc in range(8):
                    evf = EVA if dvc % 2 == 0 else EVB
                    P.emit("vector",
                           (lambda o, a, r_: (lambda e: e.tensor_tensor(
                               out=o, in0=a, in1=r_, op=ALU.mult)))(
                               evf[:], OSB[:, dvc * 512: dvc * 512 + 512], RREC[:]),
                           waits=[(s_evv, RRDY)],
                           incs=[(s_evv, 1)])
                    OSC = s_evv.n
                    pprev = (dvc - 1) % 2
                    gq_free = attn_st.get((g, dvc - 1), attn_st.get((g - 1, 7), 0))
                    P.emit("scalar",
                           (lambda o, i_: (lambda e: e.copy(out=o, in_=i_)))(
                               gq[:, 0:512], evf[:]),
                           waits=[(s_evv, OSC), (s_st2[pprev], gq_free)],
                           incs=[(s_evs, 1)])
                    OHEV = s_evs.n
                    P.emit("vector",
                           (lambda o, a, h_: (lambda e: e.tensor_tensor(
                               out=o, in0=a, in1=h_, op=ALU.subtract)))(
                               gq[:, 512:1024], evf[:], gq[:, 0:512]),
                           waits=[(s_evs, OHEV)], incs=[(s_evv, 1)])
                    OLEV = s_evv.n
                    attn_st[(g, dvc)] = dma(
                        "gpsimd",
                        o_scr[:, g * 8192 + dvc * 1024: g * 8192 + dvc * 1024 + 1024],
                        gq[:, 0:1024], s_st2[dvc % 2], waits=[(s_evv, OLEV)])
            barrier()

        if "ffn" in dbg_phases:
            ffn_mm = {}
            ffn_ev = {}
            y_hist = {}
            r1st = {}
            r2st = {}
            statclr = {"v": 0}
            x1_read_last = {"v": 0}
            x1s_last = {"s": 0}
            WRL = dma("sync", wr_s[:], wr_pr, s_c)
            for qq in range(NQ):
                OAT = cbf[:, 0:8192]
                OL = dma("sync", OAT, o_scr[:, qq * 8192: qq * 8192 + 8192], s_q,
                         waits=[(s_mm, ffn_mm.get((qq - 1, "base", 31), 0))])
                for oc in range(32):
                    slot = wbf[:, 8448 + (oc % 2) * 2048: 8448 + (oc % 2) * 2048 + 2048]
                    WL = dma("sync", slot, wo_pr[oc], s_w2[oc % 2],
                             waits=[(s_mm, ffn_mm.get((qq, "proj", oc - 2),
                                                      ffn_mm.get((qq - 1, "base", 31), 0)))])
                    xs = XSA if oc % 2 == 0 else XSB
                    XL = dma("gpsimd", xs[:],
                             x_img[:, oc * TC + qq * QT: oc * TC + qq * QT + 512],
                             s_xp[oc % 2],
                             waits=[(s_evv,
                                     ffn_ev.get((qq, "r1", oc - 2),
                                                ffn_ev.get((qq - 1, "out", 30 + (oc % 2)), 0)))])
                    bank = pb[oc % 2]
                    w = [(s_w2[oc % 2], WL), (s_q, OL),
                         (s_evv, ffn_ev.get((qq, "r1", oc - 2), 0))]
                    for dc in range(8):
                        mm3(bank[:],
                            slot[:, dc * 128: dc * 128 + 128],
                            slot[:, 1024 + dc * 128: 1024 + dc * 128 + 128],
                            OAT[:, dc * 1024: dc * 1024 + 512],
                            OAT[:, dc * 1024 + 512: dc * 1024 + 1024],
                            start=(dc == 0), stop=(dc == 7),
                            waits=(w if dc == 0 else ()),
                            inc=(s_mm if dc == 7 else None))
                    ffn_mm[(qq, "proj", oc)] = s_mm.n
                    r1e = EVA if oc % 2 == 0 else EVB
                    r1w = [(s_mm, ffn_mm[(qq, "proj", oc)]), (s_xp[oc % 2], XL),
                           (s_st2[oc % 2],
                            r1st.get((qq, oc - 2),
                                     r2st.get((qq - 1, 30 + (oc % 2)), 0))),
                           (s_mm, ffn_mm.get((qq, "sqm", oc - 2), 0))]
                    if oc == 0:
                        r1w.append((s_mm, x1_read_last["v"]))
                        r1w.append((s_evs, x1s_last["s"]))
                    P.emit("vector",
                           (lambda o, p_, b_, x_: (lambda e: e.scalar_tensor_tensor(
                               out=o, in0=p_, scalar=b_, in1=x_,
                               op0=ALU.add, op1=ALU.add)))(
                               r1e[:], bank[:], BO[:, oc:oc + 1], xs[:]),
                           waits=r1w,
                           incs=[(s_evv, 1)])
                    ffn_ev[(qq, "r1", oc)] = s_evv.n
                    r1st[(qq, oc)] = dma(
                        "gpsimd", r1_scr[:, oc * 512: oc * 512 + 512], r1e[:],
                        s_st2[oc % 2], waits=[(s_evv, ffn_ev[(qq, "r1", oc)])])
                    sq = SQA if oc % 2 == 0 else SQB
                    P.emit("scalar",
                           (lambda o, i: (lambda e: e.activation(
                               out=o, in_=i, func=AF.Square)))(sq[:], r1e[:]),
                           waits=[(s_evv, ffn_ev[(qq, "r1", oc)])] +
                                 ([(s_mm, ffn_mm.get((qq, "sqm", oc - 2), 0))]
                                  if oc >= 2 else []),
                           incs=[(s_evs, 1)])
                    ffn_ev[(qq, "sq", oc)] = s_evs.n
                    mm(pb[6][0:1, :], ones_s[:, 0:1], r1e[:],
                       start=(oc == 0), stop=(oc == 31),
                       waits=[(s_evv, ffn_ev[(qq, "r1", oc)])] +
                             ([(s_evv, statclr["v"])] if oc == 0 else []),
                       inc=(s_mm if oc == 31 else None))
                    mm(pb[7][0:1, :], ones_s[:, 0:1], sq[:],
                       start=(oc == 0), stop=(oc == 31),
                       waits=[(s_evs, ffn_ev[(qq, "sq", oc)])],
                       inc=(s_mm if oc == 31 else None))
                    if oc == 31:
                        ffn_mm[(qq, "stats")] = s_mm.n
                    ffn_mm[(qq, "sqm", oc)] = s_mm.n

                def ln_stats_epilogue(stats_mm_val):
                    P.emit("vector",
                           (lambda: (lambda e: e.tensor_copy(out=st1[:], in_=pb[6][0:1, :])))(),
                           waits=[(s_mm, stats_mm_val)])
                    P.emit("vector",
                           (lambda: (lambda e: e.tensor_copy(out=st2[:], in_=pb[7][0:1, :])))(),
                           incs=[(s_evv, 1)])
                    statclr["v"] = s_evv.n
                    P.emit("vector",
                           (lambda: (lambda e: e.tensor_scalar_mul(st3[:], st1[:], 1.0 / D)))())
                    P.emit("vector",
                           (lambda: (lambda e: e.tensor_scalar_mul(st6[:], st2[:], 1.0 / D)))())
                    P.emit("vector",
                           (lambda: (lambda e: e.tensor_tensor(
                               out=st4[:], in0=st3[:], in1=st3[:], op=ALU.mult)))())
                    P.emit("vector",
                           (lambda: (lambda e: e.tensor_tensor(
                               out=st2[:], in0=st6[:], in1=st4[:], op=ALU.subtract)))())
                    P.emit("vector",
                           (lambda: (lambda e: e.tensor_scalar_add(st6[:], st2[:], EPS)))(),
                           incs=[(s_evv, 1)])
                    VEV = s_evv.n
                    P.emit("scalar",
                           (lambda: (lambda e: e.activation(out=st1[:], in_=st6[:], func=AF.Ln)))(),
                           waits=[(s_evv, VEV)])
                    P.emit("scalar",
                           (lambda: (lambda e: e.activation(
                               out=st4[:], in_=st1[:], func=AF.Exp, scale=-0.5)))(),
                           incs=[(s_evs, 1)])
                    SEV = s_evs.n
                    P.emit("vector",
                           (lambda: (lambda e: e.tensor_tensor(
                               out=st2[:], in0=st3[:], in1=st4[:], op=ALU.mult)))(),
                           waits=[(s_evs, SEV)])
                    P.emit("vector",
                           (lambda: (lambda e: e.tensor_scalar_mul(st3[:], st2[:], -1.0)))(),
                           incs=[(s_evv, 1)])
                    NEV = s_evv.n
                    mm(pb[2][:], ones_s[0:1, :], st4[0:1, :], start=True, stop=True,
                       waits=[(s_evv, NEV)], inc=s_mm)
                    mm(pb[3][:], ones_s[0:1, :], st3[0:1, :], start=True, stop=True,
                       inc=s_mm)
                    REPL = s_mm.n
                    P.emit("vector",
                           (lambda: (lambda e: e.tensor_copy(out=R128[:], in_=pb[2][:])))(),
                           waits=[(s_mm, REPL)])
                    P.emit("vector",
                           (lambda: (lambda e: e.tensor_copy(out=M128[:], in_=pb[3][:])))(),
                           incs=[(s_evv, 1)])
                    return s_evv.n

                RRDY1 = ln_stats_epilogue(ffn_mm[(qq, "stats")])
                for dc in range(32):
                    r1l = R2LA if dc % 2 == 0 else R2LB
                    RL = dma("sync", r1l[:], r1_scr[:, dc * 512: dc * 512 + 512],
                             s_r2l[dc % 2],
                             waits=[(s_st2[dc % 2], r1st[(qq, dc)]),
                                    (s_evs, x1s_last["s"] if dc < 2 else 0),
                                    (s_evv,
                                     ffn_ev.get((qq, "x1v", dc - 2),
                                                ffn_ev.get((qq, "r1", 30 + (dc % 2)), 0)))])
                    P.emit("vector",
                           (lambda o, i: (lambda e: e.tensor_tensor(
                               out=o, in0=i, in1=R128[:], op=ALU.mult)))(
                               OUTT[:], r1l[:]),
                           waits=[(s_r2l[dc % 2], RL)] +
                                 ([(s_evv, RRDY1)] if dc == 0 else []))
                    P.emit("vector",
                           (lambda o: (lambda e: e.tensor_tensor(
                               out=o, in0=OUTT[:], in1=M128[:], op=ALU.add)))(OUT2[:]),
                           incs=[(s_evv, 1)])
                    X1C = s_evv.n
                    xh = abf[:, dc * 512: dc * 512 + 512]
                    xl = abf[:, 16384 + dc * 512: 16384 + dc * 512 + 512]
                    P.emit("scalar",
                           (lambda o, i_: (lambda e: e.copy(out=o, in_=i_)))(xh, OUT2[:]),
                           waits=[(s_evv, X1C)] +
                                 ([(s_mm, x1_read_last["v"])] if dc == 0 else []),
                           incs=[(s_evs, 1)])
                    XHEV = s_evs.n
                    P.emit("vector",
                           (lambda o, a, h_: (lambda e: e.tensor_tensor(
                               out=o, in0=a, in1=h_, op=ALU.subtract)))(
                               xl, OUT2[:], xh),
                           waits=[(s_evs, XHEV)], incs=[(s_evv, 1)])
                    ffn_ev[(qq, "x1v", dc)] = s_evv.n
                x1s_last["s"] = s_evs.n
                X1_LAST = s_evv.n
                if dbg_outs:
                    dma("gpsimd", dbg["d_x1"][qq], abf[:], s_y,
                        waits=[(s_evv, X1_LAST)])

                for tc in range(4):
                    for kc in range(32):
                        xh = abf[:, kc * 512 + tc * 128: kc * 512 + tc * 128 + 128]
                        xl = abf[:, 16384 + kc * 512 + tc * 128: 16384 + kc * 512 + tc * 128 + 128]
                        mm3(pb[4][:, 0:8], xh, xl,
                            wr_s[:, kc * 8: kc * 8 + 8],
                            wr_s[:, 256 + kc * 8: 256 + kc * 8 + 8],
                            start=(kc == 0), stop=(kc == 31),
                            waits=([(s_evv, X1_LAST),
                                    (s_evv, ffn_ev.get((qq, "wtr", tc - 1), 0)),
                                    (s_c, WRL)]
                                   if kc == 0 else ()),
                            inc=(s_mm if kc == 31 else None))
                    LMM = s_mm.n
                    P.emit("scalar",
                           (lambda: (lambda e: e.copy(out=lg[:], in_=pb[4][:, 0:8])))(),
                           waits=[(s_mm, LMM)], incs=[(s_evs, 1)])
                    LEV = s_evs.n

                    def vop(fn, waits=(), incs=()):
                        P.emit("vector", fn, waits=waits, incs=incs)
                        P.emit("vector", (lambda: (lambda e: e.engine_nop()))())
                        P.emit("vector", (lambda: (lambda e: e.engine_nop()))())
                    vop((lambda: (lambda e: e.max(out=srt[:], in_=lg[:])))(),
                        waits=[(s_evs, LEV)])
                    vop((lambda: (lambda e: e.tensor_tensor(
                        out=negm1[:], in0=srt[:, 0:1], in1=srt[:, 1:2],
                        op=ALU.subtract)))())
                    P.emit("vector", (lambda: (lambda e: e.engine_nop()))(),
                           incs=[(s_evv, 1)])
                    TEV = s_evv.n
                    P.emit("scalar",
                           (lambda: (lambda e: e.activation(
                               out=l2t[:], in_=negm1[:], func=AF.Sigmoid)))(),
                           waits=[(s_evv, TEV)], incs=[(s_evs, 1)])
                    AEV = s_evs.n
                    vop((lambda: (lambda e: e.tensor_scalar(
                        e2t[:], l2t[:], -2.0, 2.0, op0=ALU.mult, op1=ALU.add)))(),
                        waits=[(s_evs, AEV)])
                    vop((lambda: (lambda e: e.tensor_scalar(
                        r2t[:], l2t[:], 4.0, -2.0, op0=ALU.mult, op1=ALU.add)))())
                    vop((lambda: (lambda e: e.tensor_scalar(
                        msk[:], lg[:], srt[:, 0:1], None, op0=ALU.is_equal)))())
                    vop((lambda: (lambda e: e.tensor_scalar(
                        ee[:], lg[:], srt[:, 1:2], None, op0=ALU.is_ge)))())
                    vop((lambda: (lambda e: e.tensor_scalar(
                        lg[:], ee[:], e2t[:], None, op0=ALU.mult)))())
                    vop((lambda tc_: (lambda e: e.scalar_tensor_tensor(
                        out=w4[:, tc_ * 8: tc_ * 8 + 8], in0=msk[:], scalar=r2t[:],
                        in1=lg[:], op0=ALU.mult, op1=ALU.add)))(tc),
                        incs=[(s_evv, 1)])
                    WEV = s_evv.n
                    P.emit("tensor",
                           (lambda tc_: (lambda e: e.transpose(
                               pb[5][0:8, tc_ * 128: tc_ * 128 + 128],
                               w4[:, tc_ * 8: tc_ * 8 + 8], ident_s[:])))(tc),
                           waits=[(s_evv, WEV)], incs=[(s_mm, 1)])
                    ffn_mm[(qq, "wt", tc)] = s_mm.n
                    ffn_ev[(qq, "wtr", tc)] = WEV
                P.emit("vector",
                       (lambda: (lambda e: e.tensor_copy(out=wem[0:8, 0:512], in_=pb[5][0:8, :])))(),
                       waits=[(s_mm, ffn_mm[(qq, "wt", 3)])], incs=[(s_evv, 1)])
                WEMEV = s_evv.n
                if dbg_outs:
                    dma("gpsimd", dbg["d_w4"][qq], w4[:], s_y, waits=[(s_evv, WEMEV)])
                mm(pb[2][:], rt_s[0:8, :], wem[0:8, 0:512], start=True, stop=True,
                   waits=[(s_evv, WEMEV)], inc=s_mm)
                SFMM = s_mm.n
                P.emit("vector",
                       (lambda: (lambda e: e.tensor_copy(out=SFM[:], in_=pb[2][:])))(),
                       waits=[(s_mm, SFMM)], incs=[(s_evv, 1)])
                SFEV = s_evv.n
                AL = dma("sync", wbf[:, 0:8192], a_pr, s_w2[0],
                         waits=[(s_mm, ffn_mm[(qq, "proj", 31)])])
                for kc in range(32):
                    mm3(pb[3][:],
                        wbf[:, kc * 128: kc * 128 + 128],
                        wbf[:, 4096 + kc * 128: 4096 + kc * 128 + 128],
                        abf[:, kc * 512: kc * 512 + 512],
                        abf[:, 16384 + kc * 512: 16384 + kc * 512 + 512],
                        start=(kc == 0), stop=(kc == 31),
                        waits=([(s_w2[0], AL), (s_evv, X1_LAST)] if kc == 0 else ()),
                        inc=(s_mm if kc == 31 else None))
                ffn_mm[(qq, "h")] = s_mm.n
                P.emit("scalar",
                       (lambda: (lambda e: e.copy(out=HT[:], in_=pb[3][:])))(),
                       waits=[(s_mm, ffn_mm[(qq, "h")])], incs=[(s_evs, 1)])
                HEV = s_evs.n
                P.emit("vector",
                       (lambda: (lambda e: e.tensor_tensor(
                           out=OUTT[:], in0=HT[:], in1=SFM[:], op=ALU.mult)))(),
                       waits=[(s_evs, HEV), (s_evv, SFEV),
                              (s_mm, ffn_mm.get((qq - 1, "base", 31), 0))],
                       incs=[(s_evv, 1)])
                GFEV = s_evv.n
                P.emit("scalar",
                       (lambda: (lambda e: e.copy(out=gq[:, 0:512], in_=OUTT[:])))(),
                       waits=[(s_evv, GFEV)], incs=[(s_evs, 1)])
                GHEV = s_evs.n
                P.emit("vector",
                       (lambda: (lambda e: e.tensor_tensor(
                           out=gq[:, 512:1024], in0=OUTT[:], in1=gq[:, 0:512],
                           op=ALU.subtract)))(),
                       waits=[(s_evs, GHEV)], incs=[(s_evv, 1)])
                GEV = s_evv.n
                for oc in range(32):
                    slot = wbf[:, (oc % 2) * 8448:(oc % 2) * 8448 + 8448]
                    WL = dma("sync", slot, wbb_pr[oc], s_w2[oc % 2],
                             waits=[(s_mm, ffn_mm.get((qq, "base", oc - 2),
                                                      ffn_mm[(qq, "h")]))])
                    bank = pb[oc % 2]
                    w = [(s_w2[oc % 2], WL), (s_evv, GEV),
                         (s_evv, ffn_ev.get((qq, "r2", oc - 2), 0))]
                    for dc in range(32):
                        mm3(bank[:],
                            slot[:, dc * 128:(dc + 1) * 128],
                            slot[:, 4224 + dc * 128: 4224 + dc * 128 + 128],
                            abf[:, dc * 512: dc * 512 + 512],
                            abf[:, 16384 + dc * 512: 16384 + dc * 512 + 512],
                            start=(dc == 0), stop=False,
                            waits=(w if dc == 0 else ()))
                    mm(bank[:], slot[:, 4096:4224], gq[:, 0:512],
                       start=False, stop=False)
                    mm(bank[:], slot[:, 4096:4224], gq[:, 512:1024],
                       start=False, stop=False)
                    mm(bank[:], slot[:, 8320:8448], gq[:, 0:512],
                       start=False, stop=True, inc=s_mm)
                    ffn_mm[(qq, "base", oc)] = s_mm.n
                    tt = XSA if oc % 2 == 0 else XSB
                    r2e = EVA if oc % 2 == 0 else EVB
                    P.emit("vector",
                           (lambda o, p_, b_, x_: (lambda e: e.scalar_tensor_tensor(
                               out=o, in0=p_, scalar=b_, in1=x_,
                               op0=ALU.add, op1=ALU.add)))(
                               tt[:], bank[:], BB[:, oc:oc + 1],
                               abf[:, oc * 512: oc * 512 + 512]),
                           waits=[(s_mm, ffn_mm[(qq, "base", oc)])],
                           incs=[(s_evv, 1)])
                    P.emit("vector",
                           (lambda o, a, b_: (lambda e: e.tensor_tensor(
                               out=o, in0=a, in1=b_, op=ALU.add)))(
                               r2e[:], tt[:],
                               abf[:, 16384 + oc * 512: 16384 + oc * 512 + 512]),
                           waits=[(s_st2[oc % 2],
                                   r2st.get((qq, oc - 2),
                                            r1st[(qq, 30 + (oc % 2))]))],
                           incs=[(s_evv, 1)])
                    ffn_ev[(qq, "r2", oc)] = s_evv.n
                    r2st[(qq, oc)] = dma(
                        "gpsimd", r2_scr[:, oc * 512: oc * 512 + 512], r2e[:],
                        s_st2[oc % 2], waits=[(s_evv, ffn_ev[(qq, "r2", oc)])])
                    sq = SQA if oc % 2 == 0 else SQB
                    P.emit("scalar",
                           (lambda o, i: (lambda e: e.activation(
                               out=o, in_=i, func=AF.Square)))(sq[:], r2e[:]),
                           waits=[(s_evv, ffn_ev[(qq, "r2", oc)])] +
                                 ([(s_mm, ffn_mm.get((qq, "sqm2", oc - 2), 0))]
                                  if oc >= 2 else []),
                           incs=[(s_evs, 1)])
                    SQEV = s_evs.n
                    mm(pb[6][0:1, :], ones_s[:, 0:1], r2e[:],
                       start=(oc == 0), stop=(oc == 31),
                       waits=[(s_evv, ffn_ev[(qq, "r2", oc)])] +
                             ([(s_evv, statclr["v"])] if oc == 0 else []),
                       inc=(s_mm if oc == 31 else None))
                    mm(pb[7][0:1, :], ones_s[:, 0:1], sq[:],
                       start=(oc == 0), stop=(oc == 31),
                       waits=[(s_evs, SQEV)],
                       inc=(s_mm if oc == 31 else None))
                    if oc == 31:
                        ffn_mm[(qq, "stats2")] = s_mm.n
                    ffn_mm[(qq, "sqm2", oc)] = s_mm.n

                x1_read_last["v"] = ffn_mm[(qq, "base", 31)]
                if dbg_outs:
                    dma("gpsimd", dbg["d_r2"][qq], r2_scr, s_y,
                        waits=[(s_st2[0], r2st[(qq, 30)]), (s_st2[1], r2st[(qq, 31)])])
                RRDY2 = ln_stats_epilogue(ffn_mm[(qq, "stats2")])
                for dc in range(32):
                    r2l = R2LA if dc % 2 == 0 else R2LB
                    RL = dma("sync", r2l[:], r2_scr[:, dc * 512: dc * 512 + 512],
                             s_r2l[dc % 2],
                             waits=[(s_st2[dc % 2], r2st[(qq, dc)]),
                                    (s_evv,
                                     ffn_ev.get((qq, "out", dc - 2),
                                                ffn_ev.get((qq, "r2", 30 + (dc % 2)), 0)))])
                    P.emit("vector",
                           (lambda o, i: (lambda e: e.tensor_tensor(
                               out=o, in0=i, in1=R128[:], op=ALU.mult)))(
                               OUTT[:], r2l[:]),
                           waits=[(s_r2l[dc % 2], RL)] +
                                 ([(s_evv, RRDY2)] if dc == 0 else []) +
                                 [(s_y, y_hist.get((qq, dc - 1),
                                                   y_hist.get((qq - 1, 31), 0)))])
                    P.emit("vector",
                           (lambda o: (lambda e: e.tensor_tensor(
                               out=o, in0=OUTT[:], in1=M128[:], op=ALU.add)))(OUT2[:]),
                           incs=[(s_evv, 1)])
                    ffn_ev[(qq, "out", dc)] = s_evv.n
                    y_hist[(qq, dc)] = dma(
                        "gpsimd",
                        y_img[:, dc * TC + qq * QT: dc * TC + qq * QT + 512],
                        OUT2[:], s_y, waits=[(s_evv, ffn_ev[(qq, "out", dc)])])
            barrier()

        if dbg_outs:
            for q in range(NQ):
                dma("gpsimd", dbg["d_k"][q], k_gin[q], s_y)
                dma("gpsimd", dbg["d_v"][q], v_gin[q], s_y)
            dma("gpsimd", dbg["d_q"], q_scr, s_y)
            dma("gpsimd", dbg["d_o"], o_scr, s_y)

        for s in (s_st2[0], s_st2[1], s_y):
            if s.n > 0:
                P.emit("gpsimd", (lambda sh, vv: (lambda e: e.wait_ge(sh, vv)))(s.h, s.n))

        with nc.Block() as block:
            @block.sync
            def _(eng):
                P.replay("sync", eng)

            @block.tensor
            def _(eng):
                P.replay("tensor", eng)

            @block.vector
            def _(eng):
                P.replay("vector", eng)

            @block.scalar
            def _(eng):
                P.replay("scalar", eng)

            @block.gpsimd
            def _(eng):
                P.replay("gpsimd", eng)

    return nc


def stage_inputs(inputs):
    import ml_dtypes
    f32 = np.float32
    x = np.asarray(inputs["L0_attn_x"], dtype=f32)[0]
    Wqkv = np.asarray(inputs["L0_attn_Wqkv"], dtype=f32)
    bqkv = np.asarray(inputs["L0_attn_bqkv"], dtype=f32)
    Wo = np.asarray(inputs["L0_attn_Wo"], dtype=f32)
    bo = np.asarray(inputs["L0_attn_bo"], dtype=f32)
    Wb = np.asarray(inputs["L0_ffn_W_base"], dtype=f32)
    bb = np.asarray(inputs["L0_ffn_b_base"], dtype=f32)
    Wr = np.asarray(inputs["L0_ffn_W_router"], dtype=f32)
    lA = np.asarray(inputs["L0_ffn_lora_A"], dtype=f32)
    lB = np.asarray(inputs["L0_ffn_lora_B"], dtype=f32)

    shared = {}

    def qk_img(W):
        return np.ascontiguousarray(
            W.reshape(32, 128, 8, 128).transpose(2, 1, 0, 3).reshape(8, 128, 4096))
    shared["wq_img"] = qk_img(Wqkv[:, 0:1024])
    shared["wk_img"] = qk_img(Wqkv[:, 1024:2048])
    shared["wv_img"] = np.ascontiguousarray(Wqkv[:, 2048:3072].reshape(32, 128, 1024))
    shared["wo_img"] = np.ascontiguousarray(
        Wo.reshape(8, 128, 32, 128).transpose(2, 1, 0, 3).reshape(32, 128, 1024))
    wb_img = Wb.reshape(32, 128, 32, 128).transpose(0, 3, 2, 1).reshape(32, 128, 4096)
    b_img = lB.reshape(8, 32, 128, 16).transpose(0, 3, 1, 2).reshape(128, 4096)
    b_chunks = b_img.reshape(128, 32, 128).transpose(1, 0, 2)
    shared["wbb_img"] = np.ascontiguousarray(
        np.concatenate([wb_img, b_chunks], axis=2))
    shared["a_img"] = np.ascontiguousarray(
        lA.reshape(8, 16, 32, 128).transpose(3, 2, 0, 1).reshape(128, 4096))
    shared["wr_img"] = np.ascontiguousarray(
        Wr.T.reshape(32, 128, 8).transpose(1, 0, 2).reshape(128, 256))
    shared["bq_img"] = np.ascontiguousarray(bqkv[0:1024].reshape(8, 128).T)
    shared["bk_img"] = np.ascontiguousarray(bqkv[1024:2048].reshape(8, 128).T)
    shared["bv_row"] = np.ascontiguousarray(bqkv[2048:3072].reshape(1, 1024))
    shared["bo_img"] = np.ascontiguousarray(bo.reshape(32, 128).T)
    shared["bb_img"] = np.ascontiguousarray(bb.reshape(32, 128).T)
    shared["ones_in"] = np.ones((128, 128), dtype=f32)
    shared["onesb_in"] = np.ones((128, 128), dtype=ml_dtypes.bfloat16)
    shared["ident_in"] = np.eye(128, dtype=f32)
    rt = np.zeros((8, 128), dtype=f32)
    for e in range(8):
        rt[e, e * 16:(e + 1) * 16] = 1.0
    shared["rt_in"] = rt

    in_maps = []
    for c in range(NCORE):
        xs = x[c * TC:(c + 1) * TC]
        xi = np.ascontiguousarray(
            xs.reshape(TC, 32, 128).transpose(2, 1, 0).reshape(128, 32 * TC))
        m = dict(shared)
        m["x_img"] = xi
        in_maps.append(m)
    return in_maps


def assemble_output(results):
    out = np.empty((1, NCORE * TC, D), dtype=np.float32)
    for c in range(NCORE):
        yi = results[c]["y_img"].reshape(128, 32, TC)
        out[0, c * TC:(c + 1) * TC, :] = yi.transpose(2, 1, 0).reshape(TC, D)
    return out


def kernel(**inputs):
    in_maps = stage_inputs(inputs)
    nc = build_program()
    res = run_bass_kernel_spmd(nc, in_maps, core_ids=list(range(NCORE)))
    return assemble_output(res.results)


# version 03
# speedup vs baseline: 1.1713x; 1.0123x over previous
import numpy as np
from contextlib import ExitStack

import concourse.bass as bass
import concourse.mybir as mybir
from concourse.bass_utils import run_bass_kernel_spmd

F32 = mybir.dt.float32
BF16 = mybir.dt.bfloat16
AF = mybir.ActivationFunctionType
ALU = mybir.AluOpType

NCORE = 8
D = 4096
DV = 1024
TC = 2048
QT = 512
NQ = TC // QT
EPS = 1e-5
ISQRT_D = 1.0 / 32.0


class Sem:
    __slots__ = ("h", "n")

    def __init__(self, h):
        self.h = h
        self.n = 0

    def bump(self, k):
        self.n += k
        return self.n


class Prog:
    def __init__(self):
        self.ops = []

    def emit(self, eng, fn, waits=(), incs=()):
        rec_incs = []
        for s, k in incs:
            s.bump(k)
            rec_incs.append((s, k))
        self.ops.append((eng, fn, tuple(waits), tuple(rec_incs)))

    def replay(self, name, engine):
        for eng, fn, waits, incs in self.ops:
            if eng != name:
                continue
            for s, v in waits:
                if v > 0:
                    engine.wait_ge(s.h, v)
            ins = fn(engine)
            for s, k in incs:
                ins.then_inc(s.h, k)


def build_program(dbg_phases=("qkv", "attn", "ffn"), dbg_outs=False):
    nc = bass.Bass("TRN2", target_bir_lowering=False)

    x_img = nc.dram_tensor("x_img", [128, 32 * TC], F32, kind="ExternalInput").ap()
    wq_img = nc.dram_tensor("wq_img", [8, 128, 4096], F32, kind="ExternalInput").ap()
    wk_img = nc.dram_tensor("wk_img", [8, 128, 4096], F32, kind="ExternalInput").ap()
    wv_img = nc.dram_tensor("wv_img", [32, 128, 1024], F32, kind="ExternalInput").ap()
    wo_img = nc.dram_tensor("wo_img", [32, 128, 1024], F32, kind="ExternalInput").ap()
    wbb_img = nc.dram_tensor("wbb_img", [32, 128, 4224], F32, kind="ExternalInput").ap()
    a_img = nc.dram_tensor("a_img", [128, 4096], F32, kind="ExternalInput").ap()
    wr_img = nc.dram_tensor("wr_img", [128, 256], F32, kind="ExternalInput").ap()
    bq_img = nc.dram_tensor("bq_img", [128, 8], F32, kind="ExternalInput").ap()
    bk_img = nc.dram_tensor("bk_img", [128, 8], F32, kind="ExternalInput").ap()
    bv_row = nc.dram_tensor("bv_row", [1, 1024], F32, kind="ExternalInput").ap()
    bo_img = nc.dram_tensor("bo_img", [128, 32], F32, kind="ExternalInput").ap()
    bb_img = nc.dram_tensor("bb_img", [128, 32], F32, kind="ExternalInput").ap()
    ones_in = nc.dram_tensor("ones_in", [128, 128], F32, kind="ExternalInput").ap()
    onesb_in = nc.dram_tensor("onesb_in", [128, 128], BF16, kind="ExternalInput").ap()
    ident_in = nc.dram_tensor("ident_in", [128, 128], F32, kind="ExternalInput").ap()
    rt_in = nc.dram_tensor("rt_in", [8, 128], F32, kind="ExternalInput").ap()

    y_img = nc.dram_tensor("y_img", [128, 32 * TC], F32, kind="ExternalOutput").ap()

    wq_pr = nc.dram_tensor("wq_pr", [8, 128, 8192], BF16).ap()
    wk_pr = nc.dram_tensor("wk_pr", [8, 128, 8192], BF16).ap()
    wv_pr = nc.dram_tensor("wv_pr", [32, 128, 2048], BF16).ap()
    wo_pr = nc.dram_tensor("wo_pr", [32, 128, 2048], BF16).ap()
    wbb_pr = nc.dram_tensor("wbb_pr", [32, 128, 8448], BF16).ap()
    a_pr = nc.dram_tensor("a_pr", [128, 8192], BF16).ap()
    wr_pr = nc.dram_tensor("wr_pr", [128, 512], BF16).ap()
    k_gin = [nc.dram_tensor(f"k_gin{q}", [128, 8192], BF16).ap() for q in range(NQ)]
    v_gin = [nc.dram_tensor(f"v_gin{q}", [128, 8192], BF16).ap() for q in range(NQ)]
    k_gout = [nc.dram_tensor(f"k_gout{q}", [8 * 128, 8192], BF16).ap() for q in range(NQ)]
    v_gout = [nc.dram_tensor(f"v_gout{q}", [8 * 128, 8192], BF16).ap() for q in range(NQ)]
    q_scr = nc.dram_tensor("q_scr", [128, NQ * 8192], BF16).ap()
    o_scr = nc.dram_tensor("o_scr", [128, NQ * 8192], BF16).ap()
    r1_scr = nc.dram_tensor("r1_scr", [128, 16384], F32).ap()
    r2_scr = nc.dram_tensor("r2_scr", [128, 16384], F32).ap()

    dbg = {}
    if dbg_outs:
        dbg["d_q"] = nc.dram_tensor("d_q", [128, NQ * 8192], BF16, kind="ExternalOutput").ap()
        dbg["d_k"] = nc.dram_tensor("d_k", [NQ, 128, 8192], BF16, kind="ExternalOutput").ap()
        dbg["d_v"] = nc.dram_tensor("d_v", [NQ, 128, 8192], BF16, kind="ExternalOutput").ap()
        dbg["d_o"] = nc.dram_tensor("d_o", [128, NQ * 8192], BF16, kind="ExternalOutput").ap()
        dbg["d_x1"] = nc.dram_tensor("d_x1", [NQ, 128, 32768], BF16, kind="ExternalOutput").ap()
        dbg["d_r2"] = nc.dram_tensor("d_r2", [NQ, 128, 16384], F32, kind="ExternalOutput").ap()
        dbg["d_w4"] = nc.dram_tensor("d_w4", [NQ, 128, 32], F32, kind="ExternalOutput").ap()

    es = ExitStack()
    with es:
        abf = es.enter_context(nc.sbuf_tensor("abf", [128, 32768], BF16))
        big2 = es.enter_context(nc.sbuf_tensor("big2", [128, 8448], F32))
        cbf = es.enter_context(nc.sbuf_tensor("cbf", [128, 16384], BF16))
        wbf = es.enter_context(nc.sbuf_tensor("wbf", [128, 16896], BF16))
        misc = es.enter_context(nc.sbuf_tensor("misc", [128, 4096], F32))
        wr_s = es.enter_context(nc.sbuf_tensor("wr_s", [128, 512], BF16))
        gq = es.enter_context(nc.sbuf_tensor("gq", [128, 1024], BF16))
        consts = es.enter_context(nc.sbuf_tensor("consts", [128, 256], F32))
        onesb = es.enter_context(nc.sbuf_tensor("onesb", [128, 128], BF16))
        rt_s = es.enter_context(nc.sbuf_tensor("rt_s", [8, 128], F32))
        bias_s = es.enter_context(nc.sbuf_tensor("bias_s", [128, 80], F32))
        bv_s = es.enter_context(nc.sbuf_tensor("bv_s", [1, 1024], F32))
        wem = es.enter_context(nc.sbuf_tensor("wem", [8, 512], F32))
        lg = es.enter_context(nc.sbuf_tensor("lg", [128, 8], F32))
        srt = es.enter_context(nc.sbuf_tensor("srt", [128, 8], F32))
        ee = es.enter_context(nc.sbuf_tensor("ee", [128, 8], F32))
        msk = es.enter_context(nc.sbuf_tensor("msk", [128, 8], F32))
        negm1 = es.enter_context(nc.sbuf_tensor("negm1", [128, 1], F32))
        l2t = es.enter_context(nc.sbuf_tensor("l2t", [128, 1], F32))
        e2t = es.enter_context(nc.sbuf_tensor("e2t", [128, 1], F32))
        r2t = es.enter_context(nc.sbuf_tensor("r2t", [128, 1], F32))
        w4 = es.enter_context(nc.sbuf_tensor("w4", [128, 32], F32))
        st1 = es.enter_context(nc.sbuf_tensor("st1", [1, 512], F32))
        st2 = es.enter_context(nc.sbuf_tensor("st2", [1, 512], F32))
        st3 = es.enter_context(nc.sbuf_tensor("st3", [1, 512], F32))
        st4 = es.enter_context(nc.sbuf_tensor("st4", [1, 512], F32))
        st6 = es.enter_context(nc.sbuf_tensor("st6", [1, 512], F32))

        pb = [es.enter_context(nc.psum_tensor(f"pb{i}", [128, 512], F32)) for i in range(8)]

        sems = {}
        for name in ("s_c", "s_xp0", "s_xp1", "s_w0", "s_w1",
                     "s_wv0", "s_wv1", "s_wv2", "s_wv3",
                     "s_pw0", "s_pw1", "s_pst0", "s_pst1",
                     "s_q", "s_kk0", "s_kk1", "s_kva0", "s_kva1",
                     "s_r20", "s_r21",
                     "s_st0", "s_st1", "s_y",
                     "s_mm", "s_evs", "s_evv", "s_cc"):
            sems[name] = Sem(es.enter_context(nc.semaphore(name)))
        s_c, s_y = sems["s_c"], sems["s_y"]
        s_xp = [sems["s_xp0"], sems["s_xp1"]]
        s_w2 = [sems["s_w0"], sems["s_w1"]]
        s_wv4 = [sems["s_wv0"], sems["s_wv1"], sems["s_wv2"], sems["s_wv3"]]
        s_pw = [sems["s_pw0"], sems["s_pw1"]]
        s_pst = [sems["s_pst0"], sems["s_pst1"]]
        s_q = sems["s_q"]
        s_kk = [sems["s_kk0"], sems["s_kk1"]]
        s_kva = [sems["s_kva0"], sems["s_kva1"]]
        s_r2l = [sems["s_r20"], sems["s_r21"]]
        s_st2 = [sems["s_st0"], sems["s_st1"]]
        s_mm, s_evs, s_evv = sems["s_mm"], sems["s_evs"], sems["s_evv"]
        s_cc = sems["s_cc"]

        P = Prog()

        def dma(q, out_ap, in_ap, sem, waits=()):
            P.emit(q, (lambda o, i: (lambda e: e.dma_start(out=o, in_=i)))(out_ap, in_ap),
                   waits=waits, incs=[(sem, 16)])
            return sem.n

        def mm(out_ap, lhsT, rhs, start, stop, waits=(), inc=None):
            P.emit("tensor",
                   (lambda o, l, r, st, sp: (lambda e: e.matmul(o, l, r, start=st, stop=sp)))(
                       out_ap, lhsT, rhs, start, stop),
                   waits=waits, incs=([(inc, 1)] if inc is not None else []))
            return inc.n if inc is not None else None

        def mm3(out_ap, lh, ll, rh, rl, start, stop, waits=(), inc=None):
            mm(out_ap, lh, rh, start=start, stop=False, waits=waits)
            mm(out_ap, lh, rl, start=False, stop=False)
            return mm(out_ap, ll, rh, start=False, stop=stop, inc=(inc if stop else None))

        EVA = misc[:, 0:512]
        EVB = misc[:, 512:1024]
        XSA = misc[:, 1024:1536]
        XSB = misc[:, 1536:2048]
        R128 = misc[:, 2048:2560]
        M128 = misc[:, 2560:3072]
        SQA = misc[:, 3072:3584]
        SQB = misc[:, 3584:4096]
        HT = misc[:, 2048:2560]
        SFM = misc[:, 2560:3072]
        OUTT = misc[:, 3072:3584]
        OUT2 = misc[:, 3584:4096]
        R2LA = misc[:, 1024:1536]
        R2LB = misc[:, 1536:2048]
        PFA = misc[:, 1024:1536]
        PFB = misc[:, 1536:2048]

        ones_s = consts[:, 0:128]
        ident_s = consts[:, 128:256]
        BQ = bias_s[:, 0:8]
        BK = bias_s[:, 8:16]
        BO = bias_s[:, 16:48]
        BB = bias_s[:, 48:80]

        dma("sync", consts[:, 0:128], ones_in, s_c)
        dma("sync", consts[:, 128:256], ident_in, s_c)
        dma("sync", onesb[:], onesb_in, s_c)
        dma("sync", rt_s[:], rt_in, s_c)
        dma("sync", bias_s[:, 0:8], bq_img, s_c)
        dma("sync", bias_s[:, 8:16], bk_img, s_c)
        dma("sync", bias_s[:, 16:48], bo_img, s_c)
        dma("sync", bias_s[:, 48:80], bb_img, s_c)
        dma("sync", bv_s[:], bv_row, s_c)
        CONST_W = s_c.n
        for engn in ("tensor", "vector", "scalar", "gpsimd"):
            P.emit(engn, (lambda sh, vv: (lambda e: e.wait_ge(sh, vv)))(s_c.h, CONST_W))

        def barrier():
            vals = [(s, s.n) for s in sems.values() if s.n > 0]
            for engn in ("sync", "gpsimd", "tensor", "vector", "scalar"):
                for s, v in vals:
                    P.emit(engn, (lambda sh, vv: (lambda e: e.wait_ge(sh, vv)))(s.h, v))

        prep_items = []
        for fc in range(8):
            prep_items.append((wq_img[fc], wq_pr[fc], 4096))
        for fc in range(8):
            prep_items.append((wk_img[fc], wk_pr[fc], 4096))
        for kc in range(32):
            prep_items.append((wv_img[kc], wv_pr[kc], 1024))
        for oc in range(32):
            prep_items.append((wo_img[oc], wo_pr[oc], 1024))
        for oc in range(32):
            prep_items.append((wbb_img[oc], wbb_pr[oc], 4224))
        prep_items.append((a_img, a_pr, 4096))
        prep_items.append((wr_img, wr_pr, 256))

        prep_ev = {}
        prep_st = {}
        prep_state = {"i": 0}

        def prep_emit_one():
            i = prep_state["i"]
            if i >= len(prep_items):
                return
            prep_state["i"] = i + 1
            src_, dst_, ncols = prep_items[i]
            stg = big2[:, 0:ncols]
            po_hi = cbf[:, 2048: 2048 + ncols]
            po_lo = cbf[:, 2048 + ncols: 2048 + 2 * ncols]
            LW = dma("sync", stg, src_, s_pw[0],
                     waits=[(s_evv, prep_ev.get(i - 1, 0))])
            P.emit("scalar",
                   (lambda o, i_: (lambda e: e.copy(out=o, in_=i_)))(po_hi, stg),
                   waits=[(s_pw[0], LW), (s_pst[0], prep_st.get(i - 1, 0))],
                   incs=[(s_evs, 1)])
            HEVp = s_evs.n
            P.emit("vector",
                   (lambda o, a, b: (lambda e: e.tensor_tensor(
                       out=o, in0=a, in1=b, op=ALU.subtract)))(po_lo, stg, po_hi),
                   waits=[(s_evs, HEVp)], incs=[(s_evv, 1)])
            prep_ev[i] = s_evv.n
            prep_st[i] = dma("gpsimd", dst_,
                             cbf[:, 2048: 2048 + 2 * ncols], s_pst[0],
                             waits=[(s_evv, prep_ev[i])])

        def prep_ready(idx):
            while prep_state["i"] <= idx:
                prep_emit_one()
            return (s_pst[0], prep_st[idx])

        def phase_floor():
            M0, S0, V0 = s_mm.n, s_evs.n, s_evv.n
            for engn in ("sync", "gpsimd", "tensor", "vector", "scalar"):
                for sh, vv in ((s_mm.h, M0), (s_evs.h, S0), (s_evv.h, V0)):
                    if vv > 0:
                        P.emit(engn, (lambda h, v: (lambda e: e.wait_ge(h, v)))(sh, vv))

        if "qkv" in dbg_phases:
            ev_hist_qk = {}
            last_v_evacs = 0
            mm_hist = {}
            st_hist = {}
            xsplit = {}
            for qq in range(NQ):
                for kc in range(32):
                    sl = kc % 2
                    xs = XSA if sl == 0 else XSB
                    xw = [(s_evs, xsplit.get((qq, kc - 2, "hi"),
                                              xsplit.get((qq - 1, 30 + (kc % 2), "hi"), 0))),
                          (s_evv, xsplit.get((qq, kc - 2, "lo"),
                                             xsplit.get((qq - 1, 30 + (kc % 2), "lo"), 0))),
                          (s_mm, mm_hist.get((qq - 1, "v", kc), 0))]
                    XL = dma("gpsimd", xs[:],
                             x_img[:, kc * TC + qq * QT: kc * TC + qq * QT + 512],
                             s_xp[sl], waits=xw)
                    xh = abf[:, kc * 512: kc * 512 + 512]
                    xl = abf[:, 16384 + kc * 512: 16384 + kc * 512 + 512]
                    hw = [(s_xp[sl], XL),
                          (s_mm, mm_hist.get((qq - 1, "v", kc), 0))]
                    P.emit("scalar",
                           (lambda o, i_: (lambda e: e.copy(out=o, in_=i_)))(xh, xs[:]),
                           waits=hw, incs=[(s_evs, 1)])
                    xsplit[(qq, kc, "hi")] = s_evs.n
                    P.emit("vector",
                           (lambda o, a, b: (lambda e: e.tensor_tensor(
                               out=o, in0=a, in1=b, op=ALU.subtract)))(xl, xs[:], xh),
                           waits=[(s_evs, xsplit[(qq, kc, "hi")])], incs=[(s_evv, 1)])
                    xsplit[(qq, kc, "lo")] = s_evv.n
                XRDY_S = xsplit[(qq, 31, "hi")]
                XRDY_V = xsplit[(qq, 31, "lo")]

                for j in range(16):
                    fc = j % 8
                    isq = j < 8
                    img = wq_pr if isq else wk_pr
                    slot = wbf[:, (j % 2) * 8448:(j % 2) * 8448 + 8192]
                    wv_wait = mm_hist.get((qq, "qk", j - 2), 0)
                    prw = prep_ready(j if isq else 8 + fc)
                    WL = dma("sync", slot, img[fc], s_w2[j % 2],
                             waits=[prw] +
                                   ([(s_mm, wv_wait)] if j >= 2 else
                                    ([(s_mm, mm_hist.get(("vlast", qq - 1), 0))] if qq > 0 else [])))
                    if qq == 0:
                        pass
                    else:
                        prep_emit_one()
                        prep_emit_one()
                    bank = pb[j % 2]
                    mmw = [(s_w2[j % 2], WL), (s_evs, XRDY_S), (s_evv, XRDY_V),
                           (s_evv, ev_hist_qk.get((qq, j - 2), last_v_evacs))]
                    for kc in range(32):
                        mm3(bank[:],
                            slot[:, kc * 128: kc * 128 + 128],
                            slot[:, 4096 + kc * 128: 4096 + kc * 128 + 128],
                            abf[:, kc * 512: kc * 512 + 512],
                            abf[:, 16384 + kc * 512: 16384 + kc * 512 + 512],
                            start=(kc == 0), stop=(kc == 31),
                            waits=(mmw if kc == 0 else ()),
                            inc=(s_mm if kc == 31 else None))
                    mm_hist[(qq, "qk", j)] = s_mm.n
                    evh = cbf[:, (j % 2) * 1024: (j % 2) * 1024 + 512]
                    evl = cbf[:, (j % 2) * 1024 + 512: (j % 2) * 1024 + 1024]
                    bias_ap = (BQ if isq else BK)[:, fc:fc + 1]
                    ev_reuse = st_hist.get((qq, j - 2),
                                           st_hist.get((qq - 1, "v", 6 + (j % 2)), 0))
                    P.emit("scalar",
                           (lambda o, i, b: (lambda e: e.activation(
                               out=o, in_=i, func=AF.Identity, bias=b, scale=1.0)))(
                               evh, bank[:], bias_ap),
                           waits=[(s_mm, mm_hist[(qq, "qk", j)]),
                                  (s_st2[j % 2], ev_reuse)],
                           incs=[(s_evs, 1)])
                    EVH = s_evs.n
                    P.emit("vector",
                           (lambda o, p_, b_, h_: (lambda e: e.scalar_tensor_tensor(
                               out=o, in0=p_, scalar=b_, in1=h_,
                               op0=ALU.add, op1=ALU.subtract)))(
                               evl, bank[:], bias_ap, evh),
                           waits=[(s_evs, EVH)], incs=[(s_evv, 1)])
                    ev_hist_qk[(qq, j)] = s_evv.n
                    if isq:
                        dst = q_scr[:, qq * 8192 + fc * 1024: qq * 8192 + fc * 1024 + 1024]
                    else:
                        dst = k_gin[qq][:, fc * 1024: fc * 1024 + 1024]
                    st_hist[(qq, j)] = dma(
                        "gpsimd", dst, cbf[:, (j % 2) * 1024:(j % 2) * 1024 + 1024],
                        s_st2[j % 2], waits=[(s_evv, ev_hist_qk[(qq, j)])])

                wv_loads = {}
                for kc in range(32):
                    slot = wbf[:, 8448 + (kc % 4) * 2048: 8448 + (kc % 4) * 2048 + 2048]
                    w_waits = []
                    if kc >= 4:
                        w_waits.append((s_mm, mm_hist[(qq, "v", kc - 4)]))
                    else:
                        w_waits.append((s_mm, mm_hist[(qq, "qk", 15)]))
                    wv_loads[kc] = dma("sync", slot, wv_pr[kc], s_wv4[kc % 4],
                                       waits=w_waits + [prep_ready(16 + kc)])
                    mmw = [(s_wv4[kc % 4], wv_loads[kc])]
                    if kc == 0:
                        mmw.append((s_evv, ev_hist_qk[(qq, 15)]))
                    for t in range(8):
                        tkc, dvh = t // 2, t % 2
                        xh = abf[:, kc * 512 + tkc * 128: kc * 512 + tkc * 128 + 128]
                        xl = abf[:, 16384 + kc * 512 + tkc * 128: 16384 + kc * 512 + tkc * 128 + 128]
                        rh = slot[:, dvh * 512: dvh * 512 + 512]
                        rl = slot[:, 1024 + dvh * 512: 1024 + dvh * 512 + 512]
                        mm(pb[t][:], xh, rh, start=(kc == 0), stop=False,
                           waits=(mmw if t == 0 else ()))
                        mm(pb[t][:], xh, rl, start=False, stop=False)
                        mm(pb[t][:], xl, rh, start=False, stop=False,
                           inc=(s_mm if t == 7 else None))
                    mm_hist[(qq, "v", kc)] = s_mm.n
                for t in range(8):
                    tkc, dvh = t // 2, t % 2
                    mm(pb[t][:], ones_s[0:1, :], bv_s[0:1, dvh * 512: dvh * 512 + 512],
                       start=False, stop=True, inc=s_mm)
                    mm_hist[(qq, "vb", t)] = s_mm.n
                    evh = cbf[:, (t % 2) * 1024: (t % 2) * 1024 + 512]
                    evl = cbf[:, (t % 2) * 1024 + 512: (t % 2) * 1024 + 1024]
                    P.emit("scalar",
                           (lambda o, i: (lambda e: e.copy(out=o, in_=i)))(evh, pb[t][:]),
                           waits=[(s_mm, mm_hist[(qq, "vb", t)]),
                                  (s_st2[t % 2],
                                   st_hist.get((qq, "v", t - 2),
                                               st_hist[(qq, 14 + (t % 2))]))],
                           incs=[(s_evs, 1)])
                    EVH = s_evs.n
                    P.emit("vector",
                           (lambda o, p_, h_: (lambda e: e.tensor_tensor(
                               out=o, in0=p_, in1=h_, op=ALU.subtract)))(
                               evl, pb[t][:], evh),
                           waits=[(s_evs, EVH)], incs=[(s_evv, 1)])
                    evn = s_evv.n
                    dst = v_gin[qq][:, tkc * 2048 + dvh * 1024: tkc * 2048 + dvh * 1024 + 1024]
                    st_hist[(qq, "v", t)] = dma(
                        "gpsimd", dst, cbf[:, (t % 2) * 1024:(t % 2) * 1024 + 1024],
                        s_st2[t % 2], waits=[(s_evv, evn)])
                last_v_evacs = s_evv.n
                mm_hist[("vlast", qq)] = mm_hist[(qq, "vb", 7)]

                ccw = [(s_st2[0], st_hist[(qq, "v", 6)]),
                       (s_st2[1], st_hist[(qq, "v", 7)])]
                P.emit("gpsimd",
                       (lambda i, o: (lambda e: e.collective_compute(
                           "AllGather", ALU.bypass,
                           replica_groups=[list(range(NCORE))],
                           ins=[i], outs=[o])))(k_gin[qq], k_gout[qq]),
                       waits=ccw, incs=[(s_cc, 1)])
                P.emit("gpsimd",
                       (lambda i, o: (lambda e: e.collective_compute(
                           "AllGather", ALU.bypass,
                           replica_groups=[list(range(NCORE))],
                           ins=[i], outs=[o])))(v_gin[qq], v_gout[qq]),
                       incs=[(s_cc, 1)])
            qkv_qst = {(g, j): st_hist[(g, j)] for g in range(NQ) for j in (6, 7)}
            qkv_st_entry = (s_st2[0].n, s_st2[1].n, s_pst[0].n)
            phase_floor()

        if "attn" in dbg_phases:
            KSLOT = [abf[:, 0:8192], abf[:, 8192:16384]]
            VSLOT = [abf[:, 16384:24576], abf[:, 24576:32768]]
            QTS = cbf[:, 0:8192]
            OSB = big2[:, 0:4096]
            RREC = big2[:, 4096:4608]

            def psl(s, tkc, lo):
                off = 8192 + s * 4096 + tkc * 1024 + (512 if lo else 0)
                return cbf[:, off: off + 512]

            kv_hist = {}
            smm = {}
            omm = {}
            expc = {}
            flsh = {}
            sums_fin = {}
            attn_st = {}
            for g in range(NQ):
                qw = [(s_mm, omm.get((g - 1, 31, 1), 0)),
                      (s_st2[0], qkv_qst[(g, 6)]), (s_st2[1], qkv_qst[(g, 7)])]
                if g == 0:
                    qw += [(s_st2[0], qkv_st_entry[0]), (s_st2[1], qkv_st_entry[1]),
                           (s_pst[0], qkv_st_entry[2])]
                QL = dma("sync", QTS, q_scr[:, g * 8192: g * 8192 + 8192], s_q, waits=qw)
                NB = 32
                for b in range(NB):
                    qq, r = b // 8, b % 8
                    sl = b % 2
                    kvw = [(s_mm, smm.get((g, b - 2, 3),
                                          smm.get((g - 1, 30 + (b % 2), 3), 0)))]
                    kv_hist[(g, b, "k")] = dma(
                        "sync", KSLOT[sl],
                        k_gout[qq][r * 128:(r + 1) * 128, :], s_kk[sl],
                        waits=kvw + [(s_cc, 2 * (qq + 1))])
                    vw = [(s_mm, omm.get((g, b - 2, 1),
                                         omm.get((g - 1, 30 + (b % 2), 1), 0)))]
                    kv_hist[(g, b, "v")] = dma(
                        "sync", VSLOT[sl],
                        v_gout[qq][r * 128:(r + 1) * 128, :], s_kva[sl], waits=vw)
                    for tkc in range(4):
                        bank = pb[tkc % 2]
                        w = [(s_evv, expc.get((g, b, tkc - 2),
                                              expc.get((g, b - 1, tkc + 2),
                                                       expc.get((g - 1, NB - 1, tkc + 2), 0))))]
                        if tkc == 0:
                            w.append((s_kk[sl], kv_hist[(g, b, "k")]))
                            w.append((s_q, QL))
                        for dc in range(8):
                            kh = KSLOT[sl][:, dc * 1024 + tkc * 128: dc * 1024 + tkc * 128 + 128]
                            kl = KSLOT[sl][:, dc * 1024 + 512 + tkc * 128: dc * 1024 + 512 + tkc * 128 + 128]
                            qh = QTS[:, dc * 1024: dc * 1024 + 512]
                            ql = QTS[:, dc * 1024 + 512: dc * 1024 + 1024]
                            mm3(bank[:], kh, kl, qh, ql,
                                start=(dc == 0), stop=(dc == 7),
                                waits=(w if dc == 0 else ()),
                                inc=(s_mm if dc == 7 else None))
                        smm[(g, b, tkc)] = s_mm.n
                        we = [(s_mm, smm[(g, b, tkc)])]
                        if b >= 2:
                            we.append((s_mm, omm[(g, b - 2, 1)]))
                        else:
                            we.append((s_mm, omm.get((g - 1, 30 + b, 1), 0)))
                        pf = PFA if tkc % 2 == 0 else PFB
                        P.emit("scalar",
                               (lambda o, i: (lambda e: e.activation(
                                   out=o, in_=i, func=AF.Exp, scale=ISQRT_D)))(
                                   pf[:], pb[tkc % 2][:]),
                               waits=we + [(s_evv, expc.get((g, b, tkc - 2),
                                                            expc.get((g, b - 1, tkc + 2),
                                                                     expc.get((g - 1, NB - 1, tkc + 2), 0))))])
                        P.emit("scalar",
                               (lambda o, i: (lambda e: e.activation(
                                   out=o, in_=i, func=AF.Exp, scale=ISQRT_D)))(
                                   psl(b % 2, tkc, False), pb[tkc % 2][:]),
                               incs=[(s_evs, 1)])
                        PHEV = s_evs.n
                        P.emit("vector",
                               (lambda o, a, h_: (lambda e: e.tensor_tensor(
                                   out=o, in0=a, in1=h_, op=ALU.subtract)))(
                                   psl(b % 2, tkc, True), pf[:], psl(b % 2, tkc, False)),
                               waits=[(s_evs, PHEV)], incs=[(s_evv, 1)])
                        expc[(g, b, tkc)] = s_evv.n
                    if b > 0:
                        bp = b - 1
                        slp = bp % 2
                        w = [(s_evv, flsh[(g, bp, 0)])]
                        for i in range(4):
                            dvc = 4 + i
                            for tkc in range(4):
                                vbase = tkc * 2048 + (dvc // 4) * 1024 + (dvc % 4) * 128
                                vh = VSLOT[slp][:, vbase: vbase + 128]
                                vl = VSLOT[slp][:, vbase + 512: vbase + 640]
                                mm3(pb[2 + i][:], vh, vl,
                                    psl(bp % 2, tkc, False), psl(bp % 2, tkc, True),
                                    start=(tkc == 0), stop=(tkc == 3),
                                    waits=(w if (i == 0 and tkc == 0) else ()),
                                    inc=(s_mm if (i == 3 and tkc == 3) else None))
                        omm[(g, bp, 1)] = s_mm.n
                    for tkc in range(4):
                        mm(pb[7][0:1, :], onesb[:, 0:1], psl(b % 2, tkc, False),
                           start=(b == 0 and tkc == 0), stop=False,
                           waits=[(s_evv, expc[(g, b, tkc)])] +
                                 ([(s_evv, sums_fin.get(g - 1, 0))]
                                  if (b == 0 and tkc == 0) else []))
                        mm(pb[7][0:1, :], onesb[:, 0:1], psl(b % 2, tkc, True),
                           start=False, stop=(b == NB - 1 and tkc == 3),
                           inc=(s_mm if (b == NB - 1 and tkc == 3) else None))
                    if b == NB - 1:
                        smm[(g, "sums")] = s_mm.n
                    w = [(s_evv, flsh.get((g, b - 1, 1), flsh.get((g - 1, 31, 1), 0)))]
                    for i in range(4):
                        dvc = i
                        for tkc in range(4):
                            vbase = tkc * 2048 + (dvc // 4) * 1024 + (dvc % 4) * 128
                            vh = VSLOT[sl][:, vbase: vbase + 128]
                            vl = VSLOT[sl][:, vbase + 512: vbase + 640]
                            mm3(pb[2 + i][:], vh, vl,
                                psl(b % 2, tkc, False), psl(b % 2, tkc, True),
                                start=(tkc == 0), stop=(tkc == 3),
                                waits=((w + [(s_kva[sl], kv_hist[(g, b, "v")])])
                                       if (i == 0 and tkc == 0) else ()),
                                inc=(s_mm if (i == 3 and tkc == 3) else None))
                    omm[(g, b, 0)] = s_mm.n
                    for bb_, hh in ((b - 1, 1), (b, 0)):
                        if bb_ < 0:
                            continue
                        for i in range(4):
                            dvc = hh * 4 + i
                            dst = OSB[:, dvc * 512: dvc * 512 + 512]
                            if bb_ == 0:
                                fn = (lambda o, i_: (lambda e: e.tensor_copy(out=o, in_=i_)))(
                                    dst, pb[2 + i][:])
                            else:
                                fn = (lambda o, i_: (lambda e: e.tensor_tensor(
                                    out=o, in0=i_, in1=o, op=ALU.add)))(dst, pb[2 + i][:])
                            P.emit("vector", fn,
                                   waits=([(s_mm, omm[(g, bb_, hh)])] if i == 0 else []),
                                   incs=([(s_evv, 1)] if i == 3 else []))
                        flsh[(g, bb_, hh)] = s_evv.n
                bp = NB - 1
                slp = bp % 2
                w = [(s_evv, flsh[(g, bp, 0)])]
                for i in range(4):
                    dvc = 4 + i
                    for tkc in range(4):
                        vbase = tkc * 2048 + (dvc // 4) * 1024 + (dvc % 4) * 128
                        vh = VSLOT[slp][:, vbase: vbase + 128]
                        vl = VSLOT[slp][:, vbase + 512: vbase + 640]
                        mm3(pb[2 + i][:], vh, vl,
                            psl(bp % 2, tkc, False), psl(bp % 2, tkc, True),
                            start=(tkc == 0), stop=(tkc == 3),
                            waits=(w if (i == 0 and tkc == 0) else ()),
                            inc=(s_mm if (i == 3 and tkc == 3) else None))
                omm[(g, bp, 1)] = s_mm.n
                for i in range(4):
                    dvc = 4 + i
                    dst = OSB[:, dvc * 512: dvc * 512 + 512]
                    P.emit("vector",
                           (lambda o, i_: (lambda e: e.tensor_tensor(
                               out=o, in0=i_, in1=o, op=ALU.add)))(dst, pb[2 + i][:]),
                           waits=([(s_mm, omm[(g, bp, 1)])] if i == 0 else []),
                           incs=([(s_evv, 1)] if i == 3 else []))
                flsh[(g, bp, 1)] = s_evv.n

                P.emit("vector",
                       (lambda: (lambda e: e.tensor_copy(out=st1[:], in_=pb[7][0:1, :])))(),
                       waits=[(s_mm, smm[(g, "sums")])])
                P.emit("vector",
                       (lambda: (lambda e: e.reciprocal(st2[:], st1[:])))(),
                       incs=[(s_evv, 1)])
                sums_fin[g] = s_evv.n
                RECIP = s_evv.n
                mm(pb[6][:], ones_s[0:1, :], st2[0:1, :], start=True, stop=True,
                   waits=[(s_evv, RECIP)], inc=s_mm)
                RMM = s_mm.n
                P.emit("vector",
                       (lambda: (lambda e: e.tensor_copy(out=RREC[:], in_=pb[6][:])))(),
                       waits=[(s_mm, RMM)], incs=[(s_evv, 1)])
                RRDY = s_evv.n
                for dvc in range(8):
                    evf = EVA if dvc % 2 == 0 else EVB
                    P.emit("vector",
                           (lambda o, a, r_: (lambda e: e.tensor_tensor(
                               out=o, in0=a, in1=r_, op=ALU.mult)))(
                               evf[:], OSB[:, dvc * 512: dvc * 512 + 512], RREC[:]),
                           waits=[(s_evv, RRDY)],
                           incs=[(s_evv, 1)])
                    OSC = s_evv.n
                    pprev = (dvc - 1) % 2
                    gq_free = attn_st.get((g, dvc - 1), attn_st.get((g - 1, 7), 0))
                    P.emit("scalar",
                           (lambda o, i_: (lambda e: e.copy(out=o, in_=i_)))(
                               gq[:, 0:512], evf[:]),
                           waits=[(s_evv, OSC), (s_st2[pprev], gq_free)],
                           incs=[(s_evs, 1)])
                    OHEV = s_evs.n
                    P.emit("vector",
                           (lambda o, a, h_: (lambda e: e.tensor_tensor(
                               out=o, in0=a, in1=h_, op=ALU.subtract)))(
                               gq[:, 512:1024], evf[:], gq[:, 0:512]),
                           waits=[(s_evs, OHEV)], incs=[(s_evv, 1)])
                    OLEV = s_evv.n
                    attn_st[(g, dvc)] = dma(
                        "gpsimd",
                        o_scr[:, g * 8192 + dvc * 1024: g * 8192 + dvc * 1024 + 1024],
                        gq[:, 0:1024], s_st2[dvc % 2], waits=[(s_evv, OLEV)])
            attn_ost = {(g, d): attn_st[(g, d)] for g in range(NQ) for d in (6, 7)}
            phase_floor()

        if "ffn" in dbg_phases:
            ffn_mm = {}
            ffn_ev = {}
            y_hist = {}
            r1st = {}
            r2st = {}
            statclr = {"v": 0}
            x1_read_last = {"v": 0}
            x1s_last = {"s": 0}
            WRL = dma("sync", wr_s[:], wr_pr, s_c,
                      waits=[prep_ready(113)])
            for qq in range(NQ):
                OAT = cbf[:, 0:8192]
                olw = [(s_mm, ffn_mm.get((qq - 1, "base", 31), 0))]
                if qq == 0:
                    olw += [(s_st2[0], attn_ost[(3, 6)]), (s_st2[1], attn_ost[(3, 7)])]
                OL = dma("sync", OAT, o_scr[:, qq * 8192: qq * 8192 + 8192], s_q,
                         waits=olw)
                for oc in range(32):
                    slot = wbf[:, 8448 + (oc % 2) * 2048: 8448 + (oc % 2) * 2048 + 2048]
                    WL = dma("sync", slot, wo_pr[oc], s_w2[oc % 2],
                             waits=[prep_ready(48 + oc),
                                    (s_mm, ffn_mm.get((qq, "proj", oc - 2),
                                                      ffn_mm.get((qq - 1, "base", 31), 0)))])
                    xs = XSA if oc % 2 == 0 else XSB
                    XL = dma("gpsimd", xs[:],
                             x_img[:, oc * TC + qq * QT: oc * TC + qq * QT + 512],
                             s_xp[oc % 2],
                             waits=[(s_evv,
                                     ffn_ev.get((qq, "r1", oc - 2),
                                                ffn_ev.get((qq - 1, "out", 30 + (oc % 2)), 0)))])
                    bank = pb[oc % 2]
                    w = [(s_w2[oc % 2], WL), (s_q, OL),
                         (s_evv, ffn_ev.get((qq, "r1", oc - 2), 0))]
                    for dc in range(8):
                        mm3(bank[:],
                            slot[:, dc * 128: dc * 128 + 128],
                            slot[:, 1024 + dc * 128: 1024 + dc * 128 + 128],
                            OAT[:, dc * 1024: dc * 1024 + 512],
                            OAT[:, dc * 1024 + 512: dc * 1024 + 1024],
                            start=(dc == 0), stop=(dc == 7),
                            waits=(w if dc == 0 else ()),
                            inc=(s_mm if dc == 7 else None))
                    ffn_mm[(qq, "proj", oc)] = s_mm.n
                    r1e = EVA if oc % 2 == 0 else EVB
                    r1w = [(s_mm, ffn_mm[(qq, "proj", oc)]), (s_xp[oc % 2], XL),
                           (s_st2[oc % 2],
                            r1st.get((qq, oc - 2),
                                     r2st.get((qq - 1, 30 + (oc % 2)), 0))),
                           (s_mm, ffn_mm.get((qq, "sqm", oc - 2), 0))]
                    if oc == 0:
                        r1w.append((s_mm, x1_read_last["v"]))
                        r1w.append((s_evs, x1s_last["s"]))
                    P.emit("vector",
                           (lambda o, p_, b_, x_: (lambda e: e.scalar_tensor_tensor(
                               out=o, in0=p_, scalar=b_, in1=x_,
                               op0=ALU.add, op1=ALU.add)))(
                               r1e[:], bank[:], BO[:, oc:oc + 1], xs[:]),
                           waits=r1w,
                           incs=[(s_evv, 1)])
                    ffn_ev[(qq, "r1", oc)] = s_evv.n
                    r1st[(qq, oc)] = dma(
                        "gpsimd", r1_scr[:, oc * 512: oc * 512 + 512], r1e[:],
                        s_st2[oc % 2], waits=[(s_evv, ffn_ev[(qq, "r1", oc)])])
                    sq = SQA if oc % 2 == 0 else SQB
                    P.emit("scalar",
                           (lambda o, i: (lambda e: e.activation(
                               out=o, in_=i, func=AF.Square)))(sq[:], r1e[:]),
                           waits=[(s_evv, ffn_ev[(qq, "r1", oc)])] +
                                 ([(s_mm, ffn_mm.get((qq, "sqm", oc - 2), 0))]
                                  if oc >= 2 else []),
                           incs=[(s_evs, 1)])
                    ffn_ev[(qq, "sq", oc)] = s_evs.n
                    mm(pb[6][0:1, :], ones_s[:, 0:1], r1e[:],
                       start=(oc == 0), stop=(oc == 31),
                       waits=[(s_evv, ffn_ev[(qq, "r1", oc)])] +
                             ([(s_evv, statclr["v"])] if oc == 0 else []),
                       inc=(s_mm if oc == 31 else None))
                    mm(pb[7][0:1, :], ones_s[:, 0:1], sq[:],
                       start=(oc == 0), stop=(oc == 31),
                       waits=[(s_evs, ffn_ev[(qq, "sq", oc)])],
                       inc=(s_mm if oc == 31 else None))
                    if oc == 31:
                        ffn_mm[(qq, "stats")] = s_mm.n
                    ffn_mm[(qq, "sqm", oc)] = s_mm.n

                def ln_stats_epilogue(stats_mm_val):
                    P.emit("vector",
                           (lambda: (lambda e: e.tensor_copy(out=st1[:], in_=pb[6][0:1, :])))(),
                           waits=[(s_mm, stats_mm_val)])
                    P.emit("vector",
                           (lambda: (lambda e: e.tensor_copy(out=st2[:], in_=pb[7][0:1, :])))(),
                           incs=[(s_evv, 1)])
                    statclr["v"] = s_evv.n
                    P.emit("vector",
                           (lambda: (lambda e: e.tensor_scalar_mul(st3[:], st1[:], 1.0 / D)))())
                    P.emit("vector",
                           (lambda: (lambda e: e.tensor_scalar_mul(st6[:], st2[:], 1.0 / D)))())
                    P.emit("vector",
                           (lambda: (lambda e: e.tensor_tensor(
                               out=st4[:], in0=st3[:], in1=st3[:], op=ALU.mult)))())
                    P.emit("vector",
                           (lambda: (lambda e: e.tensor_tensor(
                               out=st2[:], in0=st6[:], in1=st4[:], op=ALU.subtract)))())
                    P.emit("vector",
                           (lambda: (lambda e: e.tensor_scalar_add(st6[:], st2[:], EPS)))(),
                           incs=[(s_evv, 1)])
                    VEV = s_evv.n
                    P.emit("scalar",
                           (lambda: (lambda e: e.activation(out=st1[:], in_=st6[:], func=AF.Ln)))(),
                           waits=[(s_evv, VEV)])
                    P.emit("scalar",
                           (lambda: (lambda e: e.activation(
                               out=st4[:], in_=st1[:], func=AF.Exp, scale=-0.5)))(),
                           incs=[(s_evs, 1)])
                    SEV = s_evs.n
                    P.emit("vector",
                           (lambda: (lambda e: e.tensor_tensor(
                               out=st2[:], in0=st3[:], in1=st4[:], op=ALU.mult)))(),
                           waits=[(s_evs, SEV)])
                    P.emit("vector",
                           (lambda: (lambda e: e.tensor_scalar_mul(st3[:], st2[:], -1.0)))(),
                           incs=[(s_evv, 1)])
                    NEV = s_evv.n
                    mm(pb[2][:], ones_s[0:1, :], st4[0:1, :], start=True, stop=True,
                       waits=[(s_evv, NEV)], inc=s_mm)
                    mm(pb[3][:], ones_s[0:1, :], st3[0:1, :], start=True, stop=True,
                       inc=s_mm)
                    REPL = s_mm.n
                    P.emit("vector",
                           (lambda: (lambda e: e.tensor_copy(out=R128[:], in_=pb[2][:])))(),
                           waits=[(s_mm, REPL)])
                    P.emit("vector",
                           (lambda: (lambda e: e.tensor_copy(out=M128[:], in_=pb[3][:])))(),
                           incs=[(s_evv, 1)])
                    return s_evv.n

                RRDY1 = ln_stats_epilogue(ffn_mm[(qq, "stats")])
                for dc in range(32):
                    r1l = R2LA if dc % 2 == 0 else R2LB
                    RL = dma("sync", r1l[:], r1_scr[:, dc * 512: dc * 512 + 512],
                             s_r2l[dc % 2],
                             waits=[(s_st2[dc % 2], r1st[(qq, dc)]),
                                    (s_evs, x1s_last["s"] if dc < 2 else 0),
                                    (s_evv,
                                     ffn_ev.get((qq, "x1v", dc - 2),
                                                ffn_ev.get((qq, "r1", 30 + (dc % 2)), 0)))])
                    P.emit("vector",
                           (lambda o, i: (lambda e: e.tensor_tensor(
                               out=o, in0=i, in1=R128[:], op=ALU.mult)))(
                               OUTT[:], r1l[:]),
                           waits=[(s_r2l[dc % 2], RL)] +
                                 ([(s_evv, RRDY1)] if dc == 0 else []))
                    P.emit("vector",
                           (lambda o: (lambda e: e.tensor_tensor(
                               out=o, in0=OUTT[:], in1=M128[:], op=ALU.add)))(OUT2[:]),
                           incs=[(s_evv, 1)])
                    X1C = s_evv.n
                    xh = abf[:, dc * 512: dc * 512 + 512]
                    xl = abf[:, 16384 + dc * 512: 16384 + dc * 512 + 512]
                    P.emit("scalar",
                           (lambda o, i_: (lambda e: e.copy(out=o, in_=i_)))(xh, OUT2[:]),
                           waits=[(s_evv, X1C)] +
                                 ([(s_mm, x1_read_last["v"])] if dc == 0 else []),
                           incs=[(s_evs, 1)])
                    XHEV = s_evs.n
                    P.emit("vector",
                           (lambda o, a, h_: (lambda e: e.tensor_tensor(
                               out=o, in0=a, in1=h_, op=ALU.subtract)))(
                               xl, OUT2[:], xh),
                           waits=[(s_evs, XHEV)], incs=[(s_evv, 1)])
                    ffn_ev[(qq, "x1v", dc)] = s_evv.n
                x1s_last["s"] = s_evs.n
                X1_LAST = s_evv.n
                if dbg_outs:
                    dma("gpsimd", dbg["d_x1"][qq], abf[:], s_y,
                        waits=[(s_evv, X1_LAST)])

                for tc in range(4):
                    for kc in range(32):
                        xh = abf[:, kc * 512 + tc * 128: kc * 512 + tc * 128 + 128]
                        xl = abf[:, 16384 + kc * 512 + tc * 128: 16384 + kc * 512 + tc * 128 + 128]
                        mm3(pb[4][:, 0:8], xh, xl,
                            wr_s[:, kc * 8: kc * 8 + 8],
                            wr_s[:, 256 + kc * 8: 256 + kc * 8 + 8],
                            start=(kc == 0), stop=(kc == 31),
                            waits=([(s_evv, X1_LAST),
                                    (s_evv, ffn_ev.get((qq, "wtr", tc - 1), 0)),
                                    (s_c, WRL)]
                                   if kc == 0 else ()),
                            inc=(s_mm if kc == 31 else None))
                    LMM = s_mm.n
                    P.emit("scalar",
                           (lambda: (lambda e: e.copy(out=lg[:], in_=pb[4][:, 0:8])))(),
                           waits=[(s_mm, LMM)], incs=[(s_evs, 1)])
                    LEV = s_evs.n

                    def vop(fn, waits=(), incs=()):
                        P.emit("vector", fn, waits=waits, incs=incs)
                        P.emit("vector", (lambda: (lambda e: e.engine_nop()))())
                        P.emit("vector", (lambda: (lambda e: e.engine_nop()))())
                    vop((lambda: (lambda e: e.max(out=srt[:], in_=lg[:])))(),
                        waits=[(s_evs, LEV)])
                    vop((lambda: (lambda e: e.tensor_tensor(
                        out=negm1[:], in0=srt[:, 0:1], in1=srt[:, 1:2],
                        op=ALU.subtract)))())
                    P.emit("vector", (lambda: (lambda e: e.engine_nop()))(),
                           incs=[(s_evv, 1)])
                    TEV = s_evv.n
                    P.emit("scalar",
                           (lambda: (lambda e: e.activation(
                               out=l2t[:], in_=negm1[:], func=AF.Sigmoid)))(),
                           waits=[(s_evv, TEV)], incs=[(s_evs, 1)])
                    AEV = s_evs.n
                    vop((lambda: (lambda e: e.tensor_scalar(
                        e2t[:], l2t[:], -2.0, 2.0, op0=ALU.mult, op1=ALU.add)))(),
                        waits=[(s_evs, AEV)])
                    vop((lambda: (lambda e: e.tensor_scalar(
                        r2t[:], l2t[:], 4.0, -2.0, op0=ALU.mult, op1=ALU.add)))())
                    vop((lambda: (lambda e: e.tensor_scalar(
                        msk[:], lg[:], srt[:, 0:1], None, op0=ALU.is_equal)))())
                    vop((lambda: (lambda e: e.tensor_scalar(
                        ee[:], lg[:], srt[:, 1:2], None, op0=ALU.is_ge)))())
                    vop((lambda: (lambda e: e.tensor_scalar(
                        lg[:], ee[:], e2t[:], None, op0=ALU.mult)))())
                    vop((lambda tc_: (lambda e: e.scalar_tensor_tensor(
                        out=w4[:, tc_ * 8: tc_ * 8 + 8], in0=msk[:], scalar=r2t[:],
                        in1=lg[:], op0=ALU.mult, op1=ALU.add)))(tc),
                        incs=[(s_evv, 1)])
                    WEV = s_evv.n
                    P.emit("tensor",
                           (lambda tc_: (lambda e: e.transpose(
                               pb[5][0:8, tc_ * 128: tc_ * 128 + 128],
                               w4[:, tc_ * 8: tc_ * 8 + 8], ident_s[:])))(tc),
                           waits=[(s_evv, WEV)], incs=[(s_mm, 1)])
                    ffn_mm[(qq, "wt", tc)] = s_mm.n
                    ffn_ev[(qq, "wtr", tc)] = WEV
                P.emit("vector",
                       (lambda: (lambda e: e.tensor_copy(out=wem[0:8, 0:512], in_=pb[5][0:8, :])))(),
                       waits=[(s_mm, ffn_mm[(qq, "wt", 3)])], incs=[(s_evv, 1)])
                WEMEV = s_evv.n
                if dbg_outs:
                    dma("gpsimd", dbg["d_w4"][qq], w4[:], s_y, waits=[(s_evv, WEMEV)])
                mm(pb[2][:], rt_s[0:8, :], wem[0:8, 0:512], start=True, stop=True,
                   waits=[(s_evv, WEMEV)], inc=s_mm)
                SFMM = s_mm.n
                P.emit("vector",
                       (lambda: (lambda e: e.tensor_copy(out=SFM[:], in_=pb[2][:])))(),
                       waits=[(s_mm, SFMM)], incs=[(s_evv, 1)])
                SFEV = s_evv.n
                AL = dma("sync", wbf[:, 0:8192], a_pr, s_w2[0],
                         waits=[prep_ready(112), (s_mm, ffn_mm[(qq, "proj", 31)])])
                for kc in range(32):
                    mm3(pb[3][:],
                        wbf[:, kc * 128: kc * 128 + 128],
                        wbf[:, 4096 + kc * 128: 4096 + kc * 128 + 128],
                        abf[:, kc * 512: kc * 512 + 512],
                        abf[:, 16384 + kc * 512: 16384 + kc * 512 + 512],
                        start=(kc == 0), stop=(kc == 31),
                        waits=([(s_w2[0], AL), (s_evv, X1_LAST)] if kc == 0 else ()),
                        inc=(s_mm if kc == 31 else None))
                ffn_mm[(qq, "h")] = s_mm.n
                P.emit("scalar",
                       (lambda: (lambda e: e.copy(out=HT[:], in_=pb[3][:])))(),
                       waits=[(s_mm, ffn_mm[(qq, "h")])], incs=[(s_evs, 1)])
                HEV = s_evs.n
                P.emit("vector",
                       (lambda: (lambda e: e.tensor_tensor(
                           out=OUTT[:], in0=HT[:], in1=SFM[:], op=ALU.mult)))(),
                       waits=[(s_evs, HEV), (s_evv, SFEV),
                              (s_mm, ffn_mm.get((qq - 1, "base", 31), 0))],
                       incs=[(s_evv, 1)])
                GFEV = s_evv.n
                P.emit("scalar",
                       (lambda: (lambda e: e.copy(out=gq[:, 0:512], in_=OUTT[:])))(),
                       waits=[(s_evv, GFEV)], incs=[(s_evs, 1)])
                GHEV = s_evs.n
                P.emit("vector",
                       (lambda: (lambda e: e.tensor_tensor(
                           out=gq[:, 512:1024], in0=OUTT[:], in1=gq[:, 0:512],
                           op=ALU.subtract)))(),
                       waits=[(s_evs, GHEV)], incs=[(s_evv, 1)])
                GEV = s_evv.n
                for oc in range(32):
                    slot = wbf[:, (oc % 2) * 8448:(oc % 2) * 8448 + 8448]
                    WL = dma("sync", slot, wbb_pr[oc], s_w2[oc % 2],
                             waits=[prep_ready(80 + oc),
                                    (s_mm, ffn_mm.get((qq, "base", oc - 2),
                                                      ffn_mm[(qq, "h")]))])
                    bank = pb[oc % 2]
                    w = [(s_w2[oc % 2], WL), (s_evv, GEV),
                         (s_evv, ffn_ev.get((qq, "r2", oc - 2), 0))]
                    for dc in range(32):
                        mm3(bank[:],
                            slot[:, dc * 128:(dc + 1) * 128],
                            slot[:, 4224 + dc * 128: 4224 + dc * 128 + 128],
                            abf[:, dc * 512: dc * 512 + 512],
                            abf[:, 16384 + dc * 512: 16384 + dc * 512 + 512],
                            start=(dc == 0), stop=False,
                            waits=(w if dc == 0 else ()))
                    mm(bank[:], slot[:, 4096:4224], gq[:, 0:512],
                       start=False, stop=False)
                    mm(bank[:], slot[:, 4096:4224], gq[:, 512:1024],
                       start=False, stop=False)
                    mm(bank[:], slot[:, 8320:8448], gq[:, 0:512],
                       start=False, stop=True, inc=s_mm)
                    ffn_mm[(qq, "base", oc)] = s_mm.n
                    tt = XSA if oc % 2 == 0 else XSB
                    r2e = EVA if oc % 2 == 0 else EVB
                    P.emit("vector",
                           (lambda o, p_, b_, x_: (lambda e: e.scalar_tensor_tensor(
                               out=o, in0=p_, scalar=b_, in1=x_,
                               op0=ALU.add, op1=ALU.add)))(
                               tt[:], bank[:], BB[:, oc:oc + 1],
                               abf[:, oc * 512: oc * 512 + 512]),
                           waits=[(s_mm, ffn_mm[(qq, "base", oc)])],
                           incs=[(s_evv, 1)])
                    P.emit("vector",
                           (lambda o, a, b_: (lambda e: e.tensor_tensor(
                               out=o, in0=a, in1=b_, op=ALU.add)))(
                               r2e[:], tt[:],
                               abf[:, 16384 + oc * 512: 16384 + oc * 512 + 512]),
                           waits=[(s_st2[oc % 2],
                                   r2st.get((qq, oc - 2),
                                            r1st[(qq, 30 + (oc % 2))]))],
                           incs=[(s_evv, 1)])
                    ffn_ev[(qq, "r2", oc)] = s_evv.n
                    r2st[(qq, oc)] = dma(
                        "gpsimd", r2_scr[:, oc * 512: oc * 512 + 512], r2e[:],
                        s_st2[oc % 2], waits=[(s_evv, ffn_ev[(qq, "r2", oc)])])
                    sq = SQA if oc % 2 == 0 else SQB
                    P.emit("scalar",
                           (lambda o, i: (lambda e: e.activation(
                               out=o, in_=i, func=AF.Square)))(sq[:], r2e[:]),
                           waits=[(s_evv, ffn_ev[(qq, "r2", oc)])] +
                                 ([(s_mm, ffn_mm.get((qq, "sqm2", oc - 2), 0))]
                                  if oc >= 2 else []),
                           incs=[(s_evs, 1)])
                    SQEV = s_evs.n
                    mm(pb[6][0:1, :], ones_s[:, 0:1], r2e[:],
                       start=(oc == 0), stop=(oc == 31),
                       waits=[(s_evv, ffn_ev[(qq, "r2", oc)])] +
                             ([(s_evv, statclr["v"])] if oc == 0 else []),
                       inc=(s_mm if oc == 31 else None))
                    mm(pb[7][0:1, :], ones_s[:, 0:1], sq[:],
                       start=(oc == 0), stop=(oc == 31),
                       waits=[(s_evs, SQEV)],
                       inc=(s_mm if oc == 31 else None))
                    if oc == 31:
                        ffn_mm[(qq, "stats2")] = s_mm.n
                    ffn_mm[(qq, "sqm2", oc)] = s_mm.n

                x1_read_last["v"] = ffn_mm[(qq, "base", 31)]
                if dbg_outs:
                    dma("gpsimd", dbg["d_r2"][qq], r2_scr, s_y,
                        waits=[(s_st2[0], r2st[(qq, 30)]), (s_st2[1], r2st[(qq, 31)])])
                RRDY2 = ln_stats_epilogue(ffn_mm[(qq, "stats2")])
                for dc in range(32):
                    r2l = R2LA if dc % 2 == 0 else R2LB
                    RL = dma("sync", r2l[:], r2_scr[:, dc * 512: dc * 512 + 512],
                             s_r2l[dc % 2],
                             waits=[(s_st2[dc % 2], r2st[(qq, dc)]),
                                    (s_evv,
                                     ffn_ev.get((qq, "out", dc - 2),
                                                ffn_ev.get((qq, "r2", 30 + (dc % 2)), 0)))])
                    P.emit("vector",
                           (lambda o, i: (lambda e: e.tensor_tensor(
                               out=o, in0=i, in1=R128[:], op=ALU.mult)))(
                               OUTT[:], r2l[:]),
                           waits=[(s_r2l[dc % 2], RL)] +
                                 ([(s_evv, RRDY2)] if dc == 0 else []) +
                                 [(s_y, y_hist.get((qq, dc - 1),
                                                   y_hist.get((qq - 1, 31), 0)))])
                    P.emit("vector",
                           (lambda o: (lambda e: e.tensor_tensor(
                               out=o, in0=OUTT[:], in1=M128[:], op=ALU.add)))(OUT2[:]),
                           incs=[(s_evv, 1)])
                    ffn_ev[(qq, "out", dc)] = s_evv.n
                    y_hist[(qq, dc)] = dma(
                        "gpsimd",
                        y_img[:, dc * TC + qq * QT: dc * TC + qq * QT + 512],
                        OUT2[:], s_y, waits=[(s_evv, ffn_ev[(qq, "out", dc)])])
            barrier()

        if dbg_outs:
            for q in range(NQ):
                dma("gpsimd", dbg["d_k"][q], k_gin[q], s_y)
                dma("gpsimd", dbg["d_v"][q], v_gin[q], s_y)
            dma("gpsimd", dbg["d_q"], q_scr, s_y)
            dma("gpsimd", dbg["d_o"], o_scr, s_y)

        for s in (s_st2[0], s_st2[1], s_y):
            if s.n > 0:
                P.emit("gpsimd", (lambda sh, vv: (lambda e: e.wait_ge(sh, vv)))(s.h, s.n))

        with nc.Block() as block:
            @block.sync
            def _(eng):
                P.replay("sync", eng)

            @block.tensor
            def _(eng):
                P.replay("tensor", eng)

            @block.vector
            def _(eng):
                P.replay("vector", eng)

            @block.scalar
            def _(eng):
                P.replay("scalar", eng)

            @block.gpsimd
            def _(eng):
                P.replay("gpsimd", eng)

    return nc


def stage_inputs(inputs):
    import ml_dtypes
    f32 = np.float32
    x = np.asarray(inputs["L0_attn_x"], dtype=f32)[0]
    Wqkv = np.asarray(inputs["L0_attn_Wqkv"], dtype=f32)
    bqkv = np.asarray(inputs["L0_attn_bqkv"], dtype=f32)
    Wo = np.asarray(inputs["L0_attn_Wo"], dtype=f32)
    bo = np.asarray(inputs["L0_attn_bo"], dtype=f32)
    Wb = np.asarray(inputs["L0_ffn_W_base"], dtype=f32)
    bb = np.asarray(inputs["L0_ffn_b_base"], dtype=f32)
    Wr = np.asarray(inputs["L0_ffn_W_router"], dtype=f32)
    lA = np.asarray(inputs["L0_ffn_lora_A"], dtype=f32)
    lB = np.asarray(inputs["L0_ffn_lora_B"], dtype=f32)

    shared = {}

    def qk_img(W):
        return np.ascontiguousarray(
            W.reshape(32, 128, 8, 128).transpose(2, 1, 0, 3).reshape(8, 128, 4096))
    shared["wq_img"] = qk_img(Wqkv[:, 0:1024])
    shared["wk_img"] = qk_img(Wqkv[:, 1024:2048])
    shared["wv_img"] = np.ascontiguousarray(Wqkv[:, 2048:3072].reshape(32, 128, 1024))
    shared["wo_img"] = np.ascontiguousarray(
        Wo.reshape(8, 128, 32, 128).transpose(2, 1, 0, 3).reshape(32, 128, 1024))
    wb_img = Wb.reshape(32, 128, 32, 128).transpose(0, 3, 2, 1).reshape(32, 128, 4096)
    b_img = lB.reshape(8, 32, 128, 16).transpose(0, 3, 1, 2).reshape(128, 4096)
    b_chunks = b_img.reshape(128, 32, 128).transpose(1, 0, 2)
    shared["wbb_img"] = np.ascontiguousarray(
        np.concatenate([wb_img, b_chunks], axis=2))
    shared["a_img"] = np.ascontiguousarray(
        lA.reshape(8, 16, 32, 128).transpose(3, 2, 0, 1).reshape(128, 4096))
    shared["wr_img"] = np.ascontiguousarray(
        Wr.T.reshape(32, 128, 8).transpose(1, 0, 2).reshape(128, 256))
    shared["bq_img"] = np.ascontiguousarray(bqkv[0:1024].reshape(8, 128).T)
    shared["bk_img"] = np.ascontiguousarray(bqkv[1024:2048].reshape(8, 128).T)
    shared["bv_row"] = np.ascontiguousarray(bqkv[2048:3072].reshape(1, 1024))
    shared["bo_img"] = np.ascontiguousarray(bo.reshape(32, 128).T)
    shared["bb_img"] = np.ascontiguousarray(bb.reshape(32, 128).T)
    shared["ones_in"] = np.ones((128, 128), dtype=f32)
    shared["onesb_in"] = np.ones((128, 128), dtype=ml_dtypes.bfloat16)
    shared["ident_in"] = np.eye(128, dtype=f32)
    rt = np.zeros((8, 128), dtype=f32)
    for e in range(8):
        rt[e, e * 16:(e + 1) * 16] = 1.0
    shared["rt_in"] = rt

    in_maps = []
    for c in range(NCORE):
        xs = x[c * TC:(c + 1) * TC]
        xi = np.ascontiguousarray(
            xs.reshape(TC, 32, 128).transpose(2, 1, 0).reshape(128, 32 * TC))
        m = dict(shared)
        m["x_img"] = xi
        in_maps.append(m)
    return in_maps


def assemble_output(results):
    out = np.empty((1, NCORE * TC, D), dtype=np.float32)
    for c in range(NCORE):
        yi = results[c]["y_img"].reshape(128, 32, TC)
        out[0, c * TC:(c + 1) * TC, :] = yi.transpose(2, 1, 0).reshape(TC, D)
    return out


def kernel(**inputs):
    in_maps = stage_inputs(inputs)
    nc = build_program()
    res = run_bass_kernel_spmd(nc, in_maps, core_ids=list(range(NCORE)))
    return assemble_output(res.results)


# version 05
# speedup vs baseline: 1.1809x; 1.0082x over previous
import numpy as np
from contextlib import ExitStack

import concourse.bass as bass
import concourse.mybir as mybir
from concourse.bass_utils import run_bass_kernel_spmd

F32 = mybir.dt.float32
BF16 = mybir.dt.bfloat16
AF = mybir.ActivationFunctionType
ALU = mybir.AluOpType

NCORE = 8
D = 4096
DV = 1024
TC = 2048
QT = 512
NQ = TC // QT
EPS = 1e-5
ISQRT_D = 1.0 / 32.0


class Sem:
    __slots__ = ("h", "n")

    def __init__(self, h):
        self.h = h
        self.n = 0

    def bump(self, k):
        self.n += k
        return self.n


class Prog:
    def __init__(self):
        self.ops = []

    def emit(self, eng, fn, waits=(), incs=()):
        rec_incs = []
        for s, k in incs:
            s.bump(k)
            rec_incs.append((s, k))
        self.ops.append((eng, fn, tuple(waits), tuple(rec_incs)))

    def replay(self, name, engine):
        for eng, fn, waits, incs in self.ops:
            if eng != name:
                continue
            for s, v in waits:
                if v > 0:
                    engine.wait_ge(s.h, v)
            ins = fn(engine)
            for s, k in incs:
                ins.then_inc(s.h, k)


def build_program(dbg_phases=("qkv", "attn", "ffn"), dbg_outs=False):
    nc = bass.Bass("TRN2", target_bir_lowering=False)

    x_img = nc.dram_tensor("x_img", [128, 32 * TC], F32, kind="ExternalInput").ap()
    wq_img = nc.dram_tensor("wq_img", [8, 128, 4096], F32, kind="ExternalInput").ap()
    wk_img = nc.dram_tensor("wk_img", [8, 128, 4096], F32, kind="ExternalInput").ap()
    wv_img = nc.dram_tensor("wv_img", [32, 128, 1024], F32, kind="ExternalInput").ap()
    wo_img = nc.dram_tensor("wo_img", [32, 128, 1024], F32, kind="ExternalInput").ap()
    wbb_img = nc.dram_tensor("wbb_img", [32, 128, 4224], F32, kind="ExternalInput").ap()
    a_img = nc.dram_tensor("a_img", [128, 4096], F32, kind="ExternalInput").ap()
    wr_img = nc.dram_tensor("wr_img", [128, 256], F32, kind="ExternalInput").ap()
    bq_img = nc.dram_tensor("bq_img", [128, 8], F32, kind="ExternalInput").ap()
    bk_img = nc.dram_tensor("bk_img", [128, 8], F32, kind="ExternalInput").ap()
    bv_row = nc.dram_tensor("bv_row", [1, 1024], F32, kind="ExternalInput").ap()
    bo_img = nc.dram_tensor("bo_img", [128, 32], F32, kind="ExternalInput").ap()
    bb_img = nc.dram_tensor("bb_img", [128, 32], F32, kind="ExternalInput").ap()
    ones_in = nc.dram_tensor("ones_in", [128, 128], F32, kind="ExternalInput").ap()
    onesb_in = nc.dram_tensor("onesb_in", [128, 128], BF16, kind="ExternalInput").ap()
    ident_in = nc.dram_tensor("ident_in", [128, 128], F32, kind="ExternalInput").ap()
    rt_in = nc.dram_tensor("rt_in", [8, 128], F32, kind="ExternalInput").ap()

    y_img = nc.dram_tensor("y_img", [128, 32 * TC], F32, kind="ExternalOutput").ap()

    k_gin = [nc.dram_tensor(f"k_gin{q}", [128, 8192], BF16).ap() for q in range(NQ)]
    v_gin = [nc.dram_tensor(f"v_gin{q}", [128, 8192], BF16).ap() for q in range(NQ)]
    k_gout = [nc.dram_tensor(f"k_gout{q}", [8 * 128, 8192], BF16).ap() for q in range(NQ)]
    v_gout = [nc.dram_tensor(f"v_gout{q}", [8 * 128, 8192], BF16).ap() for q in range(NQ)]
    q_scr = nc.dram_tensor("q_scr", [128, NQ * 8192], BF16).ap()
    o_scr = nc.dram_tensor("o_scr", [128, NQ * 8192], BF16).ap()
    r1_scr = nc.dram_tensor("r1_scr", [128, 16384], F32).ap()
    r2_scr = nc.dram_tensor("r2_scr", [128, 16384], F32).ap()

    dbg = {}
    if dbg_outs:
        dbg["d_q"] = nc.dram_tensor("d_q", [128, NQ * 8192], BF16, kind="ExternalOutput").ap()
        dbg["d_k"] = nc.dram_tensor("d_k", [NQ, 128, 8192], BF16, kind="ExternalOutput").ap()
        dbg["d_v"] = nc.dram_tensor("d_v", [NQ, 128, 8192], BF16, kind="ExternalOutput").ap()
        dbg["d_o"] = nc.dram_tensor("d_o", [128, NQ * 8192], BF16, kind="ExternalOutput").ap()
        dbg["d_x1"] = nc.dram_tensor("d_x1", [NQ, 128, 32768], BF16, kind="ExternalOutput").ap()
        dbg["d_r2"] = nc.dram_tensor("d_r2", [NQ, 128, 16384], F32, kind="ExternalOutput").ap()
        dbg["d_w4"] = nc.dram_tensor("d_w4", [NQ, 128, 32], F32, kind="ExternalOutput").ap()

    es = ExitStack()
    with es:
        abf = es.enter_context(nc.sbuf_tensor("abf", [128, 32768], BF16))
        big2 = es.enter_context(nc.sbuf_tensor("big2", [128, 8448], F32))
        cbf = es.enter_context(nc.sbuf_tensor("cbf", [128, 16384], BF16))
        wbf = es.enter_context(nc.sbuf_tensor("wbf", [128, 16896], BF16))
        misc = es.enter_context(nc.sbuf_tensor("misc", [128, 4096], F32))
        wr_s = es.enter_context(nc.sbuf_tensor("wr_s", [128, 512], BF16))
        gq = es.enter_context(nc.sbuf_tensor("gq", [128, 1024], BF16))
        consts = es.enter_context(nc.sbuf_tensor("consts", [128, 256], F32))
        onesb = es.enter_context(nc.sbuf_tensor("onesb", [128, 128], BF16))
        rt_s = es.enter_context(nc.sbuf_tensor("rt_s", [8, 128], F32))
        bias_s = es.enter_context(nc.sbuf_tensor("bias_s", [128, 80], F32))
        bv_s = es.enter_context(nc.sbuf_tensor("bv_s", [1, 1024], F32))
        wem = es.enter_context(nc.sbuf_tensor("wem", [8, 512], F32))
        lg = es.enter_context(nc.sbuf_tensor("lg", [128, 8], F32))
        srt = es.enter_context(nc.sbuf_tensor("srt", [128, 8], F32))
        ee = es.enter_context(nc.sbuf_tensor("ee", [128, 8], F32))
        msk = es.enter_context(nc.sbuf_tensor("msk", [128, 8], F32))
        negm1 = es.enter_context(nc.sbuf_tensor("negm1", [128, 1], F32))
        l2t = es.enter_context(nc.sbuf_tensor("l2t", [128, 1], F32))
        e2t = es.enter_context(nc.sbuf_tensor("e2t", [128, 1], F32))
        r2t = es.enter_context(nc.sbuf_tensor("r2t", [128, 1], F32))
        w4 = es.enter_context(nc.sbuf_tensor("w4", [128, 32], F32))
        st1 = es.enter_context(nc.sbuf_tensor("st1", [1, 512], F32))
        st2 = es.enter_context(nc.sbuf_tensor("st2", [1, 512], F32))
        st3 = es.enter_context(nc.sbuf_tensor("st3", [1, 512], F32))
        st4 = es.enter_context(nc.sbuf_tensor("st4", [1, 512], F32))
        st6 = es.enter_context(nc.sbuf_tensor("st6", [1, 512], F32))

        pb = [es.enter_context(nc.psum_tensor(f"pb{i}", [128, 512], F32)) for i in range(8)]

        sems = {}
        for name in ("s_c", "s_xp0", "s_xp1",
                     "s_pw0", "s_pw1",
                     "s_q", "s_kk0", "s_kk1", "s_kva0", "s_kva1",
                     "s_r20", "s_r21",
                     "s_st0", "s_st1", "s_y",
                     "s_mm", "s_evs", "s_evv", "s_cc"):
            sems[name] = Sem(es.enter_context(nc.semaphore(name)))
        s_c, s_y = sems["s_c"], sems["s_y"]
        s_xp = [sems["s_xp0"], sems["s_xp1"]]
        s_pw = [sems["s_pw0"], sems["s_pw1"]]
        s_q = sems["s_q"]
        s_kk = [sems["s_kk0"], sems["s_kk1"]]
        s_kva = [sems["s_kva0"], sems["s_kva1"]]
        s_r2l = [sems["s_r20"], sems["s_r21"]]
        s_st2 = [sems["s_st0"], sems["s_st1"]]
        s_mm, s_evs, s_evv = sems["s_mm"], sems["s_evs"], sems["s_evv"]
        s_cc = sems["s_cc"]

        P = Prog()

        def dma(q, out_ap, in_ap, sem, waits=()):
            P.emit(q, (lambda o, i: (lambda e: e.dma_start(out=o, in_=i)))(out_ap, in_ap),
                   waits=waits, incs=[(sem, 16)])
            return sem.n

        def mm(out_ap, lhsT, rhs, start, stop, waits=(), inc=None):
            P.emit("tensor",
                   (lambda o, l, r, st, sp: (lambda e: e.matmul(o, l, r, start=st, stop=sp)))(
                       out_ap, lhsT, rhs, start, stop),
                   waits=waits, incs=([(inc, 1)] if inc is not None else []))
            return inc.n if inc is not None else None

        def mm3(out_ap, lh, ll, rh, rl, start, stop, waits=(), inc=None):
            mm(out_ap, lh, rh, start=start, stop=False, waits=waits)
            mm(out_ap, lh, rl, start=False, stop=False)
            return mm(out_ap, ll, rh, start=False, stop=stop, inc=(inc if stop else None))

        EVA = misc[:, 0:512]
        EVB = misc[:, 512:1024]
        XSA = misc[:, 1024:1536]
        XSB = misc[:, 1536:2048]
        R128 = misc[:, 2048:2560]
        M128 = misc[:, 2560:3072]
        SQA = misc[:, 3072:3584]
        SQB = misc[:, 3584:4096]
        HT = misc[:, 2048:2560]
        SFM = misc[:, 2560:3072]
        OUTT = misc[:, 3072:3584]
        OUT2 = misc[:, 3584:4096]
        R2LA = misc[:, 1024:1536]
        R2LB = misc[:, 1536:2048]
        PFA = misc[:, 1024:1536]
        PFB = misc[:, 1536:2048]

        ones_s = consts[:, 0:128]
        ident_s = consts[:, 128:256]
        BQ = bias_s[:, 0:8]
        BK = bias_s[:, 8:16]
        BO = bias_s[:, 16:48]
        BB = bias_s[:, 48:80]

        dma("sync", consts[:, 0:128], ones_in, s_c)
        dma("sync", consts[:, 128:256], ident_in, s_c)
        dma("sync", onesb[:], onesb_in, s_c)
        dma("sync", rt_s[:], rt_in, s_c)
        dma("sync", bias_s[:, 0:8], bq_img, s_c)
        dma("sync", bias_s[:, 8:16], bk_img, s_c)
        dma("sync", bias_s[:, 16:48], bo_img, s_c)
        dma("sync", bias_s[:, 48:80], bb_img, s_c)
        dma("sync", bv_s[:], bv_row, s_c)
        CONST_W = s_c.n
        for engn in ("tensor", "vector", "scalar", "gpsimd"):
            P.emit(engn, (lambda sh, vv: (lambda e: e.wait_ge(sh, vv)))(s_c.h, CONST_W))

        def barrier():
            vals = [(s, s.n) for s in sems.values() if s.n > 0]
            for engn in ("sync", "gpsimd", "tensor", "vector", "scalar"):
                for s, v in vals:
                    P.emit(engn, (lambda sh, vv: (lambda e: e.wait_ge(sh, vv)))(s.h, v))

        wst = {"i": 0, "lo": {}}

        def wload(src_ap, ncols, hi_ap, lo_ap, hi_waits=(), dma_waits=()):
            i = wst["i"]
            wst["i"] = i + 1
            stg = big2[:, (i % 2) * 4224: (i % 2) * 4224 + ncols]
            LW = dma("sync", stg, src_ap, s_pw[i % 2],
                     waits=[(s_evv, wst["lo"].get(i - 2, 0))] + list(dma_waits))
            P.emit("scalar",
                   (lambda o, i_: (lambda e: e.copy(out=o, in_=i_)))(hi_ap, stg),
                   waits=[(s_pw[i % 2], LW)] + list(hi_waits),
                   incs=[(s_evs, 1)])
            HEV = s_evs.n
            P.emit("vector",
                   (lambda o, a, b: (lambda e: e.tensor_tensor(
                       out=o, in0=a, in1=b, op=ALU.subtract)))(lo_ap, stg, hi_ap),
                   waits=[(s_evs, HEV)], incs=[(s_evv, 1)])
            wst["lo"][i] = s_evv.n
            return s_evv.n

        WR_RDY = wload(wr_img, 256, wr_s[:, 0:256], wr_s[:, 256:512])

        def phase_floor():
            M0, S0, V0 = s_mm.n, s_evs.n, s_evv.n
            for engn in ("sync", "gpsimd", "tensor", "vector", "scalar"):
                for sh, vv in ((s_mm.h, M0), (s_evs.h, S0), (s_evv.h, V0)):
                    if vv > 0:
                        P.emit(engn, (lambda h, v: (lambda e: e.wait_ge(h, v)))(sh, vv))

        if "qkv" in dbg_phases:
            ev_hist_qk = {}
            last_v_evacs = 0
            mm_hist = {}
            st_hist = {}
            xsplit = {}
            def emit_xload(qq, kc):
                sl = kc % 2
                xs = XSA if sl == 0 else XSB
                xw = [(s_evs, xsplit.get((qq, kc - 2, "hi"),
                                          xsplit.get((qq - 1, 30 + (kc % 2), "hi"), 0))),
                      (s_evv, xsplit.get((qq, kc - 2, "lo"),
                                         xsplit.get((qq - 1, 30 + (kc % 2), "lo"), 0)))]
                XL = dma("sync", xs[:],
                         x_img[:, kc * TC + qq * QT: kc * TC + qq * QT + 512],
                         s_xp[sl], waits=xw)
                xh = abf[:, kc * 512: kc * 512 + 512]
                xl = abf[:, 16384 + kc * 512: 16384 + kc * 512 + 512]
                hw = [(s_xp[sl], XL),
                      (s_mm, mm_hist.get((qq - 1, "v", kc), 0))]
                P.emit("scalar",
                       (lambda o, i_: (lambda e: e.copy(out=o, in_=i_)))(xh, xs[:]),
                       waits=hw, incs=[(s_evs, 1)])
                xsplit[(qq, kc, "hi")] = s_evs.n
                P.emit("vector",
                       (lambda o, a, b: (lambda e: e.tensor_tensor(
                           out=o, in0=a, in1=b, op=ALU.subtract)))(xl, xs[:], xh),
                       waits=[(s_evs, xsplit[(qq, kc, "hi")])], incs=[(s_evv, 1)])
                xsplit[(qq, kc, "lo")] = s_evv.n

            def qk_wload(qq, j, hi_waits):
                slot = wbf[:, (j % 2) * 8448:(j % 2) * 8448 + 8192]
                img = wq_img if j < 8 else wk_img
                return wload(img[j % 8], 4096,
                             slot[:, 0:4096], slot[:, 4096:8192],
                             hi_waits=hi_waits)

            for qq in range(NQ):
                qk_rdy = {}
                qk_rdy[0] = qk_wload(
                    qq, 0, [(s_mm, mm_hist.get((qq - 1, "qk", 14), 0))])
                qk_rdy[1] = qk_wload(
                    qq, 1, [(s_mm, mm_hist.get((qq - 1, "v", 31), 0))])
                if qq == 0:
                    for kc in range(32):
                        emit_xload(qq, kc)
                XRDY_S = xsplit[(qq, 31, "hi")]
                XRDY_V = xsplit[(qq, 31, "lo")]

                for j in range(16):
                    fc = j % 8
                    isq = j < 8
                    slot = wbf[:, (j % 2) * 8448:(j % 2) * 8448 + 8192]
                    bank = pb[j % 2]
                    percol = (qq == 0 and j < 4)
                    mmw = [(s_evv, max(qk_rdy[j],
                                       ev_hist_qk.get((qq, j - 2), last_v_evacs)))]
                    if not percol:
                        mmw += [(s_evs, XRDY_S), (s_evv, XRDY_V)]
                    for kc in range(32):
                        w = list(mmw) if kc == 0 else []
                        if percol:
                            w += [(s_evs, xsplit[(qq, kc, "hi")]),
                                  (s_evv, xsplit[(qq, kc, "lo")])]
                        mm3(bank[:],
                            slot[:, kc * 128: kc * 128 + 128],
                            slot[:, 4096 + kc * 128: 4096 + kc * 128 + 128],
                            abf[:, kc * 512: kc * 512 + 512],
                            abf[:, 16384 + kc * 512: 16384 + kc * 512 + 512],
                            start=(kc == 0), stop=(kc == 31),
                            waits=w,
                            inc=(s_mm if kc == 31 else None))
                    mm_hist[(qq, "qk", j)] = s_mm.n
                    if j + 2 < 16:
                        qk_rdy[j + 2] = qk_wload(
                            qq, j + 2, [(s_mm, mm_hist[(qq, "qk", j)])])
                    evh = cbf[:, (j % 2) * 1024: (j % 2) * 1024 + 512]
                    evl = cbf[:, (j % 2) * 1024 + 512: (j % 2) * 1024 + 1024]
                    bias_ap = (BQ if isq else BK)[:, fc:fc + 1]
                    ev_reuse = st_hist.get((qq, j - 2),
                                           st_hist.get((qq - 1, "v", 6 + (j % 2)), 0))
                    P.emit("scalar",
                           (lambda o, i, b: (lambda e: e.activation(
                               out=o, in_=i, func=AF.Identity, bias=b, scale=1.0)))(
                               evh, bank[:], bias_ap),
                           waits=[(s_mm, mm_hist[(qq, "qk", j)]),
                                  (s_st2[j % 2], ev_reuse)],
                           incs=[(s_evs, 1)])
                    EVH = s_evs.n
                    P.emit("vector",
                           (lambda o, p_, b_, h_: (lambda e: e.scalar_tensor_tensor(
                               out=o, in0=p_, scalar=b_, in1=h_,
                               op0=ALU.add, op1=ALU.subtract)))(
                               evl, bank[:], bias_ap, evh),
                           waits=[(s_evs, EVH)], incs=[(s_evv, 1)])
                    ev_hist_qk[(qq, j)] = s_evv.n
                    if isq:
                        dst = q_scr[:, qq * 8192 + fc * 1024: qq * 8192 + fc * 1024 + 1024]
                    else:
                        dst = k_gin[qq][:, fc * 1024: fc * 1024 + 1024]
                    st_hist[(qq, j)] = dma(
                        "gpsimd", dst, cbf[:, (j % 2) * 1024:(j % 2) * 1024 + 1024],
                        s_st2[j % 2], waits=[(s_evv, ev_hist_qk[(qq, j)])])

                def wv_wload(kc, hi_waits):
                    slot = wbf[:, 8448 + (kc % 4) * 2048: 8448 + (kc % 4) * 2048 + 2048]
                    return wload(wv_img[kc], 1024,
                                 slot[:, 0:1024], slot[:, 1024:2048],
                                 hi_waits=hi_waits)

                wv_rdy = {}
                for kc in (0, 1):
                    wv_rdy[kc] = wv_wload(kc, [(s_mm, mm_hist[(qq, "qk", 15)])])
                for kc in range(32):
                    slot = wbf[:, 8448 + (kc % 4) * 2048: 8448 + (kc % 4) * 2048 + 2048]
                    mmw = [(s_evv, max(wv_rdy[kc],
                                       ev_hist_qk[(qq, 15)] if kc == 0 else 0))]
                    for t in range(8):
                        tkc, dvh = t // 2, t % 2
                        xh = abf[:, kc * 512 + tkc * 128: kc * 512 + tkc * 128 + 128]
                        xl = abf[:, 16384 + kc * 512 + tkc * 128: 16384 + kc * 512 + tkc * 128 + 128]
                        rh = slot[:, dvh * 512: dvh * 512 + 512]
                        rl = slot[:, 1024 + dvh * 512: 1024 + dvh * 512 + 512]
                        mm(pb[t][:], xh, rh, start=(kc == 0), stop=False,
                           waits=(mmw if t == 0 else ()))
                        mm(pb[t][:], xh, rl, start=False, stop=False)
                        mm(pb[t][:], xl, rh, start=False, stop=False,
                           inc=(s_mm if t == 7 else None))
                    mm_hist[(qq, "v", kc)] = s_mm.n
                    if kc + 2 < 32:
                        wv_rdy[kc + 2] = wv_wload(
                            kc + 2,
                            [(s_mm, mm_hist.get((qq, "v", kc - 2),
                                                mm_hist[(qq, "qk", 15)]))])
                    if qq + 1 < NQ:
                        emit_xload(qq + 1, kc)
                for t in range(8):
                    tkc, dvh = t // 2, t % 2
                    mm(pb[t][:], ones_s[0:1, :], bv_s[0:1, dvh * 512: dvh * 512 + 512],
                       start=False, stop=True, inc=s_mm)
                    mm_hist[(qq, "vb", t)] = s_mm.n
                    evh = cbf[:, (t % 2) * 1024: (t % 2) * 1024 + 512]
                    evl = cbf[:, (t % 2) * 1024 + 512: (t % 2) * 1024 + 1024]
                    P.emit("scalar",
                           (lambda o, i: (lambda e: e.copy(out=o, in_=i)))(evh, pb[t][:]),
                           waits=[(s_mm, mm_hist[(qq, "vb", t)]),
                                  (s_st2[t % 2],
                                   st_hist.get((qq, "v", t - 2),
                                               st_hist[(qq, 14 + (t % 2))]))],
                           incs=[(s_evs, 1)])
                    EVH = s_evs.n
                    P.emit("vector",
                           (lambda o, p_, h_: (lambda e: e.tensor_tensor(
                               out=o, in0=p_, in1=h_, op=ALU.subtract)))(
                               evl, pb[t][:], evh),
                           waits=[(s_evs, EVH)], incs=[(s_evv, 1)])
                    evn = s_evv.n
                    dst = v_gin[qq][:, tkc * 2048 + dvh * 1024: tkc * 2048 + dvh * 1024 + 1024]
                    st_hist[(qq, "v", t)] = dma(
                        "gpsimd", dst, cbf[:, (t % 2) * 1024:(t % 2) * 1024 + 1024],
                        s_st2[t % 2], waits=[(s_evv, evn)])
                last_v_evacs = s_evv.n
                mm_hist[("vlast", qq)] = mm_hist[(qq, "vb", 7)]

                ccw = [(s_st2[0], st_hist[(qq, "v", 6)]),
                       (s_st2[1], st_hist[(qq, "v", 7)])]
                P.emit("gpsimd",
                       (lambda i, o: (lambda e: e.collective_compute(
                           "AllGather", ALU.bypass,
                           replica_groups=[list(range(NCORE))],
                           ins=[i], outs=[o])))(k_gin[qq], k_gout[qq]),
                       waits=ccw, incs=[(s_cc, 1)])
                P.emit("gpsimd",
                       (lambda i, o: (lambda e: e.collective_compute(
                           "AllGather", ALU.bypass,
                           replica_groups=[list(range(NCORE))],
                           ins=[i], outs=[o])))(v_gin[qq], v_gout[qq]),
                       incs=[(s_cc, 1)])
            qkv_qst = {(g, j): st_hist[(g, j)] for g in range(NQ) for j in (6, 7)}
            phase_floor()

        if "attn" in dbg_phases:
            KSLOT = [abf[:, 0:8192], abf[:, 8192:16384]]
            VSLOT = [abf[:, 16384:24576], abf[:, 24576:32768]]
            QTS = cbf[:, 0:8192]
            OSB = big2[:, 0:4096]
            RREC = big2[:, 4096:4608]

            def psl(s, tkc, lo):
                off = 8192 + s * 4096 + tkc * 1024 + (512 if lo else 0)
                return cbf[:, off: off + 512]

            kv_hist = {}
            smm = {}
            omm = {}
            expc = {}
            flsh = {}
            sums_fin = {}
            attn_st = {}
            for g in range(NQ):
                qw = [(s_mm, omm.get((g - 1, 31, 1), 0)),
                      (s_st2[0], qkv_qst[(g, 6)]), (s_st2[1], qkv_qst[(g, 7)])]
                if g == 0:
                    qw += [(s_st2[0], s_st2[0].n), (s_st2[1], s_st2[1].n)]
                QL = dma("sync", QTS, q_scr[:, g * 8192: g * 8192 + 8192], s_q, waits=qw)
                NB = 32
                for b in range(NB):
                    qq, r = b // 8, b % 8
                    sl = b % 2
                    kvw = [(s_mm, smm.get((g, b - 2, 3),
                                          smm.get((g - 1, 30 + (b % 2), 3), 0)))]
                    kv_hist[(g, b, "k")] = dma(
                        "sync", KSLOT[sl],
                        k_gout[qq][r * 128:(r + 1) * 128, :], s_kk[sl],
                        waits=kvw + [(s_cc, 2 * (qq + 1))])
                    vw = [(s_mm, omm.get((g, b - 2, 1),
                                         omm.get((g - 1, 30 + (b % 2), 1), 0)))]
                    kv_hist[(g, b, "v")] = dma(
                        "sync", VSLOT[sl],
                        v_gout[qq][r * 128:(r + 1) * 128, :], s_kva[sl], waits=vw)
                    for tkc in range(4):
                        bank = pb[tkc % 2]
                        w = [(s_evv, expc.get((g, b, tkc - 2),
                                              expc.get((g, b - 1, tkc + 2),
                                                       expc.get((g - 1, NB - 1, tkc + 2), 0))))]
                        if tkc == 0:
                            w.append((s_kk[sl], kv_hist[(g, b, "k")]))
                            w.append((s_q, QL))
                        for dc in range(8):
                            kh = KSLOT[sl][:, dc * 1024 + tkc * 128: dc * 1024 + tkc * 128 + 128]
                            kl = KSLOT[sl][:, dc * 1024 + 512 + tkc * 128: dc * 1024 + 512 + tkc * 128 + 128]
                            qh = QTS[:, dc * 1024: dc * 1024 + 512]
                            ql = QTS[:, dc * 1024 + 512: dc * 1024 + 1024]
                            mm3(bank[:], kh, kl, qh, ql,
                                start=(dc == 0), stop=(dc == 7),
                                waits=(w if dc == 0 else ()),
                                inc=(s_mm if dc == 7 else None))
                        smm[(g, b, tkc)] = s_mm.n
                        we = [(s_mm, smm[(g, b, tkc)])]
                        if b >= 2:
                            we.append((s_mm, omm[(g, b - 2, 1)]))
                        else:
                            we.append((s_mm, omm.get((g - 1, 30 + b, 1), 0)))
                        pf = PFA if tkc % 2 == 0 else PFB
                        P.emit("scalar",
                               (lambda o, i: (lambda e: e.activation(
                                   out=o, in_=i, func=AF.Exp, scale=ISQRT_D)))(
                                   pf[:], pb[tkc % 2][:]),
                               waits=we + [(s_evv, expc.get((g, b, tkc - 2),
                                                            expc.get((g, b - 1, tkc + 2),
                                                                     expc.get((g - 1, NB - 1, tkc + 2), 0))))])
                        P.emit("scalar",
                               (lambda o, i: (lambda e: e.activation(
                                   out=o, in_=i, func=AF.Exp, scale=ISQRT_D)))(
                                   psl(b % 2, tkc, False), pb[tkc % 2][:]),
                               incs=[(s_evs, 1)])
                        PHEV = s_evs.n
                        P.emit("vector",
                               (lambda o, a, h_: (lambda e: e.tensor_tensor(
                                   out=o, in0=a, in1=h_, op=ALU.subtract)))(
                                   psl(b % 2, tkc, True), pf[:], psl(b % 2, tkc, False)),
                               waits=[(s_evs, PHEV)], incs=[(s_evv, 1)])
                        expc[(g, b, tkc)] = s_evv.n
                    if b > 0:
                        bp = b - 1
                        slp = bp % 2
                        w = [(s_evv, flsh[(g, bp, 0)])]
                        for i in range(4):
                            dvc = 4 + i
                            for tkc in range(4):
                                vbase = tkc * 2048 + (dvc // 4) * 1024 + (dvc % 4) * 128
                                vh = VSLOT[slp][:, vbase: vbase + 128]
                                vl = VSLOT[slp][:, vbase + 512: vbase + 640]
                                mm3(pb[2 + i][:], vh, vl,
                                    psl(bp % 2, tkc, False), psl(bp % 2, tkc, True),
                                    start=(tkc == 0), stop=(tkc == 3),
                                    waits=(w if (i == 0 and tkc == 0) else ()),
                                    inc=(s_mm if (i == 3 and tkc == 3) else None))
                        omm[(g, bp, 1)] = s_mm.n
                    for tkc in range(4):
                        mm(pb[7][0:1, :], onesb[:, 0:1], psl(b % 2, tkc, False),
                           start=(b == 0 and tkc == 0), stop=False,
                           waits=[(s_evv, expc[(g, b, tkc)])] +
                                 ([(s_evv, sums_fin.get(g - 1, 0))]
                                  if (b == 0 and tkc == 0) else []))
                        mm(pb[7][0:1, :], onesb[:, 0:1], psl(b % 2, tkc, True),
                           start=False, stop=(b == NB - 1 and tkc == 3),
                           inc=(s_mm if (b == NB - 1 and tkc == 3) else None))
                    if b == NB - 1:
                        smm[(g, "sums")] = s_mm.n
                    w = [(s_evv, flsh.get((g, b - 1, 1), flsh.get((g - 1, 31, 1), 0)))]
                    for i in range(4):
                        dvc = i
                        for tkc in range(4):
                            vbase = tkc * 2048 + (dvc // 4) * 1024 + (dvc % 4) * 128
                            vh = VSLOT[sl][:, vbase: vbase + 128]
                            vl = VSLOT[sl][:, vbase + 512: vbase + 640]
                            mm3(pb[2 + i][:], vh, vl,
                                psl(b % 2, tkc, False), psl(b % 2, tkc, True),
                                start=(tkc == 0), stop=(tkc == 3),
                                waits=((w + [(s_kva[sl], kv_hist[(g, b, "v")])])
                                       if (i == 0 and tkc == 0) else ()),
                                inc=(s_mm if (i == 3 and tkc == 3) else None))
                    omm[(g, b, 0)] = s_mm.n
                    for bb_, hh in ((b - 1, 1), (b, 0)):
                        if bb_ < 0:
                            continue
                        for i in range(4):
                            dvc = hh * 4 + i
                            dst = OSB[:, dvc * 512: dvc * 512 + 512]
                            if bb_ == 0:
                                fn = (lambda o, i_: (lambda e: e.tensor_copy(out=o, in_=i_)))(
                                    dst, pb[2 + i][:])
                            else:
                                fn = (lambda o, i_: (lambda e: e.tensor_tensor(
                                    out=o, in0=i_, in1=o, op=ALU.add)))(dst, pb[2 + i][:])
                            P.emit("vector", fn,
                                   waits=([(s_mm, omm[(g, bb_, hh)])] if i == 0 else []),
                                   incs=([(s_evv, 1)] if i == 3 else []))
                        flsh[(g, bb_, hh)] = s_evv.n
                bp = NB - 1
                slp = bp % 2
                w = [(s_evv, flsh[(g, bp, 0)])]
                for i in range(4):
                    dvc = 4 + i
                    for tkc in range(4):
                        vbase = tkc * 2048 + (dvc // 4) * 1024 + (dvc % 4) * 128
                        vh = VSLOT[slp][:, vbase: vbase + 128]
                        vl = VSLOT[slp][:, vbase + 512: vbase + 640]
                        mm3(pb[2 + i][:], vh, vl,
                            psl(bp % 2, tkc, False), psl(bp % 2, tkc, True),
                            start=(tkc == 0), stop=(tkc == 3),
                            waits=(w if (i == 0 and tkc == 0) else ()),
                            inc=(s_mm if (i == 3 and tkc == 3) else None))
                omm[(g, bp, 1)] = s_mm.n
                for i in range(4):
                    dvc = 4 + i
                    dst = OSB[:, dvc * 512: dvc * 512 + 512]
                    P.emit("vector",
                           (lambda o, i_: (lambda e: e.tensor_tensor(
                               out=o, in0=i_, in1=o, op=ALU.add)))(dst, pb[2 + i][:]),
                           waits=([(s_mm, omm[(g, bp, 1)])] if i == 0 else []),
                           incs=([(s_evv, 1)] if i == 3 else []))
                flsh[(g, bp, 1)] = s_evv.n

                P.emit("vector",
                       (lambda: (lambda e: e.tensor_copy(out=st1[:], in_=pb[7][0:1, :])))(),
                       waits=[(s_mm, smm[(g, "sums")])])
                P.emit("vector",
                       (lambda: (lambda e: e.reciprocal(st2[:], st1[:])))(),
                       incs=[(s_evv, 1)])
                sums_fin[g] = s_evv.n
                RECIP = s_evv.n
                mm(pb[6][:], ones_s[0:1, :], st2[0:1, :], start=True, stop=True,
                   waits=[(s_evv, RECIP)], inc=s_mm)
                RMM = s_mm.n
                P.emit("vector",
                       (lambda: (lambda e: e.tensor_copy(out=RREC[:], in_=pb[6][:])))(),
                       waits=[(s_mm, RMM)], incs=[(s_evv, 1)])
                RRDY = s_evv.n
                for dvc in range(8):
                    evf = EVA if dvc % 2 == 0 else EVB
                    P.emit("vector",
                           (lambda o, a, r_: (lambda e: e.tensor_tensor(
                               out=o, in0=a, in1=r_, op=ALU.mult)))(
                               evf[:], OSB[:, dvc * 512: dvc * 512 + 512], RREC[:]),
                           waits=[(s_evv, RRDY)],
                           incs=[(s_evv, 1)])
                    OSC = s_evv.n
                    pprev = (dvc - 1) % 2
                    gq_free = attn_st.get((g, dvc - 1), attn_st.get((g - 1, 7), 0))
                    P.emit("scalar",
                           (lambda o, i_: (lambda e: e.copy(out=o, in_=i_)))(
                               gq[:, 0:512], evf[:]),
                           waits=[(s_evv, OSC), (s_st2[pprev], gq_free)],
                           incs=[(s_evs, 1)])
                    OHEV = s_evs.n
                    P.emit("vector",
                           (lambda o, a, h_: (lambda e: e.tensor_tensor(
                               out=o, in0=a, in1=h_, op=ALU.subtract)))(
                               gq[:, 512:1024], evf[:], gq[:, 0:512]),
                           waits=[(s_evs, OHEV)], incs=[(s_evv, 1)])
                    OLEV = s_evv.n
                    attn_st[(g, dvc)] = dma(
                        "gpsimd",
                        o_scr[:, g * 8192 + dvc * 1024: g * 8192 + dvc * 1024 + 1024],
                        gq[:, 0:1024], s_st2[dvc % 2], waits=[(s_evv, OLEV)])
            attn_ost = {(g, d): attn_st[(g, d)] for g in range(NQ) for d in (6, 7)}
            phase_floor()

        if "ffn" in dbg_phases:
            ffn_mm = {}
            ffn_ev = {}
            y_hist = {}
            r1st = {}
            r2st = {}
            statclr = {"v": 0}
            x1_read_last = {"v": 0}
            x1s_last = {"s": 0}
            for qq in range(NQ):
                OAT = cbf[:, 0:8192]
                olw = [(s_mm, ffn_mm.get((qq - 1, "base", 31), 0))]
                if qq == 0:
                    olw += [(s_st2[0], attn_ost[(3, 6)]), (s_st2[1], attn_ost[(3, 7)])]
                OL = dma("sync", OAT, o_scr[:, qq * 8192: qq * 8192 + 8192], s_q,
                         waits=olw)

                def wo_wload(oc, hi_waits):
                    slot = wbf[:, 8448 + (oc % 2) * 2048: 8448 + (oc % 2) * 2048 + 2048]
                    return wload(wo_img[oc], 1024,
                                 slot[:, 0:1024], slot[:, 1024:2048],
                                 hi_waits=hi_waits)

                wo_rdy = {}
                for oc in (0, 1):
                    wo_rdy[oc] = wo_wload(
                        oc, [(s_mm, ffn_mm.get((qq - 1, "base", 31), 0))])
                for oc in range(32):
                    slot = wbf[:, 8448 + (oc % 2) * 2048: 8448 + (oc % 2) * 2048 + 2048]
                    xs = XSA if oc % 2 == 0 else XSB
                    XL = dma("gpsimd", xs[:],
                             x_img[:, oc * TC + qq * QT: oc * TC + qq * QT + 512],
                             s_xp[oc % 2],
                             waits=[(s_evv,
                                     ffn_ev.get((qq, "r1", oc - 2),
                                                ffn_ev.get((qq - 1, "out", 30 + (oc % 2)), 0)))])
                    bank = pb[oc % 2]
                    w = [(s_q, OL),
                         (s_evv, max(wo_rdy[oc],
                                     ffn_ev.get((qq, "r1", oc - 2), 0)))]
                    for dc in range(8):
                        mm3(bank[:],
                            slot[:, dc * 128: dc * 128 + 128],
                            slot[:, 1024 + dc * 128: 1024 + dc * 128 + 128],
                            OAT[:, dc * 1024: dc * 1024 + 512],
                            OAT[:, dc * 1024 + 512: dc * 1024 + 1024],
                            start=(dc == 0), stop=(dc == 7),
                            waits=(w if dc == 0 else ()),
                            inc=(s_mm if dc == 7 else None))
                    ffn_mm[(qq, "proj", oc)] = s_mm.n
                    if oc + 2 < 32:
                        wo_rdy[oc + 2] = wo_wload(
                            oc + 2, [(s_mm, ffn_mm[(qq, "proj", oc)])])
                    r1e = EVA if oc % 2 == 0 else EVB
                    r1w = [(s_mm, ffn_mm[(qq, "proj", oc)]), (s_xp[oc % 2], XL),
                           (s_st2[oc % 2],
                            r1st.get((qq, oc - 2),
                                     r2st.get((qq - 1, 30 + (oc % 2)), 0))),
                           (s_mm, ffn_mm.get((qq, "sqm", oc - 2), 0))]
                    if oc == 0:
                        r1w.append((s_mm, x1_read_last["v"]))
                        r1w.append((s_evs, x1s_last["s"]))
                    P.emit("vector",
                           (lambda o, p_, b_, x_: (lambda e: e.scalar_tensor_tensor(
                               out=o, in0=p_, scalar=b_, in1=x_,
                               op0=ALU.add, op1=ALU.add)))(
                               r1e[:], bank[:], BO[:, oc:oc + 1], xs[:]),
                           waits=r1w,
                           incs=[(s_evv, 1)])
                    ffn_ev[(qq, "r1", oc)] = s_evv.n
                    r1st[(qq, oc)] = dma(
                        "gpsimd", r1_scr[:, oc * 512: oc * 512 + 512], r1e[:],
                        s_st2[oc % 2], waits=[(s_evv, ffn_ev[(qq, "r1", oc)])])
                    sq = SQA if oc % 2 == 0 else SQB
                    P.emit("scalar",
                           (lambda o, i: (lambda e: e.activation(
                               out=o, in_=i, func=AF.Square)))(sq[:], r1e[:]),
                           waits=[(s_evv, ffn_ev[(qq, "r1", oc)])] +
                                 ([(s_mm, ffn_mm.get((qq, "sqm", oc - 2), 0))]
                                  if oc >= 2 else []),
                           incs=[(s_evs, 1)])
                    ffn_ev[(qq, "sq", oc)] = s_evs.n
                    mm(pb[6][0:1, :], ones_s[:, 0:1], r1e[:],
                       start=(oc == 0), stop=(oc == 31),
                       waits=[(s_evv, ffn_ev[(qq, "r1", oc)])] +
                             ([(s_evv, statclr["v"])] if oc == 0 else []),
                       inc=(s_mm if oc == 31 else None))
                    mm(pb[7][0:1, :], ones_s[:, 0:1], sq[:],
                       start=(oc == 0), stop=(oc == 31),
                       waits=[(s_evs, ffn_ev[(qq, "sq", oc)])],
                       inc=(s_mm if oc == 31 else None))
                    if oc == 31:
                        ffn_mm[(qq, "stats")] = s_mm.n
                    ffn_mm[(qq, "sqm", oc)] = s_mm.n

                def ln_stats_epilogue(stats_mm_val):
                    P.emit("vector",
                           (lambda: (lambda e: e.tensor_copy(out=st1[:], in_=pb[6][0:1, :])))(),
                           waits=[(s_mm, stats_mm_val)])
                    P.emit("vector",
                           (lambda: (lambda e: e.tensor_copy(out=st2[:], in_=pb[7][0:1, :])))(),
                           incs=[(s_evv, 1)])
                    statclr["v"] = s_evv.n
                    P.emit("vector",
                           (lambda: (lambda e: e.tensor_scalar_mul(st3[:], st1[:], 1.0 / D)))())
                    P.emit("vector",
                           (lambda: (lambda e: e.tensor_scalar_mul(st6[:], st2[:], 1.0 / D)))())
                    P.emit("vector",
                           (lambda: (lambda e: e.tensor_tensor(
                               out=st4[:], in0=st3[:], in1=st3[:], op=ALU.mult)))())
                    P.emit("vector",
                           (lambda: (lambda e: e.tensor_tensor(
                               out=st2[:], in0=st6[:], in1=st4[:], op=ALU.subtract)))())
                    P.emit("vector",
                           (lambda: (lambda e: e.tensor_scalar_add(st6[:], st2[:], EPS)))(),
                           incs=[(s_evv, 1)])
                    VEV = s_evv.n
                    P.emit("scalar",
                           (lambda: (lambda e: e.activation(out=st1[:], in_=st6[:], func=AF.Ln)))(),
                           waits=[(s_evv, VEV)])
                    P.emit("scalar",
                           (lambda: (lambda e: e.activation(
                               out=st4[:], in_=st1[:], func=AF.Exp, scale=-0.5)))(),
                           incs=[(s_evs, 1)])
                    SEV = s_evs.n
                    P.emit("vector",
                           (lambda: (lambda e: e.tensor_tensor(
                               out=st2[:], in0=st3[:], in1=st4[:], op=ALU.mult)))(),
                           waits=[(s_evs, SEV)])
                    P.emit("vector",
                           (lambda: (lambda e: e.tensor_scalar_mul(st3[:], st2[:], -1.0)))(),
                           incs=[(s_evv, 1)])
                    NEV = s_evv.n
                    mm(pb[2][:], ones_s[0:1, :], st4[0:1, :], start=True, stop=True,
                       waits=[(s_evv, NEV)], inc=s_mm)
                    mm(pb[3][:], ones_s[0:1, :], st3[0:1, :], start=True, stop=True,
                       inc=s_mm)
                    REPL = s_mm.n
                    P.emit("vector",
                           (lambda: (lambda e: e.tensor_copy(out=R128[:], in_=pb[2][:])))(),
                           waits=[(s_mm, REPL)])
                    P.emit("vector",
                           (lambda: (lambda e: e.tensor_copy(out=M128[:], in_=pb[3][:])))(),
                           incs=[(s_evv, 1)])
                    return s_evv.n

                RRDY1 = ln_stats_epilogue(ffn_mm[(qq, "stats")])
                for dc in range(32):
                    r1l = R2LA if dc % 2 == 0 else R2LB
                    RL = dma("gpsimd", r1l[:], r1_scr[:, dc * 512: dc * 512 + 512],
                             s_r2l[dc % 2],
                             waits=[(s_st2[dc % 2], r1st[(qq, dc)]),
                                    (s_evs, x1s_last["s"] if dc < 2 else 0),
                                    (s_evv,
                                     ffn_ev.get((qq, "x1v", dc - 2),
                                                ffn_ev.get((qq, "r1", 30 + (dc % 2)), 0)))])
                    P.emit("vector",
                           (lambda o, i: (lambda e: e.tensor_tensor(
                               out=o, in0=i, in1=R128[:], op=ALU.mult)))(
                               OUTT[:], r1l[:]),
                           waits=[(s_r2l[dc % 2], RL)] +
                                 ([(s_evv, RRDY1)] if dc == 0 else []))
                    P.emit("vector",
                           (lambda o: (lambda e: e.tensor_tensor(
                               out=o, in0=OUTT[:], in1=M128[:], op=ALU.add)))(OUT2[:]),
                           incs=[(s_evv, 1)])
                    X1C = s_evv.n
                    xh = abf[:, dc * 512: dc * 512 + 512]
                    xl = abf[:, 16384 + dc * 512: 16384 + dc * 512 + 512]
                    P.emit("scalar",
                           (lambda o, i_: (lambda e: e.copy(out=o, in_=i_)))(xh, OUT2[:]),
                           waits=[(s_evv, X1C)] +
                                 ([(s_mm, x1_read_last["v"])] if dc == 0 else []),
                           incs=[(s_evs, 1)])
                    XHEV = s_evs.n
                    P.emit("vector",
                           (lambda o, a, h_: (lambda e: e.tensor_tensor(
                               out=o, in0=a, in1=h_, op=ALU.subtract)))(
                               xl, OUT2[:], xh),
                           waits=[(s_evs, XHEV)], incs=[(s_evv, 1)])
                    ffn_ev[(qq, "x1v", dc)] = s_evv.n
                x1s_last["s"] = s_evs.n
                X1_LAST = s_evv.n
                if dbg_outs:
                    dma("gpsimd", dbg["d_x1"][qq], abf[:], s_y,
                        waits=[(s_evv, X1_LAST)])
                A_RDY = wload(a_img, 4096, wbf[:, 0:4096], wbf[:, 4096:8192],
                              hi_waits=[(s_mm, ffn_mm.get((qq - 1, "base", 31), 0))])

                for tc in range(4):
                    for kc in range(32):
                        xh = abf[:, kc * 512 + tc * 128: kc * 512 + tc * 128 + 128]
                        xl = abf[:, 16384 + kc * 512 + tc * 128: 16384 + kc * 512 + tc * 128 + 128]
                        mm3(pb[4][:, 0:8], xh, xl,
                            wr_s[:, kc * 8: kc * 8 + 8],
                            wr_s[:, 256 + kc * 8: 256 + kc * 8 + 8],
                            start=(kc == 0), stop=(kc == 31),
                            waits=([(s_evv, max(X1_LAST, WR_RDY,
                                                ffn_ev.get((qq, "wtr", tc - 1), 0)))]
                                   if kc == 0 else ()),
                            inc=(s_mm if kc == 31 else None))
                    LMM = s_mm.n
                    P.emit("scalar",
                           (lambda: (lambda e: e.copy(out=lg[:], in_=pb[4][:, 0:8])))(),
                           waits=[(s_mm, LMM)], incs=[(s_evs, 1)])
                    LEV = s_evs.n

                    def vop(fn, waits=(), incs=()):
                        P.emit("vector", fn, waits=waits, incs=incs)
                        P.emit("vector", (lambda: (lambda e: e.engine_nop()))())
                        P.emit("vector", (lambda: (lambda e: e.engine_nop()))())
                    vop((lambda: (lambda e: e.max(out=srt[:], in_=lg[:])))(),
                        waits=[(s_evs, LEV)])
                    vop((lambda: (lambda e: e.tensor_tensor(
                        out=negm1[:], in0=srt[:, 0:1], in1=srt[:, 1:2],
                        op=ALU.subtract)))())
                    P.emit("vector", (lambda: (lambda e: e.engine_nop()))(),
                           incs=[(s_evv, 1)])
                    TEV = s_evv.n
                    P.emit("scalar",
                           (lambda: (lambda e: e.activation(
                               out=l2t[:], in_=negm1[:], func=AF.Sigmoid)))(),
                           waits=[(s_evv, TEV)], incs=[(s_evs, 1)])
                    AEV = s_evs.n
                    vop((lambda: (lambda e: e.tensor_scalar(
                        e2t[:], l2t[:], -2.0, 2.0, op0=ALU.mult, op1=ALU.add)))(),
                        waits=[(s_evs, AEV)])
                    vop((lambda: (lambda e: e.tensor_scalar(
                        r2t[:], l2t[:], 4.0, -2.0, op0=ALU.mult, op1=ALU.add)))())
                    vop((lambda: (lambda e: e.tensor_scalar(
                        msk[:], lg[:], srt[:, 0:1], None, op0=ALU.is_equal)))())
                    vop((lambda: (lambda e: e.tensor_scalar(
                        ee[:], lg[:], srt[:, 1:2], None, op0=ALU.is_ge)))())
                    vop((lambda: (lambda e: e.tensor_scalar(
                        lg[:], ee[:], e2t[:], None, op0=ALU.mult)))())
                    vop((lambda tc_: (lambda e: e.scalar_tensor_tensor(
                        out=w4[:, tc_ * 8: tc_ * 8 + 8], in0=msk[:], scalar=r2t[:],
                        in1=lg[:], op0=ALU.mult, op1=ALU.add)))(tc),
                        incs=[(s_evv, 1)])
                    WEV = s_evv.n
                    P.emit("tensor",
                           (lambda tc_: (lambda e: e.transpose(
                               pb[5][0:8, tc_ * 128: tc_ * 128 + 128],
                               w4[:, tc_ * 8: tc_ * 8 + 8], ident_s[:])))(tc),
                           waits=[(s_evv, WEV)], incs=[(s_mm, 1)])
                    ffn_mm[(qq, "wt", tc)] = s_mm.n
                    ffn_ev[(qq, "wtr", tc)] = WEV
                P.emit("vector",
                       (lambda: (lambda e: e.tensor_copy(out=wem[0:8, 0:512], in_=pb[5][0:8, :])))(),
                       waits=[(s_mm, ffn_mm[(qq, "wt", 3)])], incs=[(s_evv, 1)])
                WEMEV = s_evv.n
                if dbg_outs:
                    dma("gpsimd", dbg["d_w4"][qq], w4[:], s_y, waits=[(s_evv, WEMEV)])
                mm(pb[2][:], rt_s[0:8, :], wem[0:8, 0:512], start=True, stop=True,
                   waits=[(s_evv, WEMEV)], inc=s_mm)
                SFMM = s_mm.n
                P.emit("vector",
                       (lambda: (lambda e: e.tensor_copy(out=SFM[:], in_=pb[2][:])))(),
                       waits=[(s_mm, SFMM)], incs=[(s_evv, 1)])
                SFEV = s_evv.n
                for kc in range(32):
                    mm3(pb[3][:],
                        wbf[:, kc * 128: kc * 128 + 128],
                        wbf[:, 4096 + kc * 128: 4096 + kc * 128 + 128],
                        abf[:, kc * 512: kc * 512 + 512],
                        abf[:, 16384 + kc * 512: 16384 + kc * 512 + 512],
                        start=(kc == 0), stop=(kc == 31),
                        waits=([(s_evv, max(A_RDY, X1_LAST))] if kc == 0 else ()),
                        inc=(s_mm if kc == 31 else None))
                ffn_mm[(qq, "h")] = s_mm.n
                P.emit("scalar",
                       (lambda: (lambda e: e.copy(out=HT[:], in_=pb[3][:])))(),
                       waits=[(s_mm, ffn_mm[(qq, "h")])], incs=[(s_evs, 1)])
                HEV = s_evs.n
                P.emit("vector",
                       (lambda: (lambda e: e.tensor_tensor(
                           out=OUTT[:], in0=HT[:], in1=SFM[:], op=ALU.mult)))(),
                       waits=[(s_evs, HEV), (s_evv, SFEV),
                              (s_mm, ffn_mm.get((qq - 1, "base", 31), 0))],
                       incs=[(s_evv, 1)])
                GFEV = s_evv.n
                P.emit("scalar",
                       (lambda: (lambda e: e.copy(out=gq[:, 0:512], in_=OUTT[:])))(),
                       waits=[(s_evv, GFEV)], incs=[(s_evs, 1)])
                GHEV = s_evs.n
                P.emit("vector",
                       (lambda: (lambda e: e.tensor_tensor(
                           out=gq[:, 512:1024], in0=OUTT[:], in1=gq[:, 0:512],
                           op=ALU.subtract)))(),
                       waits=[(s_evs, GHEV)], incs=[(s_evv, 1)])
                GEV = s_evv.n
                def wbb_wload(oc, hi_waits):
                    slot = wbf[:, (oc % 2) * 8448:(oc % 2) * 8448 + 8448]
                    return wload(wbb_img[oc], 4224,
                                 slot[:, 0:4224], slot[:, 4224:8448],
                                 hi_waits=hi_waits)

                wbb_rdy = {0: wbb_wload(0, [(s_mm, ffn_mm[(qq, "h")])]),
                           1: wbb_wload(1, [(s_mm, ffn_mm[(qq, "proj", 31)])])}
                for oc in range(32):
                    slot = wbf[:, (oc % 2) * 8448:(oc % 2) * 8448 + 8448]
                    bank = pb[oc % 2]
                    w = [(s_evv, max(wbb_rdy[oc], GEV,
                                     ffn_ev.get((qq, "r2", oc - 2), 0)))]
                    for dc in range(32):
                        mm3(bank[:],
                            slot[:, dc * 128:(dc + 1) * 128],
                            slot[:, 4224 + dc * 128: 4224 + dc * 128 + 128],
                            abf[:, dc * 512: dc * 512 + 512],
                            abf[:, 16384 + dc * 512: 16384 + dc * 512 + 512],
                            start=(dc == 0), stop=False,
                            waits=(w if dc == 0 else ()))
                    mm(bank[:], slot[:, 4096:4224], gq[:, 0:512],
                       start=False, stop=False)
                    mm(bank[:], slot[:, 4096:4224], gq[:, 512:1024],
                       start=False, stop=False)
                    mm(bank[:], slot[:, 8320:8448], gq[:, 0:512],
                       start=False, stop=True, inc=s_mm)
                    ffn_mm[(qq, "base", oc)] = s_mm.n
                    if oc + 2 < 32:
                        wbb_rdy[oc + 2] = wbb_wload(
                            oc + 2, [(s_mm, ffn_mm[(qq, "base", oc)])])
                    tt = XSA if oc % 2 == 0 else XSB
                    r2e = EVA if oc % 2 == 0 else EVB
                    P.emit("vector",
                           (lambda o, p_, b_, x_: (lambda e: e.scalar_tensor_tensor(
                               out=o, in0=p_, scalar=b_, in1=x_,
                               op0=ALU.add, op1=ALU.add)))(
                               tt[:], bank[:], BB[:, oc:oc + 1],
                               abf[:, oc * 512: oc * 512 + 512]),
                           waits=[(s_mm, ffn_mm[(qq, "base", oc)])],
                           incs=[(s_evv, 1)])
                    P.emit("vector",
                           (lambda o, a, b_: (lambda e: e.tensor_tensor(
                               out=o, in0=a, in1=b_, op=ALU.add)))(
                               r2e[:], tt[:],
                               abf[:, 16384 + oc * 512: 16384 + oc * 512 + 512]),
                           waits=[(s_st2[oc % 2],
                                   r2st.get((qq, oc - 2),
                                            r1st[(qq, 30 + (oc % 2))]))],
                           incs=[(s_evv, 1)])
                    ffn_ev[(qq, "r2", oc)] = s_evv.n
                    r2st[(qq, oc)] = dma(
                        "gpsimd", r2_scr[:, oc * 512: oc * 512 + 512], r2e[:],
                        s_st2[oc % 2], waits=[(s_evv, ffn_ev[(qq, "r2", oc)])])
                    sq = SQA if oc % 2 == 0 else SQB
                    P.emit("scalar",
                           (lambda o, i: (lambda e: e.activation(
                               out=o, in_=i, func=AF.Square)))(sq[:], r2e[:]),
                           waits=[(s_evv, ffn_ev[(qq, "r2", oc)])] +
                                 ([(s_mm, ffn_mm.get((qq, "sqm2", oc - 2), 0))]
                                  if oc >= 2 else []),
                           incs=[(s_evs, 1)])
                    SQEV = s_evs.n
                    mm(pb[6][0:1, :], ones_s[:, 0:1], r2e[:],
                       start=(oc == 0), stop=(oc == 31),
                       waits=[(s_evv, ffn_ev[(qq, "r2", oc)])] +
                             ([(s_evv, statclr["v"])] if oc == 0 else []),
                       inc=(s_mm if oc == 31 else None))
                    mm(pb[7][0:1, :], ones_s[:, 0:1], sq[:],
                       start=(oc == 0), stop=(oc == 31),
                       waits=[(s_evs, SQEV)],
                       inc=(s_mm if oc == 31 else None))
                    if oc == 31:
                        ffn_mm[(qq, "stats2")] = s_mm.n
                    ffn_mm[(qq, "sqm2", oc)] = s_mm.n

                x1_read_last["v"] = ffn_mm[(qq, "base", 31)]
                if dbg_outs:
                    dma("gpsimd", dbg["d_r2"][qq], r2_scr, s_y,
                        waits=[(s_st2[0], r2st[(qq, 30)]), (s_st2[1], r2st[(qq, 31)])])
                RRDY2 = ln_stats_epilogue(ffn_mm[(qq, "stats2")])
                for dc in range(32):
                    r2l = R2LA if dc % 2 == 0 else R2LB
                    RL = dma("gpsimd", r2l[:], r2_scr[:, dc * 512: dc * 512 + 512],
                             s_r2l[dc % 2],
                             waits=[(s_st2[dc % 2], r2st[(qq, dc)]),
                                    (s_evv,
                                     ffn_ev.get((qq, "out", dc - 2),
                                                ffn_ev.get((qq, "r2", 30 + (dc % 2)), 0)))])
                    P.emit("vector",
                           (lambda o, i: (lambda e: e.tensor_tensor(
                               out=o, in0=i, in1=R128[:], op=ALU.mult)))(
                               OUTT[:], r2l[:]),
                           waits=[(s_r2l[dc % 2], RL)] +
                                 ([(s_evv, RRDY2)] if dc == 0 else []) +
                                 [(s_y, y_hist.get((qq, dc - 1),
                                                   y_hist.get((qq - 1, 31), 0)))])
                    P.emit("vector",
                           (lambda o: (lambda e: e.tensor_tensor(
                               out=o, in0=OUTT[:], in1=M128[:], op=ALU.add)))(OUT2[:]),
                           incs=[(s_evv, 1)])
                    ffn_ev[(qq, "out", dc)] = s_evv.n
                    y_hist[(qq, dc)] = dma(
                        "gpsimd",
                        y_img[:, dc * TC + qq * QT: dc * TC + qq * QT + 512],
                        OUT2[:], s_y, waits=[(s_evv, ffn_ev[(qq, "out", dc)])])
            barrier()

        if dbg_outs:
            for q in range(NQ):
                dma("gpsimd", dbg["d_k"][q], k_gin[q], s_y)
                dma("gpsimd", dbg["d_v"][q], v_gin[q], s_y)
            dma("gpsimd", dbg["d_q"], q_scr, s_y)
            dma("gpsimd", dbg["d_o"], o_scr, s_y)

        for s in (s_st2[0], s_st2[1], s_y):
            if s.n > 0:
                P.emit("gpsimd", (lambda sh, vv: (lambda e: e.wait_ge(sh, vv)))(s.h, s.n))

        with nc.Block() as block:
            @block.sync
            def _(eng):
                P.replay("sync", eng)

            @block.tensor
            def _(eng):
                P.replay("tensor", eng)

            @block.vector
            def _(eng):
                P.replay("vector", eng)

            @block.scalar
            def _(eng):
                P.replay("scalar", eng)

            @block.gpsimd
            def _(eng):
                P.replay("gpsimd", eng)

    return nc


def stage_inputs(inputs):
    import ml_dtypes
    f32 = np.float32
    x = np.asarray(inputs["L0_attn_x"], dtype=f32)[0]
    Wqkv = np.asarray(inputs["L0_attn_Wqkv"], dtype=f32)
    bqkv = np.asarray(inputs["L0_attn_bqkv"], dtype=f32)
    Wo = np.asarray(inputs["L0_attn_Wo"], dtype=f32)
    bo = np.asarray(inputs["L0_attn_bo"], dtype=f32)
    Wb = np.asarray(inputs["L0_ffn_W_base"], dtype=f32)
    bb = np.asarray(inputs["L0_ffn_b_base"], dtype=f32)
    Wr = np.asarray(inputs["L0_ffn_W_router"], dtype=f32)
    lA = np.asarray(inputs["L0_ffn_lora_A"], dtype=f32)
    lB = np.asarray(inputs["L0_ffn_lora_B"], dtype=f32)

    shared = {}

    def qk_img(W):
        return np.ascontiguousarray(
            W.reshape(32, 128, 8, 128).transpose(2, 1, 0, 3).reshape(8, 128, 4096))
    shared["wq_img"] = qk_img(Wqkv[:, 0:1024])
    shared["wk_img"] = qk_img(Wqkv[:, 1024:2048])
    shared["wv_img"] = np.ascontiguousarray(Wqkv[:, 2048:3072].reshape(32, 128, 1024))
    shared["wo_img"] = np.ascontiguousarray(
        Wo.reshape(8, 128, 32, 128).transpose(2, 1, 0, 3).reshape(32, 128, 1024))
    wb_img = Wb.reshape(32, 128, 32, 128).transpose(0, 3, 2, 1).reshape(32, 128, 4096)
    b_img = lB.reshape(8, 32, 128, 16).transpose(0, 3, 1, 2).reshape(128, 4096)
    b_chunks = b_img.reshape(128, 32, 128).transpose(1, 0, 2)
    shared["wbb_img"] = np.ascontiguousarray(
        np.concatenate([wb_img, b_chunks], axis=2))
    shared["a_img"] = np.ascontiguousarray(
        lA.reshape(8, 16, 32, 128).transpose(3, 2, 0, 1).reshape(128, 4096))
    shared["wr_img"] = np.ascontiguousarray(
        Wr.T.reshape(32, 128, 8).transpose(1, 0, 2).reshape(128, 256))
    shared["bq_img"] = np.ascontiguousarray(bqkv[0:1024].reshape(8, 128).T)
    shared["bk_img"] = np.ascontiguousarray(bqkv[1024:2048].reshape(8, 128).T)
    shared["bv_row"] = np.ascontiguousarray(bqkv[2048:3072].reshape(1, 1024))
    shared["bo_img"] = np.ascontiguousarray(bo.reshape(32, 128).T)
    shared["bb_img"] = np.ascontiguousarray(bb.reshape(32, 128).T)
    shared["ones_in"] = np.ones((128, 128), dtype=f32)
    shared["onesb_in"] = np.ones((128, 128), dtype=ml_dtypes.bfloat16)
    shared["ident_in"] = np.eye(128, dtype=f32)
    rt = np.zeros((8, 128), dtype=f32)
    for e in range(8):
        rt[e, e * 16:(e + 1) * 16] = 1.0
    shared["rt_in"] = rt

    in_maps = []
    for c in range(NCORE):
        xs = x[c * TC:(c + 1) * TC]
        xi = np.ascontiguousarray(
            xs.reshape(TC, 32, 128).transpose(2, 1, 0).reshape(128, 32 * TC))
        m = dict(shared)
        m["x_img"] = xi
        in_maps.append(m)
    return in_maps


def assemble_output(results):
    out = np.empty((1, NCORE * TC, D), dtype=np.float32)
    for c in range(NCORE):
        yi = results[c]["y_img"].reshape(128, 32, TC)
        out[0, c * TC:(c + 1) * TC, :] = yi.transpose(2, 1, 0).reshape(TC, D)
    return out


def kernel(**inputs):
    in_maps = stage_inputs(inputs)
    nc = build_program()
    res = run_bass_kernel_spmd(nc, in_maps, core_ids=list(range(NCORE)))
    return assemble_output(res.results)
